# version 29
# baseline (speedup 1.0000x reference)
import contextlib
import numpy as np
import concourse.bass as bass
import concourse.mybir as mybir

F32 = mybir.dt.float32
FP8 = mybir.dt.float8e4
BF16 = mybir.dt.bfloat16
I8 = mybir.dt.int8
AL = mybir.AluOpType
AF = mybir.ActivationFunctionType
AX = mybir.AxisListType

B_PER_CORE = 2
L = 2048
D = 1024
N3 = 3072
INNER = 4096
LT, CT, IT = 16, 8, 32
EPS = 1e-5


def sigma(qi):
    return qi


class Sem:
    __slots__ = ("h", "name", "n")

    def __init__(self, h, name):
        self.h, self.name, self.n = h, name, 0


class Em:
    def __init__(self):
        self.q = {k: [] for k in ("T", "V", "S", "G", "Y")}

    def add(self, eng, fn):
        self.q[eng].append(fn)

    def op(self, eng, sem, fn):
        sem.n += 1
        v = sem.n
        self.q[eng].append(lambda e, f=fn, s=sem: f(e).then_inc(s.h, 1))
        return v

    def dma(self, eng, sem, out_ap, in_ap):
        sem.n += 16
        v = sem.n
        self.q[eng].append(lambda e, o=out_ap, i=in_ap, s=sem: e.dma_start(o, i).then_inc(s.h, 16))
        return v

    def dma_t(self, eng, sem, out_ap, in_ap):
        sem.n += 16
        v = sem.n
        self.q[eng].append(lambda e, o=out_ap, i=in_ap, s=sem: e.dma_start_transpose(o, i).then_inc(s.h, 16))
        return v

    def wait(self, eng, sem, val):
        if val:
            self.q[eng].append(lambda e, s=sem, v=val: e.wait_ge(s.h, v))


def build(nc, dbg=(), phases=("prep", "p1", "p2", "p3", "p4", "p5")):
    dbg = set(dbg)
    xin = nc.dram_tensor("xin", [B_PER_CORE * L, D], F32, kind="ExternalInput")
    Wqkv = nc.dram_tensor("wqkv", [D, N3], F32, kind="ExternalInput")
    Wo = nc.dram_tensor("wo", [D, D], F32, kind="ExternalInput")
    wg = nc.dram_tensor("wg", [INNER, D], F32, kind="ExternalInput")
    wu = nc.dram_tensor("wu", [INNER, D], F32, kind="ExternalInput")
    wd = nc.dram_tensor("wd", [D, INNER], F32, kind="ExternalInput")
    identC = nc.dram_tensor("identc", [128, 128], BF16, kind="ExternalInput")
    onesC = nc.dram_tensor("onesc", [128, 128], BF16, kind="ExternalInput")
    selC = nc.dram_tensor("selc", [16, 1024], BF16, kind="ExternalInput")
    out = nc.dram_tensor("out", [B_PER_CORE * L, D], F32, kind="ExternalOutput")

    def scr(name, shape, dt):
        return nc.dram_tensor(name, shape, dt,
                              kind="ExternalOutput" if name in dbg else "Internal")

    qkvD = scr("qkvd", [B_PER_CORE * L * 48, 64], BF16)
    wgTD = scr("wgtd", [D, INNER], FP8)
    wuTD = scr("wutd", [D, INNER], FP8)
    wdTD = scr("wdtd", [INNER, D], FP8)
    xpD = scr("xpd", [B_PER_CORE * L, D], F32)
    zrowD = scr("zrowd", [16, L], BF16)
    hqTD = scr("hqtd", [B_PER_CORE * LT * 128, INNER], BF16)
    wscD = scr("wscd", [1, 12], F32)
    azD = scr("azd", [B_PER_CORE * 128, 16384], BF16) if "azd" in dbg else None
    hqD = scr("hqd", [B_PER_CORE * L, INNER], BF16) if "hqd" in dbg else None
    xqD = scr("xqd", [B_PER_CORE * L, D], BF16) if "xqd" in dbg else None

    A = lambda n, s, d, o: nc.alloc_sbuf_tensor_at(n, s, d, offset=o)
    MB = 18432
    W1, W2, W3, W4 = MB, MB + 65536, MB + 131072, MB + 163840
    wtile = [A(f"wtile{i}", [128, 4096], F32, W2 + 32768 + i * 16384) for i in range(2)]
    wq8f = A("wq8f", [128, 4096], F32, W4 + 11264)
    wqb = A("wqb", [128, 4096], BF16, W4 + 27648)
    wqb2 = A("wqb2", [128, 4096], BF16, W3)
    wts = [A(f"wts{i}", [128, 1024], FP8, W4 + 35840 + i * 1024) for i in range(2)]
    pbn = A("pbn", [128, 32], F32, W4 + 37888)
    thr2 = A("thr2", [1, 6], F32, W4 + 44096)
    WqB = A("wqb_full", [128, 8 * 3072], BF16, W1)
    qsbS = [A(f"qsb{i}", [128, 1536], BF16, W1 + 49152 + i * 3072) for i in range(2)]
    qkS = [A(f"qks{i}", [128, 16 * 64], BF16, W1 + i * 2048) for i in range(4)]
    vS = [A(f"vs{i}", [128, 16 * 65], BF16, W1 + 8192 + i * 2112) for i in range(2)]
    qT1 = [A(f"qt1_{i}", [128, L], BF16, W1 + 12544 + i * 4096) for i in range(2)]
    kT1 = [A(f"kt1_{i}", [128, L], BF16, W1 + 20736 + i * 4096) for i in range(2)]
    phS = [A(f"phs{i}", [128, 1024], BF16, W1 + 28928 + i * 2048) for i in range(3)]
    azst = [A(f"azst{i}", [65, L], BF16, W1 + 35072 + i * 4096) for i in range(2)]
    zall = A("zall", [16, L], BF16, W1 + 43264)
    zala = A("zala", [128, 256], BF16, W1 + 43264)
    zif2 = A("zif2", [128, 256], F32, W1 + 47360)
    zi2 = A("zi2", [128, 256], BF16, W1 + 55552)
    zinvF = A("zinvf", [16, L], F32, W1 + 47360)
    zinv = A("zinv", [16, L], BF16, W1 + 55552)
    wgT = A("wgt", [128, 8 * 4096], FP8, W1)
    wuT8 = A("wut8", [128, 8 * 4096], FP8, W1 + 32768)
    wdT = A("wdt", [128, 32 * 1024], FP8, W2)
    featB = A("featb", [128, 8 * 2048], BF16, W2)
    x2buf = A("x2buf", [128, 1024], F32, W2 + 32768)
    ybuf = A("ybuf", [128, 1024], F32, W2 + 36864)
    xpbuf = [A(f"xpbuf{i}", [128, 1024], F32, W2 + 40960 + i * 4096) for i in range(2)]
    WoB = A("wob", [128, 8 * 1024], BF16, W2 + 49152)
    i8buf = A("i8buf", [128, 1024], I8, W4 + 13312)
    xqsb = A("xqsb", [128, 1024], BF16, W4 + 11264)
    featA = A("feata", [128, 8 * 2048], BF16, W3)
    xb = [A(f"xb{i}", [128, 1024], F32, W4 + i * 4096) for i in range(2)]
    gbuf = A("gbuf", [128, 1024], F32, W4)
    bnb = A("bnb", [128, 48], F32, W4 + 8192)
    st2 = A("st2", [128, 2], F32, W4 + 8256)
    veps = A("veps", [128, 1], F32, W4 + 8288)
    sq = A("sq", [128, 1], F32, W4 + 8320)
    rstd = A("rstd", [128, 1], F32, W4 + 8352)
    nmb = A("nmb", [128, 1], F32, W4 + 8384)
    rmx = A("rmx", [128, 1], F32, W4 + 8416)
    rmr = A("rmr", [128, 1], F32, W4 + 8448)
    accw = A("accw", [128, 4], F32, W4 + 8480)
    sgA = A("sga", [128, 16], F32, W4 + 8512)
    sdA = A("sda", [128, 16], F32, W4 + 8576)
    wsc = A("wsc", [1, 4], F32, W4 + 8640)
    red1 = A("red1", [1, 1], F32, W4 + 8672)
    wscB = A("wscb", [128, 12], F32, W4 + 44032)
    idb = A("idb", [128, 128], BF16, W4 + 8704)
    onb = A("onb", [128, 128], BF16, W4 + 8960)
    selB = A("selb", [16, 1024], BF16, W4 + 9216)
    i8h = A("i8h", [128, 4096], I8, W4 + 4096)
    hbuf = A("hbuf", [128, 4096], F32, W4 + 11264)
    hqsb = A("hqsb", [128, 4096], BF16, W4 + 27648)
    hqTst = A("hqtst", [128, 4096], BF16, W4 + 35840)
    t0f = A("t0f", [128, 1024], BF16, W4 + 44128)
    wq_f32a = A("wqf32a", [128, 3072], F32, W4 + 14336)
    wq_f32b = A("wqf32b", [128, 3072], F32, W4 + 27648)
    hq2 = [A(f"hq2_{i}", [128, 4096], BF16, W4 + 11264 + i * 8192) for i in range(2)]
    dnT = A("dnt", [128, 1024], F32, W2 + 32768)
    xfb = [A(f"xfb{i}", [128, 1024], F32, W2 + 36864 + i * 4096) for i in range(2)]

    em = Em()
    stack = contextlib.ExitStack()
    SEMS = {}

    def sem(name):
        if name not in SEMS:
            SEMS[name] = Sem(stack.enter_context(nc.semaphore(name)), name)
        return SEMS[name]

    sT, sV, sS, sG = sem("c_t"), sem("c_v"), sem("c_s"), sem("c_g")
    psF = stack.enter_context(nc.psum_tensor("psf", [128, 3072], F32))
    psB = stack.enter_context(nc.psum_tensor("psb", [128, 1024], BF16))
    psB2 = stack.enter_context(nc.psum_tensor("psb2", [128, 1024], BF16))
    bank = lambda i: psF[:, i * 512:(i + 1) * 512]

    def ln_consts(src_aps):
        em.wait("V", sV, sV.n)
        for c, ap in enumerate(src_aps):
            em.op("V", sV, lambda e, c=c, a=ap: e.bn_stats(bnb[:, c * 6:(c + 1) * 6], a))
        em.wait("V", sV, sV.n)
        em.op("V", sV, lambda e, n=len(src_aps): e.bn_aggr(st2[:], bnb[:, 0:6 * n]))
        vw = sV.n
        em.wait("S", sV, vw)
        sw = em.op("S", sS, lambda e: e.activation(sq[:], st2[:, 1:2], AF.Sqrt, bias=wscB[:, 9:10]))
        em.wait("V", sS, sw)
        em.op("V", sV, lambda e: e.reciprocal(rstd[:], sq[:]))
        em.wait("V", sV, sV.n)
        em.op("V", sV, lambda e: e.scalar_tensor_tensor(nmb[:], st2[:, 0:1], -1.0, rstd[:], AL.mult, AL.mult))
        cw = sV.n
        return cw

    idv = onv = None
    prep_done_y = 0

    PREP = {"wscB_ready": 0, "done": (0, 0)}
    def prep_gen():
        gld = [sem("pw0"), sem("pw1")]
        pst = [sem("pwst0"), sem("pwst1")]
        sbc = sem("sbc")
        WSPEC = ((wg, 32, 1024, wgTD, INNER), (wu, 32, 1024, wuTD, INNER),
                 (wd, 8, 4096, wdTD, D))
        slot_rd = [0, 0]
        stage_rd = [0, 0]
        for wi, (W, n_t, tw, dst, dcols) in enumerate(WSPEC):
            for t in range(n_t):
                s = t % 2
                em.wait("Y", sV, slot_rd[s])
                lv = em.dma("Y", gld[s], wtile[s][:, 0:tw], W.ap()[t * 128:(t + 1) * 128, :])
                em.wait("V", gld[s], lv)
                if t == 0:
                    em.wait("V", sV, sV.n)
                em.op("V", sV, lambda e, s=s, w=tw, t=t: e.tensor_reduce(
                    pbn[:, t % 32: t % 32 + 1], wtile[s][:, 0:w], AX.X, AL.add,
                    apply_absolute_value=True))
                slot_rd[s] = sV.n
                if t == n_t - 1:
                    em.wait("V", sV, sV.n)
                    em.op("V", sV, lambda e, wi=wi, n_t=n_t: e.tensor_reduce(
                        accw[:, wi:wi + 1], pbn[:, 0:n_t] if n_t <= 32 else pbn[:, 0:32],
                        AX.X, AL.add))
                yield
        em.wait("G", sV, sV.n)
        gv = em.op("G", sG, lambda e: e.tensor_reduce(red1[0:1, 0:1], accw[:, 0:1], AX.C, AL.add))
        gv = em.op("G", sG, lambda e: e.tensor_reduce(wsc[0:1, 3:4], accw[:, 1:2], AX.C, AL.add))
        gv = em.op("G", sG, lambda e: e.tensor_reduce(thr2[0:1, 0:1], accw[:, 2:3], AX.C, AL.add))
        em.wait("V", sG, gv)
        for wi, (W, n_t, tw, dst, dcols) in enumerate(WSPEC):
            NN = float(n_t * 128 * tw)
            srcap = (red1[0:1, 0:1], wsc[0:1, 3:4], thr2[0:1, 0:1])[wi]
            em.op("V", sV, lambda e, wi=wi, NN=NN, a=srcap: e.tensor_scalar(
                wsc[:, wi:wi + 1], a, 1.0 / NN, 1e-5, AL.mult, AL.max))
        em.wait("V", sV, sV.n)
        for wi in range(3):
            em.op("V", sV, lambda e, wi=wi: e.tensor_scalar(thr2[:, 2 * wi:2 * wi + 1], wsc[:, wi:wi + 1], 0.5, None, AL.mult))
            em.op("V", sV, lambda e, wi=wi: e.tensor_scalar(thr2[:, 2 * wi + 1:2 * wi + 2], wsc[:, wi:wi + 1], -0.5, None, AL.mult))
        tv_ = sV.n
        em.wait("G", sV, tv_)
        bv = em.dma("G", sbc, wscD.ap()[0:1, 3:9], thr2[:])
        bv2 = em.dma("G", sbc, wscD.ap()[0:1, 0:3], wsc[:, 0:3])
        em.wait("G", sbc, bv2)
        bv = em.dma("G", sbc, wscB[:, 3:9], bass.AP(wscD, 3, [[1, 1], [0, 128], [1, 6]]))
        bv2 = em.dma("G", sbc, wscB[:, 0:3], bass.AP(wscD, 0, [[1, 1], [0, 128], [1, 3]]))
        PREP["wscB_ready"] = bv2
        thr_ready = bv2
        yield
        em.wait("V", sT, sT.n)
        wbufs = [wqb, wqb2]
        wqb_rd = [0, 0]
        pend2 = [None]

        def flush2():
            t_, tw_, dst_, dcols_, cpv_, wb_, tg_ = pend2[0]
            em.wait("T", sV, cpv_)
            nblk = tw_ // 128
            tvl = None
            for b0 in range(0, nblk, 8):
                nb = min(8, nblk - b0)
                ss = (b0 // 8) % 2
                tv = None
                for j in range(nb):
                    tv = em.op("T", sT, lambda e, j=j, b0=b0, wb=wb_: e.transpose(
                        psB[:, j * 128:(j + 1) * 128], wb[:, (b0 + j) * 128:(b0 + j + 1) * 128], idb[:]))
                tvl = tv
                em.wait("V", sT, tv)
                em.wait("V", pst[ss], stage_rd[ss])
                scv = em.op("V", sV, lambda e, nb=nb, ss=ss: e.tensor_copy(
                    wts[ss][:, 0:nb * 128], psB[:, 0:nb * 128]))
                em.wait("T", sV, scv)
                em.wait("Y", sV, scv)
                dst_ap = bass.AP(dst_, (b0 * 128) * dcols_ + t_ * 128,
                                 [[dcols_, 128], [128 * dcols_, nb], [1, 128]])
                src_ap = bass.AP(wts[ss], 0, [[1024, 128], [128, nb], [1, 128]])
                stage_rd[ss] = em.dma("Y", pst[ss], dst_ap, src_ap)
            wqb_rd[tg_ % 2] = tvl
            pend2[0] = None
        tg = 0
        for wi, (W, n_t, tw, dst, dcols) in enumerate(WSPEC):
            for t in range(n_t):
                s = tg % 2
                em.wait("Y", sV, slot_rd[s])
                lv = em.dma("Y", gld[s], wtile[s][:, 0:tw], W.ap()[t * 128:(t + 1) * 128, :])
                em.wait("V", gld[s], lv)
                if wi == 0 and t == 0:
                    em.wait("V", SEMS["sbc"], thr_ready)
                em.wait("V", sV, sV.n)
                em.op("V", sV, lambda e, s=s, w=tw, wi=wi: e.tensor_scalar(
                    wq8f[:, 0:w], wtile[s][:, 0:w], wscB[:, 4 + 2 * wi:5 + 2 * wi], None, AL.is_le))
                em.wait("V", sV, sV.n)
                em.wait("V", sT, wqb_rd[tg % 2])
                cpv = em.op("V", sV, lambda e, s=s, w=tw, wi=wi, wb=wbufs[tg % 2]: e.scalar_tensor_tensor(
                    wb[:, 0:w], wtile[s][:, 0:w], wscB[:, 3 + 2 * wi:4 + 2 * wi],
                    wq8f[:, 0:w], AL.is_ge, AL.subtract))
                slot_rd[s] = cpv
                if pend2[0] is not None:
                    flush2()
                pend2[0] = (t, tw, dst, dcols, cpv, wbufs[tg % 2], tg)
                tg += 1
                yield
        flush2()
        PREP["done"] = (SEMS["pwst0"].n, SEMS["pwst1"].n)

    _pgen = [None]

    def pump_prep(n):
        if "prep" not in phases:
            return
        if _pgen[0] is None:
            _pgen[0] = prep_gen()
        for _ in range(n):
            try:
                next(_pgen[0])
            except StopIteration:
                break

    sx = [sem("sx0"), sem("sx1")]
    sxp = [sem("sxp0"), sem("sxp1")]
    for b in range(B_PER_CORE):
        xrow = b * L
        if "p1" in phases:
            swq2 = [sem("p1w0"), sem("p1w1")]
            sqk = [sem("p1q0"), sem("p1q1")]
            if idv is None:
                gid = sem("gid")
                epsv = em.op("V", sV, lambda e: e.memset(wscB[:, 9:10], EPS))
                em.wait("S", sV, epsv)
                em.wait("V", sV, epsv)
                idv = em.dma("G", gid, idb[:], identC.ap())
                onv = em.dma("G", gid, onb[:], onesC.ap())
                onv = em.dma("G", gid, selB[:], selC.ap())
                em.wait("T", gid, onv)
            em.wait("G", sV, sV.n)
            em.wait("G", sT, sT.n)
            em.wait("T", sV, sV.n)
            em.wait("T", sS, sS.n)
            em.wait("S", sV, sV.n)
            if "outst" in SEMS:
                em.wait("S", SEMS["outst"], SEMS["outst"].n)
            wqstg = [wq_f32a, wq_f32b]
            wq_cast = [sV.n, sV.n]
            wv = 0
            for ctw in range(CT):
                sw_ = ctw % 2
                em.wait("Y", sV, wq_cast[sw_])
                lvw = em.dma("Y", swq2[sw_], wqstg[sw_][:],
                             Wqkv.ap()[ctw * 128:(ctw + 1) * 128, :])
                em.wait("V", swq2[sw_], lvw)
                wq_cast[sw_] = em.op("V", sV, lambda e, ctw=ctw, sw_=sw_: e.tensor_copy(
                    WqB[:, ctw * 3072:(ctw + 1) * 3072], wqstg[sw_][:]))
            wv = max(wq_cast)
            x1T_ready = [0] * LT
            xb_rd = [sS.n, sS.n]
            qkv_ev = [0, 0]
            qsb_rd = [0, 0]
            qkv_store_tile = []
            wq_started = False
            for lt in range(LT):
                s = lt % 2
                em.wait("S", sS, xb_rd[s])
                lv = em.dma("S", sx[s], xb[s][:], xin.ap()[xrow + lt * 128: xrow + (lt + 1) * 128, :])
                em.wait("V", sx[s], lv)
                cw = ln_consts([xb[s][:, 0:512], xb[s][:, 512:1024]])
                em.wait("S", sV, cw)
                av = em.op("S", sS, lambda e, s=s: e.activation(
                    xqsb[:], xb[s][:], AF.Identity, bias=nmb[:], scale=rstd[:]))
                xb_rd[s] = av
                em.wait("T", sS, av)
                tv = None
                for ct in range(CT):
                    tv = em.op("T", sT, lambda e, ct=ct: e.transpose(
                        psB[:, ct * 128:(ct + 1) * 128], xqsb[:, ct * 128:(ct + 1) * 128], idb[:]))
                em.wait("V", sT, tv)
                cpv = em.op("V", sV, lambda e, lt=lt: e.tensor_copy(
                    bass.AP(featA, lt * 128, [[16384, 128], [2048, 8], [1, 128]]),
                    bass.AP(psB, 0, [[1024, 128], [128, 8], [1, 128]])))
                x1T_ready[lt] = cpv
                em.wait("T", sV, cpv)
                if not wq_started:
                    em.wait("T", sV, wv)
                    wq_started = True
                for hf in range(2):
                    em.wait("T", sV, qkv_ev[hf])
                    mv = None
                    for ct in range(CT):
                        fap = bass.AP(featA, ct * 2048 + lt * 128, [[16384, 128], [1, 128]])
                        for nc3 in range(3):
                            nc6 = hf * 3 + nc3
                            mv = em.op("T", sT, lambda e, a=fap, ct=ct, nc6=nc6, bk=hf * 3 + nc3: e.matmul(
                                bank(bk), a, WqB[:, ct * 3072 + nc6 * 512: ct * 3072 + nc6 * 512 + 512],
                                start=(ct == 0), stop=(ct == CT - 1)))
                    em.wait("V", sT, mv)
                    em.wait("V", sqk[hf], qsb_rd[hf])
                    ev = em.op("V", sV, lambda e, hf=hf: e.tensor_copy(
                        qsbS[hf][:], psF[:, hf * 1536: hf * 1536 + 1536]))
                    qkv_ev[hf] = ev
                    em.wait("Y", sV, ev)
                    dstap = bass.AP(qkvD, (xrow + lt * 128) * 48 * 64 + hf * 1536,
                                    [[3072, 128], [1, 1536]])
                    qsb_rd[hf] = em.dma("Y", sqk[hf], dstap, qsbS[hf][:])
                qkv_store_tile.append((SEMS["p1q0"].n, SEMS["p1q1"].n))
                if b == 0 and lt >= 2:
                    pump_prep(6)
            p1q = (SEMS["p1q0"].n, SEMS["p1q1"].n)
        if "p2" in phases:
            sqh = [sem("p2q0"), sem("p2q1")]
            skh = [sem("p2k0"), sem("p2k1")]
            svh = [sem("p2v0"), sem("p2v1")]
            szc = sem("szc")
            zdma_last = szc.n
            em.wait("G", sT, sT.n)
            em.wait("V", sT, sT.n)
            em.wait("T", sV, sV.n)
            em.wait("T", sS, sS.n)
            for i in range(2):
                em.op("V", sV, lambda e, i=i: e.memset(
                    bass.AP(vS[i], 64, [[1040, 128], [65, 16], [1, 1]]), 1.0))
            vones = sV.n
            em.wait("G", sV, vones)
            em.wait("Y", sV, vones)
            em.wait("S", sV, vones)
            qk_rd = [0, 0]
            v_rd = [0, 0]
            dd_rd = [0, 0]
            az_ev = [0, 0]
            expv = [0, 0]
            pend_head = None
            av_done = 0
            zld = sem("zld")
            base = b * L * 48

            def head_ready(h):
                need = min(LT, (33 + h) * 2048 // (48 * 128) + 1)
                return qkv_store_tile[need - 1]
            for h in range(16):
                hs = h % 2
                rdy = head_ready(h)
                def apt(seg, dstbuf):
                    return (dstbuf[:], bass.AP(qkvD, (base + seg * L) * 64, [[128, 1024], [1, 128]]))
                em.wait("Y", SEMS["p1q0"], rdy[0])
                em.wait("Y", SEMS["p1q1"], rdy[1])
                em.wait("Y", sV, qk_rd[hs])
                o_, i_ = apt(h, qkS[hs])
                qv = em.dma_t("Y", sqh[hs], o_, i_)
                em.wait("Y", SEMS["p1q0"], rdy[0])
                em.wait("Y", SEMS["p1q1"], rdy[1])
                em.wait("Y", sV, qk_rd[hs])
                o_, i_ = apt(16 + h, qkS[2 + hs])
                kv = em.dma_t("Y", skh[hs], o_, i_)
                em.wait("G", SEMS["p1q0"], rdy[0])
                em.wait("G", SEMS["p1q1"], rdy[1])
                em.wait("G", sT, v_rd[hs])
                vv = em.dma("G", svh[hs],
                            bass.AP(vS[hs], 0, [[16 * 65, 128], [65, 16], [1, 64]]),
                            bass.AP(qkvD, (base + (32 + h) * L) * 64,
                                    [[64, 128], [128 * 64, 16], [1, 64]]))

                em.wait("V", sqh[hs], qv)
                em.wait("V", sT, dd_rd[hs])
                for hh in range(2):
                    em.op("V", sV, lambda e, hs=hs, hh=hh: e.tensor_copy(
                        bass.AP(qT1[hs], hh * 64 * L, [[L, 64], [2, 1024]]),
                        bass.AP(qkS[hs], 0, [[1024, 64], [1, 1024]])))
                    qtv = em.op("V", sV, lambda e, hs=hs, hh=hh: e.tensor_copy(
                        bass.AP(qT1[hs], hh * 64 * L + 1, [[L, 64], [2, 1024]]),
                        bass.AP(qkS[hs], 64 * 1024, [[1024, 64], [1, 1024]])))
                em.wait("V", skh[hs], kv)
                for hh in range(2):
                    em.op("V", sV, lambda e, hs=hs, hh=hh: e.tensor_copy(
                        bass.AP(kT1[hs], hh * 64 * L, [[L, 64], [2, 1024]]),
                        bass.AP(qkS[2 + hs], 0, [[1024, 64], [1, 1024]])))
                    ktv = em.op("V", sV, lambda e, hs=hs, hh=hh: e.tensor_copy(
                        bass.AP(kT1[hs], hh * 64 * L + 1, [[L, 64], [2, 1024]]),
                        bass.AP(qkS[2 + hs], 64 * 1024, [[1024, 64], [1, 1024]])))
                em.wait("T", sV, ktv)
                qk_rd[hs] = ktv
                if pend_head is not None:
                    ph, pazs = pend_head
                    em.wait("V", sT, av_done)
                    az_ev[1] = em.op("V", sV, lambda e, azs=pazs: e.tensor_copy(
                        azs[:, 3 * 512:4 * 512], psF[0:65, 5 * 512:6 * 512]))
                    pp_ = ph // 2
                    rlo_ = (ph % 2) * 64
                    em.wait("V", sV, sV.n)
                    em.wait("V", szc, zdma_last)
                    em.op("V", sV, lambda e, p=pp_, rlo=rlo_, azs=pazs: e.tensor_copy(
                        featB[rlo:rlo + 64, p * 2048:(p + 1) * 2048], azs[0:64, :]))
                    em.wait("G", sV, sV.n)
                    zdma_last = em.dma("G", szc, zrowD.ap()[ph:ph + 1, :], pazs[64:65, :])
                    pend_head = None
                em.wait("T", svh[hs], vv)
                if b == 0 and h == 0:
                    em.wait("T", sV, vones)
                azs = azst[h % 2]
                for s4 in range(4):
                    ab = 4 + (s4 % 2)
                    em.wait("T", sV, az_ev[s4 % 2])
                    for ktp in range(9):
                        pp = ktp % 2
                        if ktp < 8:
                            em.wait("T", sS, expv[pp])
                            for half in range(2):
                                kt = 2 * ktp + half
                                rr = slice(half * 64, half * 64 + 64)
                                smv = em.op("T", sT, lambda e, hs=hs, kt=kt, rr=rr, s4=s4, bk=2 * pp + half: e.matmul(
                                    bank(bk), kT1[hs][rr, kt * 128:(kt + 1) * 128],
                                    qT1[hs][rr, s4 * 512:(s4 + 1) * 512],
                                    start=True, stop=True, skip_group_check=True))
                            em.wait("S", sT, smv)
                            expv[pp] = em.op("S", sS, lambda e, pp=pp: e.activation(
                                phS[pp][:], psF[:, 2 * pp * 512: (2 * pp + 2) * 512], AF.Exp, scale=0.125))
                        if ktp >= 1:
                            qp = (ktp - 1) % 2
                            em.wait("T", sS, expv[qp])
                            for half in range(2):
                                kt = 2 * (ktp - 1) + half
                                em.op("T", sT, lambda e, hs=hs, kt=kt, ab=ab, qp=qp, half=half: e.matmul(
                                    psF[0:65, ab * 512:(ab + 1) * 512], vS[hs][:, kt * 65: kt * 65 + 65],
                                    phS[qp][:, half * 512:(half + 1) * 512],
                                    start=(kt == 0), stop=(kt == 15)))
                    v_rd[hs] = sT.n
                    dd_rd[hs] = sT.n
                    if s4 < 3:
                        em.wait("V", sT, sT.n)
                        col = s4 * 512
                        az_ev[s4 % 2] = em.op("V", sV, lambda e, ab=ab, col=col, azs=azs: e.tensor_copy(
                            azs[:, col:col + 512], psF[0:65, ab * 512:(ab + 1) * 512]))
                    else:
                        av_done = sT.n
                        pend_head = (h, azs)
                    if b == 0:
                        pump_prep(2)
            if b == 0:
                pump_prep(1000)
            if pend_head is not None:
                ph, pazs = pend_head
                em.wait("V", sT, av_done)
                em.op("V", sV, lambda e, azs=pazs: e.tensor_copy(
                    azs[:, 3 * 512:4 * 512], psF[0:65, 5 * 512:6 * 512]))
                em.wait("V", sV, sV.n)
                em.wait("V", szc, zdma_last)
                em.op("V", sV, lambda e, p=ph // 2, rlo=(ph % 2) * 64, azs=pazs: e.tensor_copy(
                    featB[rlo:rlo + 64, p * 2048:(p + 1) * 2048], azs[0:64, :]))
                em.wait("G", sV, sV.n)
                zdma_last = em.dma("G", szc, zrowD.ap()[ph:ph + 1, :], pazs[64:65, :])
                pend_head = None
            em.wait("G", szc, zdma_last)
            zlv = em.dma("G", zld, zala[:], bass.AP(zrowD, 0, [[256, 128], [1, 256]]))
            em.wait("V", zld, zlv)
            em.op("V", sV, lambda e: e.reciprocal(zif2[:], zala[:]))
            em.wait("V", sV, sV.n)
            zcv = em.op("V", sV, lambda e: e.tensor_copy(zi2[:], zif2[:]))
            em.wait("G", sV, zcv)
            zsv = em.dma("G", zld, bass.AP(zrowD, 0, [[256, 128], [1, 256]]), zi2[:])
            em.wait("G", zld, zsv)
            zlv2 = em.dma("G", zld, zinv[:], zrowD.ap())
            em.wait("V", zld, zlv2)
            em.wait("T", zld, zlv2)
            zb_rd = [0, 0]
            zidx = 0
            for p in range(8):
                for c in range(4):
                    bk = 4 + (zidx % 2)
                    em.wait("T", sV, zb_rd[zidx % 2])
                    m2 = em.op("T", sT, lambda e, p=p, c=c, bk=bk: e.matmul(
                        bank(bk), selB[0:16, p * 128:(p + 1) * 128],
                        zinv[0:16, c * 512:(c + 1) * 512], start=True, stop=True))
                    em.wait("V", sT, m2)
                    zb_rd[zidx % 2] = em.op("V", sV, lambda e, p=p, c=c, bk=bk: e.tensor_tensor(
                        featB[:, p * 2048 + c * 512: p * 2048 + (c + 1) * 512],
                        featB[:, p * 2048 + c * 512: p * 2048 + (c + 1) * 512],
                        bank(bk), AL.mult))
                    zidx += 1
            featB_done = sV.n
            if azD is not None:
                em.wait("Y", sV, featB_done)
                dv_ = em.dma("Y", sem("dbgaz"), azD.ap()[b * 128:(b + 1) * 128, :], featB[:])
                em.wait("V", SEMS["dbgaz"], dv_)
        if "p3" in phases:
            swo = sem("p3wo")
            em.wait("G", sV, sV.n)
            em.wait("G", sT, sT.n)
            if "prep" in phases:
                em.wait("G", SEMS["pwst0"], SEMS["pwst0"].n)
                em.wait("G", SEMS["pwst1"], SEMS["pwst1"].n)
            wov = em.dma("G", swo, bass.AP(WoB, 0, [[8192, 128], [1024, 8], [1, 1024]]),
                         bass.AP(Wo, 0, [[D, 128], [128 * D, 8], [1, 1024]]))
            em.wait("T", swo, wov)
            if "p4" in phases and "prep" in phases:
                swg = sem("p4wg")
                em.wait("G", SEMS["pwst0"], SEMS["pwst0"].n)
                em.wait("G", SEMS["pwst1"], SEMS["pwst1"].n)
                em.wait("G", sV, sV.n)
                em.wait("G", sT, sT.n)
                for nm in ("p2q0", "p2q1", "p2k0", "p2k1", "p2v0", "p2v1"):
                    if nm in SEMS:
                        em.wait("G", SEMS[nm], SEMS[nm].n)
                em.dma("G", swg, bass.AP(wgT, 0, [[32768, 128], [4096, 8], [1, INNER]]),
                       bass.AP(wgTD, 0, [[INNER, 128], [128 * INNER, 8], [1, INNER]]))
                wg_pref = em.dma("G", swg, bass.AP(wuT8, 0, [[32768, 128], [4096, 8], [1, INNER]]),
                                 bass.AP(wuTD, 0, [[INNER, 128], [128 * INNER, 8], [1, INNER]]))
            if "p2" in phases:
                em.wait("T", sV, featB_done)
            em.wait("T", sS, sS.n)
            if "outst" in SEMS:
                em.wait("V", SEMS["outst"], SEMS["outst"].n)
            prev_yv = [0, 0]
            xb_rd3 = [sV.n, sV.n]
            xp_wr = [SEMS["sxp0"].n, SEMS["sxp1"].n]
            x2_rd = 0
            t0_fr3 = 0
            pend_xq = None

            def flush_xq(p):
                if p is None:
                    return
                qq, qvv = p
                em.wait("T", sV, qvv)
                tv = None
                for ct in range(CT):
                    tv = em.op("T", sT, lambda e, ct=ct: e.transpose(
                        psB[:, ct * 128:(ct + 1) * 128], xqsb[:, ct * 128:(ct + 1) * 128], idb[:]))
                em.wait("V", sT, tv)
                cpv = em.op("V", sV, lambda e, qq=qq: e.tensor_copy(
                    bass.AP(featA, qq * 128, [[16384, 128], [2048, 8], [1, 128]]),
                    bass.AP(psB, 0, [[1024, 128], [128, 8], [1, 128]])))
                em.wait("T", sV, cpv)
            for qi in range(LT):
                sg = sigma(qi)
                s = qi % 2
                pb = s * 2
                em.wait("S", sV, xb_rd3[s])
                lvx = em.dma("S", sx[s], xb[s][:], xin.ap()[xrow + sg * 128: xrow + (sg + 1) * 128, :])
                em.wait("T", sS, prev_yv[s])
                mv = None
                for ft in range(8):
                    fap = featB[:, ft * 2048 + qi * 128: ft * 2048 + qi * 128 + 128]
                    em.op("T", sT, lambda e, a=fap, ft=ft, pb=pb: e.matmul(
                        bank(pb), a, WoB[:, ft * 1024: ft * 1024 + 512], start=(ft == 0), stop=(ft == 7)))
                    mv = em.op("T", sT, lambda e, a=fap, ft=ft, pb=pb: e.matmul(
                        bank(pb + 1), a, WoB[:, ft * 1024 + 512: ft * 1024 + 1024],
                        start=(ft == 0), stop=(ft == 7)))
                flush_xq(pend_xq)
                pend_xq = None
                em.wait("V", sT, mv)
                cw = ln_consts([bank(pb), bank(pb + 1)])
                em.wait("S", sV, cw)
                yv = em.op("S", sS, lambda e, pb=pb: e.activation(
                    ybuf[:], psF[:, pb * 512: pb * 512 + 1024], AF.Identity, bias=nmb[:], scale=rstd[:]))
                prev_yv[s] = yv
                em.wait("V", sS, yv)
                em.wait("V", sx[s], lvx)
                em.wait("V", sxp[s], xp_wr[s])
                xpv = em.op("V", sV, lambda e, s=s: e.tensor_add(xpbuf[s][:], xb[s][:], ybuf[:]))
                xb_rd3[s] = xpv
                em.wait("Y", sV, xpv)
                xp_wr[s] = em.dma("Y", sxp[s], xpD.ap()[xrow + sg * 128: xrow + (sg + 1) * 128, :],
                                  xpbuf[s][:])
                cw = ln_consts([xpbuf[s][:, 0:512], xpbuf[s][:, 512:1024]])
                em.wait("S", sV, cw)
                em.wait("S", sV, x2_rd)
                x2v = em.op("S", sS, lambda e, s=s: e.activation(
                    x2buf[:], xpbuf[s][:], AF.Identity, bias=nmb[:], scale=rstd[:]))
                em.wait("V", sS, x2v)
                em.op("V", sV, lambda e: e.tensor_reduce(rmx[:], x2buf[:], AX.X, AL.max, apply_absolute_value=True))
                em.wait("V", sV, sV.n)
                em.op("V", sV, lambda e: e.tensor_scalar(rmx[:], rmx[:], 1e-5, None, AL.max))
                em.wait("V", sV, sV.n)
                em.op("V", sV, lambda e: e.reciprocal(rmr[:], rmx[:]))
                em.wait("V", sV, sV.n)
                if "prep" in phases:
                    if qi == 0:
                        em.wait("V", SEMS["sbc"], PREP["wscB_ready"])
                    em.op("V", sV, lambda e, qi=qi: e.scalar_tensor_tensor(
                        sgA[:, qi:qi + 1], rmx[:], 1.0 / 127.0, wscB[:, 0:1], AL.mult, AL.mult))
                uq = em.op("V", sV, lambda e: e.tensor_scalar(xqsb[:], x2buf[:], rmr[:], 127.0, AL.mult, AL.mult))
                x2_rd = uq
                em.wait("V", sV, uq)
                c1 = em.op("V", sV, lambda e: e.tensor_copy(i8buf[:], xqsb[:]))
                em.wait("V", sV, c1)
                em.wait("V", sV, t0_fr3)
                c2 = em.op("V", sV, lambda e: e.tensor_copy(t0f[:], i8buf[:]))
                em.wait("V", sV, c2)
                em.op("V", sV, lambda e: e.tensor_sub(xqsb[:], xqsb[:], t0f[:]))
                em.wait("V", sV, sV.n)
                em.op("V", sV, lambda e: e.scalar_tensor_tensor(t0f[:], xqsb[:], 0.5, t0f[:], AL.is_ge, AL.add))
                em.wait("V", sV, sV.n)
                em.op("V", sV, lambda e: e.tensor_scalar(xqsb[:], xqsb[:], -0.5, None, AL.is_le))
                em.wait("V", sV, sV.n)
                qv = em.op("V", sV, lambda e: e.tensor_sub(xqsb[:], t0f[:], xqsb[:]))
                t0_fr3 = qv
                pend_xq = (qi, qv)
                if xqD is not None:
                    em.wait("Y", sV, qv)
                    dqv_ = em.dma("Y", sem("dbgxq"), xqD.ap()[xrow + sg * 128: xrow + (sg + 1) * 128, :], xqsb[:])
                    em.wait("V", SEMS["dbgxq"], dqv_)
            flush_xq(pend_xq)
            pend_xq = None
            xqT_done = sV.n
            p3_xp = (SEMS["sxp0"].n, SEMS["sxp1"].n)

        if "p4" in phases:
            swu = sem("p4wu")
            swd = sem("p5wd")
            sout = sem("outst")
            sxg = [sem("p5x0"), sem("p5x1")]
            em.wait("G", sT, sT.n)
            em.wait("G", sV, sV.n)
            em.wait("G", SEMS["sxp0"], SEMS["sxp0"].n)
            em.wait("G", SEMS["sxp1"], SEMS["sxp1"].n)
            em.wait("S", SEMS["sxp0"], SEMS["sxp0"].n)
            em.wait("S", SEMS["sxp1"], SEMS["sxp1"].n)
            dv = em.dma("G", swd, bass.AP(wdT, 0, [[32768, 128], [1024, 32], [1, D]]),
                        bass.AP(wdTD, 0, [[D, 128], [128 * D, 32], [1, D]]))
            em.wait("T", SEMS["p4wg"], wg_pref)
            em.wait("T", swd, dv)
            if "prep" in phases:
                em.wait("T", SEMS["pwst0"], PREP["done"][0])
                em.wait("T", SEMS["pwst1"], PREP["done"][1])
            em.wait("T", sV, xqT_done)
            em.wait("T", sS, sS.n)
            pair_s = [0, 0, 0]
            pair_v = [0, 0, 0]
            gb_rd = [0, 0]
            pend = None

            def fused_tail(p):
                bb_unused = None
                qq, quarters, sgq, dpair, s5 = p
                cpvs = [0, 0]
                for oh in range(4):
                    pb_ = (psB, psB2)[oh % 2]
                    em.wait("T", sV, quarters[oh])
                    em.wait("T", sS, cpvs[oh % 2])
                    tv = None
                    for j in range(8):
                        blk = oh * 8 + j
                        tv = em.op("T", sT, lambda e, blk=blk, j=j, pb_=pb_: e.transpose(
                            pb_[:, j * 128:(j + 1) * 128],
                            hqsb[:, blk * 128:(blk + 1) * 128], idb[:]))
                    em.wait("S", sT, tv)
                    cpvs[oh % 2] = em.op("S", sS, lambda e, oh=oh, pb_=pb_: e.activation(
                        hqTst[:, oh * 1024:(oh + 1) * 1024], pb_[:], AF.Identity))
                em.wait("T", sS, cpvs[0])
                em.wait("T", sS, cpvs[1])
                em.wait("T", sS, pair_s[dpair])
                em.wait("T", sV, pair_v[dpair])
                pb = 2 * dpair
                mv = None
                for it in range(IT):
                    em.op("T", sT, lambda e, it=it, pb=pb: e.matmul(
                        bank(pb), hqTst[:, it * 128:(it + 1) * 128],
                        wdT[:, it * 1024: it * 1024 + 512], start=(it == 0), stop=(it == IT - 1)))
                    mv = em.op("T", sT, lambda e, it=it, pb=pb: e.matmul(
                        bank(pb + 1), hqTst[:, it * 128:(it + 1) * 128],
                        wdT[:, it * 1024 + 512: it * 1024 + 1024], start=(it == 0), stop=(it == IT - 1)))
                em.wait("V", sT, mv)
                em.wait("S", sV, xld_rd[s5])
                em.wait("S", SEMS["outst"], SEMS["outst"].n)
                xv = em.dma("S", sxg[s5], xfb[s5][:], xpD.ap()[xrow + sgq * 128: xrow + (sgq + 1) * 128, :])
                em.op("V", sV, lambda e, pb=pb: e.bn_stats(bnb[:, 4:10], bank(pb)))
                em.op("V", sV, lambda e, pb=pb: e.bn_stats(bnb[:, 10:16], bank(pb + 1)))
                em.wait("V", sV, sV.n)
                em.op("V", sV, lambda e: e.bn_aggr(st2[:], bnb[:, 4:16]))
                em.wait("V", sV, sV.n)
                em.op("V", sV, lambda e, qq=qq: e.tensor_tensor(veps[:], sdA[:, qq:qq + 1], sdA[:, qq:qq + 1], AL.mult))
                vw = sV.n
                em.wait("S", sV, vw)
                sw = em.op("S", sS, lambda e: e.activation(sq[:], st2[:, 1:2], AF.Sqrt, bias=wscB[:, 9:10], scale=veps[:]))
                em.wait("V", sS, sw)
                em.op("V", sV, lambda e: e.reciprocal(rstd[:], sq[:]))
                em.wait("V", sV, sV.n)
                em.op("V", sV, lambda e, qq=qq: e.tensor_tensor(rmr[:], sdA[:, qq:qq + 1], rstd[:], AL.mult))
                em.wait("V", sV, sV.n)
                em.op("V", sV, lambda e: e.scalar_tensor_tensor(nmb[:], st2[:, 0:1], -1.0, rmr[:], AL.mult, AL.mult))
                cw = sV.n
                em.wait("S", sV, cw)
                em.wait("S", sV, dn_rd[0])
                yv = em.op("S", sS, lambda e, pb=pb: e.activation(
                    dnT[:], psF[:, pb * 512: pb * 512 + 1024], AF.Identity, bias=nmb[:], scale=rmr[:]))
                pair_s[dpair] = yv
                em.wait("V", sS, yv)
                em.wait("V", sxg[s5], SEMS[f"p5x{s5}"].n)
                ov = em.op("V", sV, lambda e, s5=s5: e.tensor_add(xfb[s5][:], xfb[s5][:], dnT[:]))
                dn_rd[0] = ov
                xld_rd[s5] = ov
                pair_v[dpair] = ov
                em.wait("Y", sV, ov)
                em.dma("Y", sout, out.ap()[xrow + sgq * 128: xrow + (sgq + 1) * 128, :], xfb[s5][:])

            dn_rd = [0]
            xld_rd = [sV.n, sV.n]
            for qi in range(LT):
                s = qi % 3
                dprev = (s + 2) % 3
                ogp = [(dprev, s), ((s + 1) % 3, dprev), (s, (s + 1) % 3), ((s + 1) % 3, s)]
                for og in range(4):
                    if og == 2 and pend is not None:
                        fused_tail(pend)
                        pend = None
                    gp, up = ogp[og]
                    mv = None
                    if og == 3:
                        em.wait("T", sS, pair_s[up])
                        em.wait("T", sV, pair_v[up])
                        for it in range(8):
                            fap = bass.AP(featA, it * 2048 + qi * 128, [[16384, 128], [1, 128]])
                            for oc in range(2):
                                col = og * 1024 + oc * 512
                                em.op("T", sT, lambda e, a=fap, it=it, oc=oc, col=col, up=up: e.matmul(
                                    bank(2 * up + oc), a, wuT8[:, it * 4096 + col: it * 4096 + col + 512],
                                    start=(it == 0), stop=(it == 7)))
                        em.wait("T", sS, pair_s[gp])
                        em.wait("T", sV, pair_v[gp])
                        for it in range(8):
                            fap = bass.AP(featA, it * 2048 + qi * 128, [[16384, 128], [1, 128]])
                            for oc in range(2):
                                col = og * 1024 + oc * 512
                                mv = em.op("T", sT, lambda e, a=fap, it=it, oc=oc, col=col, gp=gp: e.matmul(
                                    bank(2 * gp + oc), a, wgT[:, it * 4096 + col: it * 4096 + col + 512],
                                    start=(it == 0), stop=(it == 7)))
                    else:
                        for it in range(8):
                            fap = bass.AP(featA, it * 2048 + qi * 128, [[16384, 128], [1, 128]])
                            for oc in range(2):
                                col = og * 1024 + oc * 512
                                if it == 0 and oc == 0:
                                    em.wait("T", sS, max(pair_s[gp], pair_s[up]))
                                    em.wait("T", sV, max(pair_v[gp], pair_v[up]))
                                em.op("T", sT, lambda e, a=fap, it=it, oc=oc, col=col, gp=gp: e.matmul(
                                    bank(2 * gp + oc), a, wgT[:, it * 4096 + col: it * 4096 + col + 512],
                                    start=(it == 0), stop=(it == 7)))
                                mv = em.op("T", sT, lambda e, a=fap, it=it, oc=oc, col=col, up=up: e.matmul(
                                    bank(2 * up + oc), a, wuT8[:, it * 4096 + col: it * 4096 + col + 512],
                                    start=(it == 0), stop=(it == 7)))
                    em.wait("S", sT, mv)
                    for oc in range(2):
                        col = og * 1024 + oc * 512
                        em.wait("S", sV, gb_rd[oc])
                        gev = em.op("S", sS, lambda e, oc=oc, qi=qi, gp=gp: e.activation(
                            gbuf[:, oc * 512:(oc + 1) * 512], bank(2 * gp + oc), AF.Silu, scale=sgA[:, qi:qi + 1]))
                        pair_s[gp] = gev
                        em.wait("V", sS, gev)
                        hv = em.op("V", sV, lambda e, oc=oc, qi=qi, col=col, up=up: e.scalar_tensor_tensor(
                            hbuf[:, col:col + 512], bank(2 * up + oc), sgA[:, qi:qi + 1],
                            gbuf[:, oc * 512:(oc + 1) * 512], AL.mult, AL.mult))
                        pair_v[up] = hv
                        gb_rd[oc] = hv
                    em.wait("V", sV, sV.n)
                    em.op("V", sV, lambda e, og=og: e.tensor_reduce(
                        bnb[:, og:og + 1], hbuf[:, og * 1024:(og + 1) * 1024], AX.X, AL.max,
                        apply_absolute_value=True))
                em.wait("V", sV, sV.n)
                em.op("V", sV, lambda e: e.tensor_reduce(rmx[:], bnb[:, 0:4], AX.X, AL.max))
                em.wait("V", sV, sV.n)
                em.op("V", sV, lambda e: e.tensor_scalar(rmx[:], rmx[:], 1e-5, None, AL.max))
                em.wait("V", sV, sV.n)
                em.op("V", sV, lambda e: e.reciprocal(rmr[:], rmx[:]))
                em.wait("V", sV, sV.n)
                em.op("V", sV, lambda e, qi=qi: e.scalar_tensor_tensor(
                    sdA[:, qi:qi + 1], rmx[:], 1.0 / 127.0, wscB[:, 2:3], AL.mult, AL.mult))
                em.wait("V", sV, sV.n)
                r127 = em.op("V", sV, lambda e: e.tensor_scalar(
                    bnb[:, 16:17], rmr[:], 127.0, None, AL.mult))
                em.wait("S", sV, r127)
                t0_free = 0
                hq_quarters = []
                for oh in range(4):
                    cs = slice(oh * 1024, (oh + 1) * 1024)
                    uqv = em.op("S", sS, lambda e, cs=cs: e.activation(
                        hqsb[:, cs], hbuf[:, cs], AF.Identity, scale=bnb[:, 16:17]))
                    em.wait("S", sS, uqv)
                    c1 = em.op("S", sS, lambda e, cs=cs: e.activation(i8h[:, cs], hqsb[:, cs], AF.Identity))
                    em.wait("S", sS, c1)
                    em.wait("S", sV, t0_free)
                    c2 = em.op("S", sS, lambda e, cs=cs: e.activation(t0f[:], i8h[:, cs], AF.Identity))
                    em.wait("V", sS, c2)
                    em.op("V", sV, lambda e, cs=cs: e.tensor_sub(hqsb[:, cs], hqsb[:, cs], t0f[:]))
                    em.wait("V", sV, sV.n)
                    em.op("V", sV, lambda e, cs=cs: e.scalar_tensor_tensor(t0f[:], hqsb[:, cs], 0.5, t0f[:], AL.is_ge, AL.add))
                    em.wait("V", sV, sV.n)
                    em.op("V", sV, lambda e, cs=cs: e.tensor_scalar(hqsb[:, cs], hqsb[:, cs], -0.5, None, AL.is_le))
                    em.wait("V", sV, sV.n)
                    hqv = em.op("V", sV, lambda e, cs=cs: e.tensor_sub(hqsb[:, cs], t0f[:], hqsb[:, cs]))
                    t0_free = hqv
                    hq_quarters.append(hqv)
                if hqD is not None:
                    em.wait("Y", sV, sV.n)
                    dhq = em.dma("Y", sem("dbghq"), hqD.ap()[xrow + sigma(qi) * 128: xrow + (sigma(qi) + 1) * 128, :], hqsb[:])
                    em.wait("V", SEMS["dbghq"], dhq)
                pend = (qi, hq_quarters, sigma(qi), qi % 3, qi % 2)
                if qi == LT - 1:
                    fused_tail(pend)
                    pend = None

    if "p4" in phases:
        em.wait("Y", SEMS["outst"], SEMS["outst"].n)

    with nc.Block() as block:
        @block.tensor
        def _(e):
            for f in em.q["T"]:
                f(e)

        @block.vector
        def _(e):
            for f in em.q["V"]:
                f(e)

        @block.scalar
        def _(e):
            for f in em.q["S"]:
                f(e)

        @block.gpsimd
        def _(e):
            for f in em.q["G"]:
                f(e)

        @block.sync
        def _(e):
            for f in em.q["Y"]:
                f(e)
    stack.close()
    return nc


from concourse.bass_utils import run_bass_kernel_spmd
import ml_dtypes


def _sel_const():
    s = np.zeros((16, 1024), dtype=np.float32)
    for p in range(8):
        for m in range(128):
            s[2 * p + (m >= 64), p * 128 + m] = 1.0
    return s.astype(ml_dtypes.bfloat16)


def kernel(L0_attn_xin, L0_attn_W_qkv, L0_attn_W_o, L0_ffn_w_gate, L0_ffn_w_up, L0_ffn_w_down):
    x = np.ascontiguousarray(np.asarray(L0_attn_xin, dtype=np.float32))
    consts = {
        "wqkv": np.ascontiguousarray(np.asarray(L0_attn_W_qkv, dtype=np.float32)),
        "wo": np.ascontiguousarray(np.asarray(L0_attn_W_o, dtype=np.float32)),
        "wg": np.ascontiguousarray(np.asarray(L0_ffn_w_gate, dtype=np.float32)),
        "wu": np.ascontiguousarray(np.asarray(L0_ffn_w_up, dtype=np.float32)),
        "wd": np.ascontiguousarray(np.asarray(L0_ffn_w_down, dtype=np.float32)),
        "identc": np.eye(128, dtype=ml_dtypes.bfloat16),
        "onesc": np.ones((128, 128), dtype=ml_dtypes.bfloat16),
        "selc": _sel_const(),
    }
    nc = bass.Bass("TRN2", target_bir_lowering=False)
    build(nc)
    in_maps = []
    for c in range(8):
        im = dict(consts)
        im["xin"] = np.ascontiguousarray(x[2 * c: 2 * c + 2].reshape(2 * L, D))
        in_maps.append(im)
    res = run_bass_kernel_spmd(nc, in_maps, core_ids=list(range(8)))
    out = np.empty((16, L, D), dtype=np.float32)
    for c in range(8):
        out[2 * c: 2 * c + 2] = res.results[c]["out"].reshape(2, L, D)
    return out


# version 30
# speedup vs baseline: 1.0257x; 1.0257x over previous
import contextlib
import numpy as np
import concourse.bass as bass
import concourse.mybir as mybir

F32 = mybir.dt.float32
FP8 = mybir.dt.float8e4
BF16 = mybir.dt.bfloat16
I8 = mybir.dt.int8
AL = mybir.AluOpType
AF = mybir.ActivationFunctionType
AX = mybir.AxisListType

B_PER_CORE = 2
L = 2048
D = 1024
N3 = 3072
INNER = 4096
LT, CT, IT = 16, 8, 32
EPS = 1e-5


def sigma(qi):
    return qi


class Sem:
    __slots__ = ("h", "name", "n")

    def __init__(self, h, name):
        self.h, self.name, self.n = h, name, 0


class Em:
    def __init__(self):
        self.q = {k: [] for k in ("T", "V", "S", "G", "Y")}

    def add(self, eng, fn):
        self.q[eng].append(fn)

    def op(self, eng, sem, fn):
        sem.n += 1
        v = sem.n
        self.q[eng].append(lambda e, f=fn, s=sem: f(e).then_inc(s.h, 1))
        return v

    def dma(self, eng, sem, out_ap, in_ap):
        sem.n += 16
        v = sem.n
        self.q[eng].append(lambda e, o=out_ap, i=in_ap, s=sem: e.dma_start(o, i).then_inc(s.h, 16))
        return v

    def dma_t(self, eng, sem, out_ap, in_ap):
        sem.n += 16
        v = sem.n
        self.q[eng].append(lambda e, o=out_ap, i=in_ap, s=sem: e.dma_start_transpose(o, i).then_inc(s.h, 16))
        return v

    def wait(self, eng, sem, val):
        if val:
            self.q[eng].append(lambda e, s=sem, v=val: e.wait_ge(s.h, v))


def build(nc, dbg=(), phases=("prep", "p1", "p2", "p3", "p4", "p5")):
    dbg = set(dbg)
    xin = nc.dram_tensor("xin", [B_PER_CORE * L, D], F32, kind="ExternalInput")
    Wqkv = nc.dram_tensor("wqkv", [D, N3], F32, kind="ExternalInput")
    Wo = nc.dram_tensor("wo", [D, D], F32, kind="ExternalInput")
    wg = nc.dram_tensor("wg", [INNER, D], F32, kind="ExternalInput")
    wu = nc.dram_tensor("wu", [INNER, D], F32, kind="ExternalInput")
    wd = nc.dram_tensor("wd", [D, INNER], F32, kind="ExternalInput")
    identC = nc.dram_tensor("identc", [128, 128], BF16, kind="ExternalInput")
    onesC = nc.dram_tensor("onesc", [128, 128], BF16, kind="ExternalInput")
    selC = nc.dram_tensor("selc", [16, 1024], BF16, kind="ExternalInput")
    out = nc.dram_tensor("out", [B_PER_CORE * L, D], F32, kind="ExternalOutput")

    def scr(name, shape, dt):
        return nc.dram_tensor(name, shape, dt,
                              kind="ExternalOutput" if name in dbg else "Internal")

    qkvD = scr("qkvd", [B_PER_CORE * L * 48, 64], BF16)
    wgTD = scr("wgtd", [D, INNER], FP8)
    wuTD = scr("wutd", [D, INNER], FP8)
    wdTD = scr("wdtd", [INNER, D], FP8)
    xpD = scr("xpd", [B_PER_CORE * L, D], F32)
    zrowD = scr("zrowd", [16, L], BF16)
    hqTD = scr("hqtd", [B_PER_CORE * LT * 128, INNER], BF16)
    wscD = scr("wscd", [1, 12], F32)
    azD = scr("azd", [B_PER_CORE * 128, 16384], BF16) if "azd" in dbg else None
    hqD = scr("hqd", [B_PER_CORE * L, INNER], BF16) if "hqd" in dbg else None
    xqD = scr("xqd", [B_PER_CORE * L, D], BF16) if "xqd" in dbg else None

    A = lambda n, s, d, o: nc.alloc_sbuf_tensor_at(n, s, d, offset=o)
    MB = 18432
    W1, W2, W3, W4 = MB, MB + 65536, MB + 131072, MB + 163840
    wtile = [A(f"wtile{i}", [128, 4096], F32, W2 + 32768 + i * 16384) for i in range(2)]
    wq8f = A("wq8f", [128, 4096], F32, W4 + 11264)
    wqb = A("wqb", [128, 4096], BF16, W4 + 27648)
    wqb2 = A("wqb2", [128, 4096], BF16, W3)
    wts = [A(f"wts{i}", [128, 1024], FP8, W4 + 35840 + i * 1024) for i in range(2)]
    pbn = A("pbn", [128, 32], F32, W4 + 37888)
    thr2 = A("thr2", [1, 6], F32, W4 + 44096)
    WqB = A("wqb_full", [128, 8 * 3072], BF16, W1)
    qsbS = [A(f"qsb{i}", [128, 1536], BF16, W1 + 49152 + i * 3072) for i in range(2)]
    qkS = [A(f"qks{i}", [128, 16 * 64], BF16, W1 + i * 2048) for i in range(4)]
    vS = [A(f"vs{i}", [128, 16 * 65], BF16, W1 + 8192 + i * 2112) for i in range(2)]
    qT1 = [A(f"qt1_{i}", [128, L], BF16, W1 + 12544 + i * 4096) for i in range(2)]
    kT1 = [A(f"kt1_{i}", [128, L], BF16, W1 + 20736 + i * 4096) for i in range(2)]
    phS = [A(f"phs{i}", [128, 1024], BF16, W1 + 28928 + i * 2048) for i in range(3)]
    azst = [A(f"azst{i}", [65, L], BF16, W1 + 35072 + i * 4096) for i in range(2)]
    zall = A("zall", [16, L], BF16, W1 + 43264)
    zala = A("zala", [128, 256], BF16, W1 + 43264)
    zif2 = A("zif2", [128, 256], F32, W1 + 47360)
    zi2 = A("zi2", [128, 256], BF16, W1 + 55552)
    zinvF = A("zinvf", [16, L], F32, W1 + 47360)
    zinv = A("zinv", [16, L], BF16, W1 + 55552)
    wgT = A("wgt", [128, 8 * 4096], FP8, W1)
    wuT8 = A("wut8", [128, 8 * 4096], FP8, W1 + 32768)
    wdT = A("wdt", [128, 32 * 1024], FP8, W2)
    featB = A("featb", [128, 8 * 2048], BF16, W2)
    x2buf = A("x2buf", [128, 1024], F32, W2 + 32768)
    ybuf = A("ybuf", [128, 1024], F32, W2 + 36864)
    xpbuf = [A(f"xpbuf{i}", [128, 1024], F32, W2 + 40960 + i * 4096) for i in range(2)]
    WoB = A("wob", [128, 8 * 1024], BF16, W2 + 49152)
    i8buf = A("i8buf", [128, 1024], I8, W4 + 13312)
    xqsb = A("xqsb", [128, 1024], BF16, W4 + 11264)
    featA = A("feata", [128, 8 * 2048], BF16, W3)
    xb = [A(f"xb{i}", [128, 1024], F32, W4 + i * 4096) for i in range(2)]
    gbuf = A("gbuf", [128, 1024], F32, W4)
    bnb = A("bnb", [128, 48], F32, W4 + 8192)
    st2 = A("st2", [128, 2], F32, W4 + 8256)
    veps = A("veps", [128, 1], F32, W4 + 8288)
    sq = A("sq", [128, 1], F32, W4 + 8320)
    rstd = A("rstd", [128, 1], F32, W4 + 8352)
    nmb = A("nmb", [128, 1], F32, W4 + 8384)
    rmx = A("rmx", [128, 1], F32, W4 + 8416)
    rmr = A("rmr", [128, 1], F32, W4 + 8448)
    accw = A("accw", [128, 4], F32, W4 + 8480)
    sgA = A("sga", [128, 16], F32, W4 + 8512)
    sdA = A("sda", [128, 16], F32, W4 + 8576)
    wsc = A("wsc", [1, 4], F32, W4 + 8640)
    red1 = A("red1", [1, 1], F32, W4 + 8672)
    wscB = A("wscb", [128, 12], F32, W4 + 44032)
    idb = A("idb", [128, 128], BF16, W4 + 8704)
    onb = A("onb", [128, 128], BF16, W4 + 8960)
    selB = A("selb", [16, 1024], BF16, W4 + 9216)
    i8h = A("i8h", [128, 4096], I8, W4 + 4096)
    hbuf = A("hbuf", [128, 4096], F32, W4 + 11264)
    hqsb = A("hqsb", [128, 4096], BF16, W4 + 27648)
    hqTst = A("hqtst", [128, 4096], BF16, W4 + 35840)
    t0f = A("t0f", [128, 1024], BF16, W4 + 44128)
    wq_f32a = A("wqf32a", [128, 3072], F32, W4 + 14336)
    wq_f32b = A("wqf32b", [128, 3072], F32, W4 + 27648)
    hq2 = [A(f"hq2_{i}", [128, 4096], BF16, W4 + 11264 + i * 8192) for i in range(2)]
    dnT = A("dnt", [128, 1024], F32, W2 + 32768)
    xfb = [A(f"xfb{i}", [128, 1024], F32, W2 + 36864 + i * 4096) for i in range(2)]

    em = Em()
    stack = contextlib.ExitStack()
    SEMS = {}

    def sem(name):
        if name not in SEMS:
            SEMS[name] = Sem(stack.enter_context(nc.semaphore(name)), name)
        return SEMS[name]

    sT, sV, sS, sG = sem("c_t"), sem("c_v"), sem("c_s"), sem("c_g")
    psF = stack.enter_context(nc.psum_tensor("psf", [128, 3072], F32))
    psB = stack.enter_context(nc.psum_tensor("psb", [128, 1024], BF16))
    psB2 = stack.enter_context(nc.psum_tensor("psb2", [128, 1024], BF16))
    bank = lambda i: psF[:, i * 512:(i + 1) * 512]

    def ln_consts(src_aps):
        em.wait("V", sV, sV.n)
        for c, ap in enumerate(src_aps):
            em.op("V", sV, lambda e, c=c, a=ap: e.bn_stats(bnb[:, c * 6:(c + 1) * 6], a))
        em.wait("V", sV, sV.n)
        em.op("V", sV, lambda e, n=len(src_aps): e.bn_aggr(st2[:], bnb[:, 0:6 * n]))
        vw = sV.n
        em.wait("S", sV, vw)
        sw = em.op("S", sS, lambda e: e.activation(sq[:], st2[:, 1:2], AF.Sqrt, bias=wscB[:, 9:10]))
        em.wait("V", sS, sw)
        em.op("V", sV, lambda e: e.reciprocal(rstd[:], sq[:]))
        em.wait("V", sV, sV.n)
        em.op("V", sV, lambda e: e.scalar_tensor_tensor(nmb[:], st2[:, 0:1], -1.0, rstd[:], AL.mult, AL.mult))
        cw = sV.n
        return cw

    idv = onv = None
    prep_done_y = 0

    PREP = {"wscB_ready": 0, "done": (0, 0)}
    def prep_gen():
        gld = [sem("pw0"), sem("pw1")]
        pst = [sem("pwst0"), sem("pwst1")]
        sbc = sem("sbc")
        WSPEC = ((wg, 32, 1024, wgTD, INNER), (wu, 32, 1024, wuTD, INNER),
                 (wd, 8, 4096, wdTD, D))
        slot_rd = [0, 0]
        stage_rd = [0, 0]
        for wi, (W, n_t, tw, dst, dcols) in enumerate(WSPEC):
            for t in range(n_t):
                s = t % 2
                em.wait("Y", sV, slot_rd[s])
                lv = em.dma("Y", gld[s], wtile[s][:, 0:tw], W.ap()[t * 128:(t + 1) * 128, :])
                em.wait("V", gld[s], lv)
                if t == 0:
                    em.wait("V", sV, sV.n)
                em.op("V", sV, lambda e, s=s, w=tw, t=t: e.tensor_reduce(
                    pbn[:, t % 32: t % 32 + 1], wtile[s][:, 0:w], AX.X, AL.add,
                    apply_absolute_value=True))
                slot_rd[s] = sV.n
                if t == n_t - 1:
                    em.wait("V", sV, sV.n)
                    em.op("V", sV, lambda e, wi=wi, n_t=n_t: e.tensor_reduce(
                        accw[:, wi:wi + 1], pbn[:, 0:n_t] if n_t <= 32 else pbn[:, 0:32],
                        AX.X, AL.add))
                yield
        em.wait("G", sV, sV.n)
        gv = em.op("G", sG, lambda e: e.tensor_reduce(red1[0:1, 0:1], accw[:, 0:1], AX.C, AL.add))
        gv = em.op("G", sG, lambda e: e.tensor_reduce(wsc[0:1, 3:4], accw[:, 1:2], AX.C, AL.add))
        gv = em.op("G", sG, lambda e: e.tensor_reduce(thr2[0:1, 0:1], accw[:, 2:3], AX.C, AL.add))
        em.wait("V", sG, gv)
        for wi, (W, n_t, tw, dst, dcols) in enumerate(WSPEC):
            NN = float(n_t * 128 * tw)
            srcap = (red1[0:1, 0:1], wsc[0:1, 3:4], thr2[0:1, 0:1])[wi]
            em.op("V", sV, lambda e, wi=wi, NN=NN, a=srcap: e.tensor_scalar(
                wsc[:, wi:wi + 1], a, 1.0 / NN, 1e-5, AL.mult, AL.max))
        em.wait("V", sV, sV.n)
        for wi in range(3):
            em.op("V", sV, lambda e, wi=wi: e.tensor_scalar(thr2[:, 2 * wi:2 * wi + 1], wsc[:, wi:wi + 1], 0.5, None, AL.mult))
            em.op("V", sV, lambda e, wi=wi: e.tensor_scalar(thr2[:, 2 * wi + 1:2 * wi + 2], wsc[:, wi:wi + 1], -0.5, None, AL.mult))
        tv_ = sV.n
        em.wait("G", sV, tv_)
        bv = em.dma("G", sbc, wscD.ap()[0:1, 3:9], thr2[:])
        bv2 = em.dma("G", sbc, wscD.ap()[0:1, 0:3], wsc[:, 0:3])
        em.wait("G", sbc, bv2)
        bv = em.dma("G", sbc, wscB[:, 3:9], bass.AP(wscD, 3, [[1, 1], [0, 128], [1, 6]]))
        bv2 = em.dma("G", sbc, wscB[:, 0:3], bass.AP(wscD, 0, [[1, 1], [0, 128], [1, 3]]))
        PREP["wscB_ready"] = bv2
        thr_ready = bv2
        yield
        em.wait("V", sT, sT.n)
        wbufs = [wqb, wqb2]
        wqb_rd = [0, 0]
        pend2 = [None]

        def flush2():
            t_, tw_, dst_, dcols_, cpv_, wb_, tg_ = pend2[0]
            em.wait("T", sV, cpv_)
            nblk = tw_ // 128
            tvl = None
            for b0 in range(0, nblk, 8):
                nb = min(8, nblk - b0)
                ss = (b0 // 8) % 2
                tv = None
                for j in range(nb):
                    tv = em.op("T", sT, lambda e, j=j, b0=b0, wb=wb_: e.transpose(
                        psB[:, j * 128:(j + 1) * 128], wb[:, (b0 + j) * 128:(b0 + j + 1) * 128], idb[:]))
                tvl = tv
                em.wait("V", sT, tv)
                em.wait("V", pst[ss], stage_rd[ss])
                scv = em.op("V", sV, lambda e, nb=nb, ss=ss: e.tensor_copy(
                    wts[ss][:, 0:nb * 128], psB[:, 0:nb * 128]))
                em.wait("T", sV, scv)
                em.wait("Y", sV, scv)
                dst_ap = bass.AP(dst_, (b0 * 128) * dcols_ + t_ * 128,
                                 [[dcols_, 128], [128 * dcols_, nb], [1, 128]])
                src_ap = bass.AP(wts[ss], 0, [[1024, 128], [128, nb], [1, 128]])
                stage_rd[ss] = em.dma("Y", pst[ss], dst_ap, src_ap)
            wqb_rd[tg_ % 2] = tvl
            pend2[0] = None
        tg = 0
        for wi, (W, n_t, tw, dst, dcols) in enumerate(WSPEC):
            for t in range(n_t):
                s = tg % 2
                em.wait("Y", sV, slot_rd[s])
                lv = em.dma("Y", gld[s], wtile[s][:, 0:tw], W.ap()[t * 128:(t + 1) * 128, :])
                em.wait("V", gld[s], lv)
                if wi == 0 and t == 0:
                    em.wait("V", SEMS["sbc"], thr_ready)
                em.wait("V", sV, sV.n)
                em.op("V", sV, lambda e, s=s, w=tw, wi=wi: e.tensor_scalar(
                    wq8f[:, 0:w], wtile[s][:, 0:w], wscB[:, 4 + 2 * wi:5 + 2 * wi], None, AL.is_le))
                em.wait("V", sV, sV.n)
                em.wait("V", sT, wqb_rd[tg % 2])
                cpv = em.op("V", sV, lambda e, s=s, w=tw, wi=wi, wb=wbufs[tg % 2]: e.scalar_tensor_tensor(
                    wb[:, 0:w], wtile[s][:, 0:w], wscB[:, 3 + 2 * wi:4 + 2 * wi],
                    wq8f[:, 0:w], AL.is_ge, AL.subtract))
                slot_rd[s] = cpv
                if pend2[0] is not None:
                    flush2()
                pend2[0] = (t, tw, dst, dcols, cpv, wbufs[tg % 2], tg)
                tg += 1
                yield
        flush2()
        PREP["done"] = (SEMS["pwst0"].n, SEMS["pwst1"].n)

    _pgen = [None]

    def pump_prep(n):
        if "prep" not in phases:
            return
        if _pgen[0] is None:
            _pgen[0] = prep_gen()
        for _ in range(n):
            try:
                next(_pgen[0])
            except StopIteration:
                break

    sx = [sem("sx0"), sem("sx1")]
    sxp = [sem("sxp0"), sem("sxp1")]
    for b in range(B_PER_CORE):
        xrow = b * L
        if "p1" in phases:
            swq2 = [sem("p1w0"), sem("p1w1")]
            sqk = [sem("p1q0"), sem("p1q1")]
            if idv is None:
                gid = sem("gid")
                epsv = em.op("V", sV, lambda e: e.memset(wscB[:, 9:10], EPS))
                em.wait("S", sV, epsv)
                em.wait("V", sV, epsv)
                idv = em.dma("G", gid, idb[:], identC.ap())
                onv = em.dma("G", gid, onb[:], onesC.ap())
                onv = em.dma("G", gid, selB[:], selC.ap())
                em.wait("T", gid, onv)
            em.wait("G", sV, sV.n)
            em.wait("G", sT, sT.n)
            em.wait("T", sV, sV.n)
            em.wait("T", sS, sS.n)
            em.wait("S", sV, sV.n)
            if "outst" in SEMS:
                em.wait("S", SEMS["outst"], SEMS["outst"].n)
            wqstg = [wq_f32a, wq_f32b]
            wq_cast = [sV.n, sV.n]
            wv = 0
            for ctw in range(CT):
                sw_ = ctw % 2
                em.wait("Y", sV, wq_cast[sw_])
                lvw = em.dma("Y", swq2[sw_], wqstg[sw_][:],
                             Wqkv.ap()[ctw * 128:(ctw + 1) * 128, :])
                em.wait("V", swq2[sw_], lvw)
                wq_cast[sw_] = em.op("V", sV, lambda e, ctw=ctw, sw_=sw_: e.tensor_copy(
                    WqB[:, ctw * 3072:(ctw + 1) * 3072], wqstg[sw_][:]))
            wv = max(wq_cast)
            x1T_ready = [0] * LT
            xb_rd = [sS.n, sS.n]
            qkv_ev = [0, 0]
            qsb_rd = [0, 0]
            qkv_store_tile = []
            wq_started = False
            for lt in range(LT):
                s = lt % 2
                em.wait("S", sS, xb_rd[s])
                lv = em.dma("S", sx[s], xb[s][:], xin.ap()[xrow + lt * 128: xrow + (lt + 1) * 128, :])
                em.wait("V", sx[s], lv)
                cw = ln_consts([xb[s][:, 0:512], xb[s][:, 512:1024]])
                em.wait("S", sV, cw)
                av = em.op("S", sS, lambda e, s=s: e.activation(
                    xqsb[:], xb[s][:], AF.Identity, bias=nmb[:], scale=rstd[:]))
                xb_rd[s] = av
                em.wait("T", sS, av)
                tv = None
                for ct in range(CT):
                    tv = em.op("T", sT, lambda e, ct=ct: e.transpose(
                        psB[:, ct * 128:(ct + 1) * 128], xqsb[:, ct * 128:(ct + 1) * 128], idb[:]))
                em.wait("V", sT, tv)
                cpv = em.op("V", sV, lambda e, lt=lt: e.tensor_copy(
                    bass.AP(featA, lt * 128, [[16384, 128], [2048, 8], [1, 128]]),
                    bass.AP(psB, 0, [[1024, 128], [128, 8], [1, 128]])))
                x1T_ready[lt] = cpv
                em.wait("T", sV, cpv)
                if not wq_started:
                    em.wait("T", sV, wv)
                    wq_started = True
                for hf in range(2):
                    em.wait("T", sV, qkv_ev[hf])
                    mv = None
                    for ct in range(CT):
                        fap = bass.AP(featA, ct * 2048 + lt * 128, [[16384, 128], [1, 128]])
                        for nc3 in range(3):
                            nc6 = hf * 3 + nc3
                            mv = em.op("T", sT, lambda e, a=fap, ct=ct, nc6=nc6, bk=hf * 3 + nc3: e.matmul(
                                bank(bk), a, WqB[:, ct * 3072 + nc6 * 512: ct * 3072 + nc6 * 512 + 512],
                                start=(ct == 0), stop=(ct == CT - 1)))
                    em.wait("V", sT, mv)
                    em.wait("V", sqk[hf], qsb_rd[hf])
                    ev = em.op("V", sV, lambda e, hf=hf: e.tensor_copy(
                        qsbS[hf][:], psF[:, hf * 1536: hf * 1536 + 1536]))
                    qkv_ev[hf] = ev
                    em.wait("Y", sV, ev)
                    dstap = bass.AP(qkvD, (xrow + lt * 128) * 48 * 64 + hf * 1536,
                                    [[3072, 128], [1, 1536]])
                    qsb_rd[hf] = em.dma("Y", sqk[hf], dstap, qsbS[hf][:])
                qkv_store_tile.append((SEMS["p1q0"].n, SEMS["p1q1"].n))
            p1q = (SEMS["p1q0"].n, SEMS["p1q1"].n)
        if "p2" in phases:
            sqh = [sem("p2q0"), sem("p2q1")]
            skh = [sem("p2k0"), sem("p2k1")]
            svh = [sem("p2v0"), sem("p2v1")]
            szc = sem("szc")
            zdma_last = szc.n
            em.wait("G", sT, sT.n)
            em.wait("V", sT, sT.n)
            em.wait("T", sV, sV.n)
            em.wait("T", sS, sS.n)
            for i in range(2):
                em.op("V", sV, lambda e, i=i: e.memset(
                    bass.AP(vS[i], 64, [[1040, 128], [65, 16], [1, 1]]), 1.0))
            vones = sV.n
            em.wait("G", sV, vones)
            em.wait("Y", sV, vones)
            em.wait("S", sV, vones)
            qk_rd = [0, 0]
            v_rd = [0, 0]
            dd_rd = [0, 0]
            az_ev = [0, 0]
            expv = [0, 0]
            pend_head = None
            av_done = 0
            zld = sem("zld")
            base = b * L * 48

            def head_ready(h):
                need = min(LT, (33 + h) * 2048 // (48 * 128) + 1)
                return qkv_store_tile[need - 1]
            for h in range(16):
                hs = h % 2
                rdy = head_ready(h)
                def apt(seg, dstbuf):
                    return (dstbuf[:], bass.AP(qkvD, (base + seg * L) * 64, [[128, 1024], [1, 128]]))
                em.wait("Y", SEMS["p1q0"], rdy[0])
                em.wait("Y", SEMS["p1q1"], rdy[1])
                em.wait("Y", sV, qk_rd[hs])
                o_, i_ = apt(h, qkS[hs])
                qv = em.dma_t("Y", sqh[hs], o_, i_)
                em.wait("Y", SEMS["p1q0"], rdy[0])
                em.wait("Y", SEMS["p1q1"], rdy[1])
                em.wait("Y", sV, qk_rd[hs])
                o_, i_ = apt(16 + h, qkS[2 + hs])
                kv = em.dma_t("Y", skh[hs], o_, i_)
                em.wait("G", SEMS["p1q0"], rdy[0])
                em.wait("G", SEMS["p1q1"], rdy[1])
                em.wait("G", sT, v_rd[hs])
                vv = em.dma("G", svh[hs],
                            bass.AP(vS[hs], 0, [[16 * 65, 128], [65, 16], [1, 64]]),
                            bass.AP(qkvD, (base + (32 + h) * L) * 64,
                                    [[64, 128], [128 * 64, 16], [1, 64]]))

                em.wait("V", sqh[hs], qv)
                em.wait("V", sT, dd_rd[hs])
                for hh in range(2):
                    em.op("V", sV, lambda e, hs=hs, hh=hh: e.tensor_copy(
                        bass.AP(qT1[hs], hh * 64 * L, [[L, 64], [2, 1024]]),
                        bass.AP(qkS[hs], 0, [[1024, 64], [1, 1024]])))
                    qtv = em.op("V", sV, lambda e, hs=hs, hh=hh: e.tensor_copy(
                        bass.AP(qT1[hs], hh * 64 * L + 1, [[L, 64], [2, 1024]]),
                        bass.AP(qkS[hs], 64 * 1024, [[1024, 64], [1, 1024]])))
                em.wait("V", skh[hs], kv)
                for hh in range(2):
                    em.op("V", sV, lambda e, hs=hs, hh=hh: e.tensor_copy(
                        bass.AP(kT1[hs], hh * 64 * L, [[L, 64], [2, 1024]]),
                        bass.AP(qkS[2 + hs], 0, [[1024, 64], [1, 1024]])))
                    ktv = em.op("V", sV, lambda e, hs=hs, hh=hh: e.tensor_copy(
                        bass.AP(kT1[hs], hh * 64 * L + 1, [[L, 64], [2, 1024]]),
                        bass.AP(qkS[2 + hs], 64 * 1024, [[1024, 64], [1, 1024]])))
                em.wait("T", sV, ktv)
                qk_rd[hs] = ktv
                if pend_head is not None:
                    ph, pazs = pend_head
                    em.wait("V", sT, av_done)
                    az_ev[1] = em.op("V", sV, lambda e, azs=pazs: e.tensor_copy(
                        azs[:, 3 * 512:4 * 512], psF[0:65, 5 * 512:6 * 512]))
                    pp_ = ph // 2
                    rlo_ = (ph % 2) * 64
                    em.wait("V", sV, sV.n)
                    em.wait("V", szc, zdma_last)
                    em.op("V", sV, lambda e, p=pp_, rlo=rlo_, azs=pazs: e.tensor_copy(
                        featB[rlo:rlo + 64, p * 2048:(p + 1) * 2048], azs[0:64, :]))
                    em.wait("G", sV, sV.n)
                    zdma_last = em.dma("G", szc, zrowD.ap()[ph:ph + 1, :], pazs[64:65, :])
                    pend_head = None
                em.wait("T", svh[hs], vv)
                if b == 0 and h == 0:
                    em.wait("T", sV, vones)
                azs = azst[h % 2]
                for s4 in range(4):
                    ab = 4 + (s4 % 2)
                    em.wait("T", sV, az_ev[s4 % 2])
                    for ktp in range(9):
                        pp = ktp % 2
                        if ktp < 8:
                            em.wait("T", sS, expv[pp])
                            for half in range(2):
                                kt = 2 * ktp + half
                                rr = slice(half * 64, half * 64 + 64)
                                smv = em.op("T", sT, lambda e, hs=hs, kt=kt, rr=rr, s4=s4, bk=2 * pp + half: e.matmul(
                                    bank(bk), kT1[hs][rr, kt * 128:(kt + 1) * 128],
                                    qT1[hs][rr, s4 * 512:(s4 + 1) * 512],
                                    start=True, stop=True, skip_group_check=True))
                            em.wait("S", sT, smv)
                            expv[pp] = em.op("S", sS, lambda e, pp=pp: e.activation(
                                phS[pp][:], psF[:, 2 * pp * 512: (2 * pp + 2) * 512], AF.Exp, scale=0.125))
                        if ktp >= 1:
                            qp = (ktp - 1) % 2
                            em.wait("T", sS, expv[qp])
                            for half in range(2):
                                kt = 2 * (ktp - 1) + half
                                em.op("T", sT, lambda e, hs=hs, kt=kt, ab=ab, qp=qp, half=half: e.matmul(
                                    psF[0:65, ab * 512:(ab + 1) * 512], vS[hs][:, kt * 65: kt * 65 + 65],
                                    phS[qp][:, half * 512:(half + 1) * 512],
                                    start=(kt == 0), stop=(kt == 15)))
                    v_rd[hs] = sT.n
                    dd_rd[hs] = sT.n
                    if s4 < 3:
                        em.wait("V", sT, sT.n)
                        col = s4 * 512
                        az_ev[s4 % 2] = em.op("V", sV, lambda e, ab=ab, col=col, azs=azs: e.tensor_copy(
                            azs[:, col:col + 512], psF[0:65, ab * 512:(ab + 1) * 512]))
                    else:
                        av_done = sT.n
                        pend_head = (h, azs)
                    if b == 0:
                        pump_prep(3)
            if b == 0:
                pump_prep(1000)
            if pend_head is not None:
                ph, pazs = pend_head
                em.wait("V", sT, av_done)
                em.op("V", sV, lambda e, azs=pazs: e.tensor_copy(
                    azs[:, 3 * 512:4 * 512], psF[0:65, 5 * 512:6 * 512]))
                em.wait("V", sV, sV.n)
                em.wait("V", szc, zdma_last)
                em.op("V", sV, lambda e, p=ph // 2, rlo=(ph % 2) * 64, azs=pazs: e.tensor_copy(
                    featB[rlo:rlo + 64, p * 2048:(p + 1) * 2048], azs[0:64, :]))
                em.wait("G", sV, sV.n)
                zdma_last = em.dma("G", szc, zrowD.ap()[ph:ph + 1, :], pazs[64:65, :])
                pend_head = None
            em.wait("G", szc, zdma_last)
            zlv = em.dma("G", zld, zala[:], bass.AP(zrowD, 0, [[256, 128], [1, 256]]))
            em.wait("V", zld, zlv)
            em.op("V", sV, lambda e: e.reciprocal(zif2[:], zala[:]))
            em.wait("V", sV, sV.n)
            zcv = em.op("V", sV, lambda e: e.tensor_copy(zi2[:], zif2[:]))
            em.wait("G", sV, zcv)
            zsv = em.dma("G", zld, bass.AP(zrowD, 0, [[256, 128], [1, 256]]), zi2[:])
            em.wait("G", zld, zsv)
            zlv2 = em.dma("G", zld, zinv[:], zrowD.ap())
            em.wait("V", zld, zlv2)
            em.wait("T", zld, zlv2)
            zb_rd = [0, 0]
            zidx = 0
            for p in range(8):
                for c in range(4):
                    bk = 4 + (zidx % 2)
                    em.wait("T", sV, zb_rd[zidx % 2])
                    m2 = em.op("T", sT, lambda e, p=p, c=c, bk=bk: e.matmul(
                        bank(bk), selB[0:16, p * 128:(p + 1) * 128],
                        zinv[0:16, c * 512:(c + 1) * 512], start=True, stop=True))
                    em.wait("V", sT, m2)
                    zb_rd[zidx % 2] = em.op("V", sV, lambda e, p=p, c=c, bk=bk: e.tensor_tensor(
                        featB[:, p * 2048 + c * 512: p * 2048 + (c + 1) * 512],
                        featB[:, p * 2048 + c * 512: p * 2048 + (c + 1) * 512],
                        bank(bk), AL.mult))
                    zidx += 1
            featB_done = sV.n
            if azD is not None:
                em.wait("Y", sV, featB_done)
                dv_ = em.dma("Y", sem("dbgaz"), azD.ap()[b * 128:(b + 1) * 128, :], featB[:])
                em.wait("V", SEMS["dbgaz"], dv_)
        if "p3" in phases:
            swo = sem("p3wo")
            em.wait("G", sV, sV.n)
            em.wait("G", sT, sT.n)
            if "prep" in phases:
                em.wait("G", SEMS["pwst0"], SEMS["pwst0"].n)
                em.wait("G", SEMS["pwst1"], SEMS["pwst1"].n)
            wov = em.dma("G", swo, bass.AP(WoB, 0, [[8192, 128], [1024, 8], [1, 1024]]),
                         bass.AP(Wo, 0, [[D, 128], [128 * D, 8], [1, 1024]]))
            em.wait("T", swo, wov)
            if "p4" in phases and "prep" in phases:
                swg = sem("p4wg")
                em.wait("G", SEMS["pwst0"], SEMS["pwst0"].n)
                em.wait("G", SEMS["pwst1"], SEMS["pwst1"].n)
                em.wait("G", sV, sV.n)
                em.wait("G", sT, sT.n)
                for nm in ("p2q0", "p2q1", "p2k0", "p2k1", "p2v0", "p2v1"):
                    if nm in SEMS:
                        em.wait("G", SEMS[nm], SEMS[nm].n)
                em.dma("G", swg, bass.AP(wgT, 0, [[32768, 128], [4096, 8], [1, INNER]]),
                       bass.AP(wgTD, 0, [[INNER, 128], [128 * INNER, 8], [1, INNER]]))
                wg_pref = em.dma("G", swg, bass.AP(wuT8, 0, [[32768, 128], [4096, 8], [1, INNER]]),
                                 bass.AP(wuTD, 0, [[INNER, 128], [128 * INNER, 8], [1, INNER]]))
            if "p2" in phases:
                em.wait("T", sV, featB_done)
            em.wait("T", sS, sS.n)
            if "outst" in SEMS:
                em.wait("V", SEMS["outst"], SEMS["outst"].n)
            prev_yv = [0, 0]
            xb_rd3 = [sV.n, sV.n]
            xp_wr = [SEMS["sxp0"].n, SEMS["sxp1"].n]
            x2_rd = 0
            t0_fr3 = 0
            pend_xq = None

            def flush_xq(p):
                if p is None:
                    return
                qq, qvv = p
                em.wait("T", sV, qvv)
                tv = None
                for ct in range(CT):
                    tv = em.op("T", sT, lambda e, ct=ct: e.transpose(
                        psB[:, ct * 128:(ct + 1) * 128], xqsb[:, ct * 128:(ct + 1) * 128], idb[:]))
                em.wait("V", sT, tv)
                cpv = em.op("V", sV, lambda e, qq=qq: e.tensor_copy(
                    bass.AP(featA, qq * 128, [[16384, 128], [2048, 8], [1, 128]]),
                    bass.AP(psB, 0, [[1024, 128], [128, 8], [1, 128]])))
                em.wait("T", sV, cpv)
            for qi in range(LT):
                sg = sigma(qi)
                s = qi % 2
                pb = s * 2
                em.wait("S", sV, xb_rd3[s])
                lvx = em.dma("S", sx[s], xb[s][:], xin.ap()[xrow + sg * 128: xrow + (sg + 1) * 128, :])
                em.wait("T", sS, prev_yv[s])
                mv = None
                for ft in range(8):
                    fap = featB[:, ft * 2048 + qi * 128: ft * 2048 + qi * 128 + 128]
                    em.op("T", sT, lambda e, a=fap, ft=ft, pb=pb: e.matmul(
                        bank(pb), a, WoB[:, ft * 1024: ft * 1024 + 512], start=(ft == 0), stop=(ft == 7)))
                    mv = em.op("T", sT, lambda e, a=fap, ft=ft, pb=pb: e.matmul(
                        bank(pb + 1), a, WoB[:, ft * 1024 + 512: ft * 1024 + 1024],
                        start=(ft == 0), stop=(ft == 7)))
                flush_xq(pend_xq)
                pend_xq = None
                em.wait("V", sT, mv)
                cw = ln_consts([bank(pb), bank(pb + 1)])
                em.wait("S", sV, cw)
                yv = em.op("S", sS, lambda e, pb=pb: e.activation(
                    ybuf[:], psF[:, pb * 512: pb * 512 + 1024], AF.Identity, bias=nmb[:], scale=rstd[:]))
                prev_yv[s] = yv
                em.wait("V", sS, yv)
                em.wait("V", sx[s], lvx)
                em.wait("V", sxp[s], xp_wr[s])
                xpv = em.op("V", sV, lambda e, s=s: e.tensor_add(xpbuf[s][:], xb[s][:], ybuf[:]))
                xb_rd3[s] = xpv
                em.wait("Y", sV, xpv)
                xp_wr[s] = em.dma("Y", sxp[s], xpD.ap()[xrow + sg * 128: xrow + (sg + 1) * 128, :],
                                  xpbuf[s][:])
                cw = ln_consts([xpbuf[s][:, 0:512], xpbuf[s][:, 512:1024]])
                em.wait("S", sV, cw)
                em.wait("S", sV, x2_rd)
                x2v = em.op("S", sS, lambda e, s=s: e.activation(
                    x2buf[:], xpbuf[s][:], AF.Identity, bias=nmb[:], scale=rstd[:]))
                em.wait("V", sS, x2v)
                em.op("V", sV, lambda e: e.tensor_reduce(rmx[:], x2buf[:], AX.X, AL.max, apply_absolute_value=True))
                em.wait("V", sV, sV.n)
                em.op("V", sV, lambda e: e.tensor_scalar(rmx[:], rmx[:], 1e-5, None, AL.max))
                em.wait("V", sV, sV.n)
                em.op("V", sV, lambda e: e.reciprocal(rmr[:], rmx[:]))
                em.wait("V", sV, sV.n)
                if "prep" in phases:
                    if qi == 0:
                        em.wait("V", SEMS["sbc"], PREP["wscB_ready"])
                    em.op("V", sV, lambda e, qi=qi: e.scalar_tensor_tensor(
                        sgA[:, qi:qi + 1], rmx[:], 1.0 / 127.0, wscB[:, 0:1], AL.mult, AL.mult))
                uq = em.op("V", sV, lambda e: e.tensor_scalar(xqsb[:], x2buf[:], rmr[:], 127.0, AL.mult, AL.mult))
                x2_rd = uq
                em.wait("V", sV, uq)
                c1 = em.op("V", sV, lambda e: e.tensor_copy(i8buf[:], xqsb[:]))
                em.wait("V", sV, c1)
                em.wait("V", sV, t0_fr3)
                c2 = em.op("V", sV, lambda e: e.tensor_copy(t0f[:], i8buf[:]))
                em.wait("V", sV, c2)
                em.op("V", sV, lambda e: e.tensor_sub(xqsb[:], xqsb[:], t0f[:]))
                em.wait("V", sV, sV.n)
                em.op("V", sV, lambda e: e.scalar_tensor_tensor(t0f[:], xqsb[:], 0.5, t0f[:], AL.is_ge, AL.add))
                em.wait("V", sV, sV.n)
                em.op("V", sV, lambda e: e.tensor_scalar(xqsb[:], xqsb[:], -0.5, None, AL.is_le))
                em.wait("V", sV, sV.n)
                qv = em.op("V", sV, lambda e: e.tensor_sub(xqsb[:], t0f[:], xqsb[:]))
                t0_fr3 = qv
                pend_xq = (qi, qv)
                if xqD is not None:
                    em.wait("Y", sV, qv)
                    dqv_ = em.dma("Y", sem("dbgxq"), xqD.ap()[xrow + sg * 128: xrow + (sg + 1) * 128, :], xqsb[:])
                    em.wait("V", SEMS["dbgxq"], dqv_)
            flush_xq(pend_xq)
            pend_xq = None
            xqT_done = sV.n
            p3_xp = (SEMS["sxp0"].n, SEMS["sxp1"].n)

        if "p4" in phases:
            swu = sem("p4wu")
            swd = sem("p5wd")
            sout = sem("outst")
            sxg = [sem("p5x0"), sem("p5x1")]
            em.wait("G", sT, sT.n)
            em.wait("G", sV, sV.n)
            em.wait("G", SEMS["sxp0"], SEMS["sxp0"].n)
            em.wait("G", SEMS["sxp1"], SEMS["sxp1"].n)
            em.wait("S", SEMS["sxp0"], SEMS["sxp0"].n)
            em.wait("S", SEMS["sxp1"], SEMS["sxp1"].n)
            dv = em.dma("G", swd, bass.AP(wdT, 0, [[32768, 128], [1024, 32], [1, D]]),
                        bass.AP(wdTD, 0, [[D, 128], [128 * D, 32], [1, D]]))
            em.wait("T", SEMS["p4wg"], wg_pref)
            em.wait("T", swd, dv)
            if "prep" in phases:
                em.wait("T", SEMS["pwst0"], PREP["done"][0])
                em.wait("T", SEMS["pwst1"], PREP["done"][1])
            em.wait("T", sV, xqT_done)
            em.wait("T", sS, sS.n)
            pair_s = [0, 0, 0]
            pair_v = [0, 0, 0]
            gb_rd = [0, 0]
            pend = None

            def fused_tail(p):
                bb_unused = None
                qq, quarters, sgq, dpair, s5 = p
                cpvs = [0, 0]
                for oh in range(4):
                    pb_ = (psB, psB2)[oh % 2]
                    em.wait("T", sV, quarters[oh])
                    em.wait("T", sS, cpvs[oh % 2])
                    tv = None
                    for j in range(8):
                        blk = oh * 8 + j
                        tv = em.op("T", sT, lambda e, blk=blk, j=j, pb_=pb_: e.transpose(
                            pb_[:, j * 128:(j + 1) * 128],
                            hqsb[:, blk * 128:(blk + 1) * 128], idb[:]))
                    em.wait("S", sT, tv)
                    cpvs[oh % 2] = em.op("S", sS, lambda e, oh=oh, pb_=pb_: e.activation(
                        hqTst[:, oh * 1024:(oh + 1) * 1024], pb_[:], AF.Identity))
                em.wait("T", sS, cpvs[0])
                em.wait("T", sS, cpvs[1])
                em.wait("T", sS, pair_s[dpair])
                em.wait("T", sV, pair_v[dpair])
                pb = 2 * dpair
                mv = None
                for it in range(IT):
                    em.op("T", sT, lambda e, it=it, pb=pb: e.matmul(
                        bank(pb), hqTst[:, it * 128:(it + 1) * 128],
                        wdT[:, it * 1024: it * 1024 + 512], start=(it == 0), stop=(it == IT - 1)))
                    mv = em.op("T", sT, lambda e, it=it, pb=pb: e.matmul(
                        bank(pb + 1), hqTst[:, it * 128:(it + 1) * 128],
                        wdT[:, it * 1024 + 512: it * 1024 + 1024], start=(it == 0), stop=(it == IT - 1)))
                em.wait("V", sT, mv)
                em.wait("S", sV, xld_rd[s5])
                em.wait("S", SEMS["outst"], SEMS["outst"].n)
                xv = em.dma("S", sxg[s5], xfb[s5][:], xpD.ap()[xrow + sgq * 128: xrow + (sgq + 1) * 128, :])
                em.op("V", sV, lambda e, pb=pb: e.bn_stats(bnb[:, 4:10], bank(pb)))
                em.op("V", sV, lambda e, pb=pb: e.bn_stats(bnb[:, 10:16], bank(pb + 1)))
                em.wait("V", sV, sV.n)
                em.op("V", sV, lambda e: e.bn_aggr(st2[:], bnb[:, 4:16]))
                em.wait("V", sV, sV.n)
                em.op("V", sV, lambda e, qq=qq: e.tensor_tensor(veps[:], sdA[:, qq:qq + 1], sdA[:, qq:qq + 1], AL.mult))
                vw = sV.n
                em.wait("S", sV, vw)
                sw = em.op("S", sS, lambda e: e.activation(sq[:], st2[:, 1:2], AF.Sqrt, bias=wscB[:, 9:10], scale=veps[:]))
                em.wait("V", sS, sw)
                em.op("V", sV, lambda e: e.reciprocal(rstd[:], sq[:]))
                em.wait("V", sV, sV.n)
                em.op("V", sV, lambda e, qq=qq: e.tensor_tensor(rmr[:], sdA[:, qq:qq + 1], rstd[:], AL.mult))
                em.wait("V", sV, sV.n)
                em.op("V", sV, lambda e: e.scalar_tensor_tensor(nmb[:], st2[:, 0:1], -1.0, rmr[:], AL.mult, AL.mult))
                cw = sV.n
                em.wait("S", sV, cw)
                em.wait("S", sV, dn_rd[0])
                yv = em.op("S", sS, lambda e, pb=pb: e.activation(
                    dnT[:], psF[:, pb * 512: pb * 512 + 1024], AF.Identity, bias=nmb[:], scale=rmr[:]))
                pair_s[dpair] = yv
                em.wait("V", sS, yv)
                em.wait("V", sxg[s5], SEMS[f"p5x{s5}"].n)
                ov = em.op("V", sV, lambda e, s5=s5: e.tensor_add(xfb[s5][:], xfb[s5][:], dnT[:]))
                dn_rd[0] = ov
                xld_rd[s5] = ov
                pair_v[dpair] = ov
                em.wait("Y", sV, ov)
                em.dma("Y", sout, out.ap()[xrow + sgq * 128: xrow + (sgq + 1) * 128, :], xfb[s5][:])

            dn_rd = [0]
            xld_rd = [sV.n, sV.n]
            for qi in range(LT):
                s = qi % 3
                dprev = (s + 2) % 3
                ogp = [(dprev, s), ((s + 1) % 3, dprev), (s, (s + 1) % 3), ((s + 1) % 3, s)]
                for og in range(4):
                    if og == 2 and pend is not None:
                        fused_tail(pend)
                        pend = None
                    gp, up = ogp[og]
                    mv = None
                    if og == 3:
                        em.wait("T", sS, pair_s[up])
                        em.wait("T", sV, pair_v[up])
                        for it in range(8):
                            fap = bass.AP(featA, it * 2048 + qi * 128, [[16384, 128], [1, 128]])
                            for oc in range(2):
                                col = og * 1024 + oc * 512
                                em.op("T", sT, lambda e, a=fap, it=it, oc=oc, col=col, up=up: e.matmul(
                                    bank(2 * up + oc), a, wuT8[:, it * 4096 + col: it * 4096 + col + 512],
                                    start=(it == 0), stop=(it == 7)))
                        em.wait("T", sS, pair_s[gp])
                        em.wait("T", sV, pair_v[gp])
                        for it in range(8):
                            fap = bass.AP(featA, it * 2048 + qi * 128, [[16384, 128], [1, 128]])
                            for oc in range(2):
                                col = og * 1024 + oc * 512
                                mv = em.op("T", sT, lambda e, a=fap, it=it, oc=oc, col=col, gp=gp: e.matmul(
                                    bank(2 * gp + oc), a, wgT[:, it * 4096 + col: it * 4096 + col + 512],
                                    start=(it == 0), stop=(it == 7)))
                    else:
                        for it in range(8):
                            fap = bass.AP(featA, it * 2048 + qi * 128, [[16384, 128], [1, 128]])
                            for oc in range(2):
                                col = og * 1024 + oc * 512
                                if it == 0 and oc == 0:
                                    em.wait("T", sS, max(pair_s[gp], pair_s[up]))
                                    em.wait("T", sV, max(pair_v[gp], pair_v[up]))
                                em.op("T", sT, lambda e, a=fap, it=it, oc=oc, col=col, gp=gp: e.matmul(
                                    bank(2 * gp + oc), a, wgT[:, it * 4096 + col: it * 4096 + col + 512],
                                    start=(it == 0), stop=(it == 7)))
                                mv = em.op("T", sT, lambda e, a=fap, it=it, oc=oc, col=col, up=up: e.matmul(
                                    bank(2 * up + oc), a, wuT8[:, it * 4096 + col: it * 4096 + col + 512],
                                    start=(it == 0), stop=(it == 7)))
                    em.wait("S", sT, mv)
                    for oc in range(2):
                        col = og * 1024 + oc * 512
                        em.wait("S", sV, gb_rd[oc])
                        gev = em.op("S", sS, lambda e, oc=oc, qi=qi, gp=gp: e.activation(
                            gbuf[:, oc * 512:(oc + 1) * 512], bank(2 * gp + oc), AF.Silu, scale=sgA[:, qi:qi + 1]))
                        pair_s[gp] = gev
                        em.wait("V", sS, gev)
                        hv = em.op("V", sV, lambda e, oc=oc, qi=qi, col=col, up=up: e.scalar_tensor_tensor(
                            hbuf[:, col:col + 512], bank(2 * up + oc), sgA[:, qi:qi + 1],
                            gbuf[:, oc * 512:(oc + 1) * 512], AL.mult, AL.mult))
                        pair_v[up] = hv
                        gb_rd[oc] = hv
                    em.wait("V", sV, sV.n)
                    em.op("V", sV, lambda e, og=og: e.tensor_reduce(
                        bnb[:, og:og + 1], hbuf[:, og * 1024:(og + 1) * 1024], AX.X, AL.max,
                        apply_absolute_value=True))
                em.wait("V", sV, sV.n)
                em.op("V", sV, lambda e: e.tensor_reduce(rmx[:], bnb[:, 0:4], AX.X, AL.max))
                em.wait("V", sV, sV.n)
                em.op("V", sV, lambda e: e.tensor_scalar(rmx[:], rmx[:], 1e-5, None, AL.max))
                em.wait("V", sV, sV.n)
                em.op("V", sV, lambda e: e.reciprocal(rmr[:], rmx[:]))
                em.wait("V", sV, sV.n)
                em.op("V", sV, lambda e, qi=qi: e.scalar_tensor_tensor(
                    sdA[:, qi:qi + 1], rmx[:], 1.0 / 127.0, wscB[:, 2:3], AL.mult, AL.mult))
                em.wait("V", sV, sV.n)
                r127 = em.op("V", sV, lambda e: e.tensor_scalar(
                    bnb[:, 16:17], rmr[:], 127.0, None, AL.mult))
                em.wait("S", sV, r127)
                t0_free = 0
                hq_quarters = []
                for oh in range(4):
                    cs = slice(oh * 1024, (oh + 1) * 1024)
                    uqv = em.op("S", sS, lambda e, cs=cs: e.activation(
                        hqsb[:, cs], hbuf[:, cs], AF.Identity, scale=bnb[:, 16:17]))
                    em.wait("S", sS, uqv)
                    c1 = em.op("S", sS, lambda e, cs=cs: e.activation(i8h[:, cs], hqsb[:, cs], AF.Identity))
                    em.wait("S", sS, c1)
                    em.wait("S", sV, t0_free)
                    c2 = em.op("S", sS, lambda e, cs=cs: e.activation(t0f[:], i8h[:, cs], AF.Identity))
                    em.wait("V", sS, c2)
                    em.op("V", sV, lambda e, cs=cs: e.tensor_sub(hqsb[:, cs], hqsb[:, cs], t0f[:]))
                    em.wait("V", sV, sV.n)
                    em.op("V", sV, lambda e, cs=cs: e.scalar_tensor_tensor(t0f[:], hqsb[:, cs], 0.5, t0f[:], AL.is_ge, AL.add))
                    em.wait("V", sV, sV.n)
                    em.op("V", sV, lambda e, cs=cs: e.tensor_scalar(hqsb[:, cs], hqsb[:, cs], -0.5, None, AL.is_le))
                    em.wait("V", sV, sV.n)
                    hqv = em.op("V", sV, lambda e, cs=cs: e.tensor_sub(hqsb[:, cs], t0f[:], hqsb[:, cs]))
                    t0_free = hqv
                    hq_quarters.append(hqv)
                if hqD is not None:
                    em.wait("Y", sV, sV.n)
                    dhq = em.dma("Y", sem("dbghq"), hqD.ap()[xrow + sigma(qi) * 128: xrow + (sigma(qi) + 1) * 128, :], hqsb[:])
                    em.wait("V", SEMS["dbghq"], dhq)
                pend = (qi, hq_quarters, sigma(qi), qi % 3, qi % 2)
                if qi == LT - 1:
                    fused_tail(pend)
                    pend = None

    if "p4" in phases:
        em.wait("Y", SEMS["outst"], SEMS["outst"].n)

    with nc.Block() as block:
        @block.tensor
        def _(e):
            for f in em.q["T"]:
                f(e)

        @block.vector
        def _(e):
            for f in em.q["V"]:
                f(e)

        @block.scalar
        def _(e):
            for f in em.q["S"]:
                f(e)

        @block.gpsimd
        def _(e):
            for f in em.q["G"]:
                f(e)

        @block.sync
        def _(e):
            for f in em.q["Y"]:
                f(e)
    stack.close()
    return nc


from concourse.bass_utils import run_bass_kernel_spmd
import ml_dtypes


def _sel_const():
    s = np.zeros((16, 1024), dtype=np.float32)
    for p in range(8):
        for m in range(128):
            s[2 * p + (m >= 64), p * 128 + m] = 1.0
    return s.astype(ml_dtypes.bfloat16)


def kernel(L0_attn_xin, L0_attn_W_qkv, L0_attn_W_o, L0_ffn_w_gate, L0_ffn_w_up, L0_ffn_w_down):
    x = np.ascontiguousarray(np.asarray(L0_attn_xin, dtype=np.float32))
    consts = {
        "wqkv": np.ascontiguousarray(np.asarray(L0_attn_W_qkv, dtype=np.float32)),
        "wo": np.ascontiguousarray(np.asarray(L0_attn_W_o, dtype=np.float32)),
        "wg": np.ascontiguousarray(np.asarray(L0_ffn_w_gate, dtype=np.float32)),
        "wu": np.ascontiguousarray(np.asarray(L0_ffn_w_up, dtype=np.float32)),
        "wd": np.ascontiguousarray(np.asarray(L0_ffn_w_down, dtype=np.float32)),
        "identc": np.eye(128, dtype=ml_dtypes.bfloat16),
        "onesc": np.ones((128, 128), dtype=ml_dtypes.bfloat16),
        "selc": _sel_const(),
    }
    nc = bass.Bass("TRN2", target_bir_lowering=False)
    build(nc)
    in_maps = []
    for c in range(8):
        im = dict(consts)
        im["xin"] = np.ascontiguousarray(x[2 * c: 2 * c + 2].reshape(2 * L, D))
        in_maps.append(im)
    res = run_bass_kernel_spmd(nc, in_maps, core_ids=list(range(8)))
    out = np.empty((16, L, D), dtype=np.float32)
    for c in range(8):
        out[2 * c: 2 * c + 2] = res.results[c]["out"].reshape(2, L, D)
    return out


# version 31
# speedup vs baseline: 1.0622x; 1.0356x over previous
import contextlib
import numpy as np
import concourse.bass as bass
import concourse.mybir as mybir

F32 = mybir.dt.float32
FP8 = mybir.dt.float8e4
BF16 = mybir.dt.bfloat16
I8 = mybir.dt.int8
AL = mybir.AluOpType
AF = mybir.ActivationFunctionType
AX = mybir.AxisListType

B_PER_CORE = 2
L = 2048
D = 1024
N3 = 3072
INNER = 4096
LT, CT, IT = 16, 8, 32
EPS = 1e-5


def sigma(qi):
    return qi


class Sem:
    __slots__ = ("h", "name", "n")

    def __init__(self, h, name):
        self.h, self.name, self.n = h, name, 0


class Em:
    def __init__(self):
        self.q = {k: [] for k in ("T", "V", "S", "G", "Y")}

    def add(self, eng, fn):
        self.q[eng].append(fn)

    def op(self, eng, sem, fn):
        sem.n += 1
        v = sem.n
        self.q[eng].append(lambda e, f=fn, s=sem: f(e).then_inc(s.h, 1))
        return v

    def dma(self, eng, sem, out_ap, in_ap):
        sem.n += 16
        v = sem.n
        self.q[eng].append(lambda e, o=out_ap, i=in_ap, s=sem: e.dma_start(o, i).then_inc(s.h, 16))
        return v

    def dma_t(self, eng, sem, out_ap, in_ap):
        sem.n += 16
        v = sem.n
        self.q[eng].append(lambda e, o=out_ap, i=in_ap, s=sem: e.dma_start_transpose(o, i).then_inc(s.h, 16))
        return v

    def wait(self, eng, sem, val):
        if val:
            self.q[eng].append(lambda e, s=sem, v=val: e.wait_ge(s.h, v))


def build(nc, dbg=(), phases=("prep", "p1", "p2", "p3", "p4", "p5")):
    dbg = set(dbg)
    xin = nc.dram_tensor("xin", [B_PER_CORE * L, D], F32, kind="ExternalInput")
    Wqkv = nc.dram_tensor("wqkv", [D, N3], F32, kind="ExternalInput")
    Wo = nc.dram_tensor("wo", [D, D], F32, kind="ExternalInput")
    wg = nc.dram_tensor("wg", [INNER, D], F32, kind="ExternalInput")
    wu = nc.dram_tensor("wu", [INNER, D], F32, kind="ExternalInput")
    wd = nc.dram_tensor("wd", [D, INNER], F32, kind="ExternalInput")
    identC = nc.dram_tensor("identc", [128, 128], BF16, kind="ExternalInput")
    onesC = nc.dram_tensor("onesc", [128, 128], BF16, kind="ExternalInput")
    selC = nc.dram_tensor("selc", [16, 1024], BF16, kind="ExternalInput")
    out = nc.dram_tensor("out", [B_PER_CORE * L, D], F32, kind="ExternalOutput")

    def scr(name, shape, dt):
        return nc.dram_tensor(name, shape, dt,
                              kind="ExternalOutput" if name in dbg else "Internal")

    qkvD = scr("qkvd", [B_PER_CORE * L * 48, 64], BF16)
    wgTD = scr("wgtd", [D, INNER], FP8)
    wuTD = scr("wutd", [D, INNER], FP8)
    wdTD = scr("wdtd", [INNER, D], FP8)
    xpD = scr("xpd", [B_PER_CORE * L, D], F32)
    zrowD = scr("zrowd", [16, L], BF16)
    hqTD = scr("hqtd", [B_PER_CORE * LT * 128, INNER], BF16)
    wscD = scr("wscd", [1, 12], F32)
    azD = scr("azd", [B_PER_CORE * 128, 16384], BF16) if "azd" in dbg else None
    hqD = scr("hqd", [B_PER_CORE * L, INNER], BF16) if "hqd" in dbg else None
    xqD = scr("xqd", [B_PER_CORE * L, D], BF16) if "xqd" in dbg else None

    A = lambda n, s, d, o: nc.alloc_sbuf_tensor_at(n, s, d, offset=o)
    MB = 18432
    W1, W2, W3, W4 = MB, MB + 65536, MB + 131072, MB + 163840
    wtile = [A(f"wtile{i}", [128, 4096], F32, W2 + 32768 + i * 16384) for i in range(2)]
    wq8f = A("wq8f", [128, 4096], F32, W4 + 11264)
    wqb = A("wqb", [128, 4096], BF16, W4 + 27648)
    wqb2 = A("wqb2", [128, 4096], BF16, W3)
    wts = [A(f"wts{i}", [128, 1024], FP8, W4 + 35840 + i * 1024) for i in range(2)]
    pbn = A("pbn", [128, 32], F32, W4 + 37888)
    thr2 = A("thr2", [1, 6], F32, W4 + 44096)
    WqB = A("wqb_full", [128, 8 * 3072], BF16, W1)
    qsbS = [A(f"qsb{i}", [128, 1536], BF16, W1 + 49152 + i * 3072) for i in range(2)]
    qkS = [A(f"qks{i}", [128, 16 * 64], BF16, W1 + i * 2048) for i in range(4)]
    vS = [A(f"vs{i}", [128, 16 * 65], BF16, W1 + 8192 + i * 2112) for i in range(2)]
    qT1 = [A(f"qt1_{i}", [128, L], BF16, W1 + 12544 + i * 4096) for i in range(2)]
    kT1 = [A(f"kt1_{i}", [128, L], BF16, W1 + 20736 + i * 4096) for i in range(2)]
    phS = [A(f"phs{i}", [128, 1024], BF16, W1 + 28928 + i * 2048) for i in range(3)]
    azst = [A(f"azst{i}", [65, L], BF16, W1 + 35072 + i * 4096) for i in range(2)]
    zall = A("zall", [16, L], BF16, W1 + 43264)
    zala = A("zala", [128, 256], BF16, W1 + 43264)
    zif2 = A("zif2", [128, 256], F32, W1 + 47360)
    zi2 = A("zi2", [128, 256], BF16, W1 + 55552)
    zinvF = A("zinvf", [16, L], F32, W1 + 47360)
    zinv = A("zinv", [16, L], BF16, W1 + 55552)
    wgT = A("wgt", [128, 8 * 4096], FP8, W1)
    wuT8 = A("wut8", [128, 8 * 4096], FP8, W1 + 32768)
    wdT = A("wdt", [128, 32 * 1024], FP8, W2)
    featB = A("featb", [128, 8 * 2048], BF16, W2)
    x2buf = A("x2buf", [128, 1024], F32, W2 + 32768)
    ybuf = A("ybuf", [128, 1024], F32, W2 + 36864)
    xpbuf = [A(f"xpbuf{i}", [128, 1024], F32, W2 + 40960 + i * 4096) for i in range(2)]
    WoB = A("wob", [128, 8 * 1024], BF16, W2 + 49152)
    i8buf = A("i8buf", [128, 1024], I8, W4 + 13312)
    xqsb = A("xqsb", [128, 1024], BF16, W4 + 11264)
    featA = A("feata", [128, 8 * 2048], BF16, W3)
    xb = [A(f"xb{i}", [128, 1024], F32, W4 + i * 4096) for i in range(2)]
    gbuf = A("gbuf", [128, 1024], F32, W4)
    bnb = A("bnb", [128, 48], F32, W4 + 8192)
    st2 = A("st2", [128, 2], F32, W4 + 8256)
    veps = A("veps", [128, 1], F32, W4 + 8288)
    sq = A("sq", [128, 1], F32, W4 + 8320)
    rstd = A("rstd", [128, 1], F32, W4 + 8352)
    nmb = A("nmb", [128, 1], F32, W4 + 8384)
    rmx = A("rmx", [128, 1], F32, W4 + 8416)
    rmr = A("rmr", [128, 1], F32, W4 + 8448)
    accw = A("accw", [128, 4], F32, W4 + 8480)
    sgA = A("sga", [128, 16], F32, W4 + 8512)
    sdA = A("sda", [128, 16], F32, W4 + 8576)
    wsc = A("wsc", [1, 4], F32, W4 + 8640)
    red1 = A("red1", [1, 1], F32, W4 + 8672)
    wscB = A("wscb", [128, 12], F32, W4 + 44032)
    idb = A("idb", [128, 128], BF16, W4 + 8704)
    onb = A("onb", [128, 128], BF16, W4 + 8960)
    selB = A("selb", [16, 1024], BF16, W4 + 9216)
    i8h = A("i8h", [128, 4096], I8, W4 + 4096)
    hbuf = A("hbuf", [128, 4096], F32, W4 + 11264)
    hqsb = A("hqsb", [128, 4096], BF16, W4 + 27648)
    hqTst = A("hqtst", [128, 4096], BF16, W4 + 35840)
    t0f = A("t0f", [128, 1024], BF16, W4 + 44128)
    wq_f32a = A("wqf32a", [128, 3072], F32, W4 + 14336)
    wq_f32b = A("wqf32b", [128, 3072], F32, W4 + 27648)
    hq2 = [A(f"hq2_{i}", [128, 4096], BF16, W4 + 11264 + i * 8192) for i in range(2)]
    dnT = A("dnt", [128, 1024], F32, W2 + 32768)
    xfb = [A(f"xfb{i}", [128, 1024], F32, W2 + 36864 + i * 4096) for i in range(2)]

    em = Em()
    stack = contextlib.ExitStack()
    SEMS = {}

    def sem(name):
        if name not in SEMS:
            SEMS[name] = Sem(stack.enter_context(nc.semaphore(name)), name)
        return SEMS[name]

    sT, sV, sS, sG = sem("c_t"), sem("c_v"), sem("c_s"), sem("c_g")
    psF = stack.enter_context(nc.psum_tensor("psf", [128, 3072], F32))
    psB = stack.enter_context(nc.psum_tensor("psb", [128, 1024], BF16))
    psB2 = stack.enter_context(nc.psum_tensor("psb2", [128, 1024], BF16))
    bank = lambda i: psF[:, i * 512:(i + 1) * 512]

    def ln_consts(src_aps):
        em.wait("V", sV, sV.n)
        for c, ap in enumerate(src_aps):
            em.op("V", sV, lambda e, c=c, a=ap: e.bn_stats(bnb[:, c * 6:(c + 1) * 6], a))
        em.wait("V", sV, sV.n)
        em.op("V", sV, lambda e, n=len(src_aps): e.bn_aggr(st2[:], bnb[:, 0:6 * n]))
        vw = sV.n
        em.wait("S", sV, vw)
        sw = em.op("S", sS, lambda e: e.activation(sq[:], st2[:, 1:2], AF.Sqrt, bias=wscB[:, 9:10]))
        em.wait("V", sS, sw)
        em.op("V", sV, lambda e: e.reciprocal(rstd[:], sq[:]))
        em.wait("V", sV, sV.n)
        em.op("V", sV, lambda e: e.scalar_tensor_tensor(nmb[:], st2[:, 0:1], -1.0, rstd[:], AL.mult, AL.mult))
        cw = sV.n
        return cw

    idv = onv = None
    prep_done_y = 0

    PREP = {"wscB_ready": 0, "done": (0, 0)}
    def prep_gen():
        gld = [sem("pw0"), sem("pw1")]
        pst = [sem("pwst0"), sem("pwst1")]
        sbc = sem("sbc")
        WSPEC = ((wg, 32, 1024, wgTD, INNER), (wu, 32, 1024, wuTD, INNER),
                 (wd, 8, 4096, wdTD, D))
        slot_rd = [0, 0]
        stage_rd = [0, 0]
        tiles1 = [(W, t, tw, n_t, wi) for wi, (W, n_t, tw, dst, dcols) in enumerate(WSPEC)
                  for t in range(n_t)]
        lv1 = {}

        def issue1(k):
            Wk, tk, twk, _, _ = tiles1[k]
            sk = k % 2
            em.wait("Y", sV, slot_rd[sk])
            lv1[k] = em.dma("Y", gld[sk], wtile[sk][:, 0:twk], Wk.ap()[tk * 128:(tk + 1) * 128, :])
        issue1(0)
        for k, (W_, t_, tw_, nt_, wi_) in enumerate(tiles1):
            if k + 1 < len(tiles1):
                issue1(k + 1)
            s = k % 2
            em.wait("V", gld[s], lv1[k])
            if t_ == 0:
                em.wait("V", sV, sV.n)
            em.op("V", sV, lambda e, s=s, w=tw_, t=t_: e.tensor_reduce(
                pbn[:, t % 32: t % 32 + 1], wtile[s][:, 0:w], AX.X, AL.add,
                apply_absolute_value=True))
            slot_rd[s] = sV.n
            if t_ == nt_ - 1:
                em.wait("V", sV, sV.n)
                em.op("V", sV, lambda e, wi=wi_, n_t=nt_: e.tensor_reduce(
                    accw[:, wi:wi + 1], pbn[:, 0:n_t] if n_t <= 32 else pbn[:, 0:32],
                    AX.X, AL.add))
            yield
        em.wait("G", sV, sV.n)
        gv = em.op("G", sG, lambda e: e.tensor_reduce(red1[0:1, 0:1], accw[:, 0:1], AX.C, AL.add))
        gv = em.op("G", sG, lambda e: e.tensor_reduce(wsc[0:1, 3:4], accw[:, 1:2], AX.C, AL.add))
        gv = em.op("G", sG, lambda e: e.tensor_reduce(thr2[0:1, 0:1], accw[:, 2:3], AX.C, AL.add))
        em.wait("V", sG, gv)
        for wi, (W, n_t, tw, dst, dcols) in enumerate(WSPEC):
            NN = float(n_t * 128 * tw)
            srcap = (red1[0:1, 0:1], wsc[0:1, 3:4], thr2[0:1, 0:1])[wi]
            em.op("V", sV, lambda e, wi=wi, NN=NN, a=srcap: e.tensor_scalar(
                wsc[:, wi:wi + 1], a, 1.0 / NN, 1e-5, AL.mult, AL.max))
        em.wait("V", sV, sV.n)
        for wi in range(3):
            em.op("V", sV, lambda e, wi=wi: e.tensor_scalar(thr2[:, 2 * wi:2 * wi + 1], wsc[:, wi:wi + 1], 0.5, None, AL.mult))
            em.op("V", sV, lambda e, wi=wi: e.tensor_scalar(thr2[:, 2 * wi + 1:2 * wi + 2], wsc[:, wi:wi + 1], -0.5, None, AL.mult))
        tv_ = sV.n
        em.wait("G", sV, tv_)
        bv = em.dma("G", sbc, wscD.ap()[0:1, 3:9], thr2[:])
        bv2 = em.dma("G", sbc, wscD.ap()[0:1, 0:3], wsc[:, 0:3])
        em.wait("G", sbc, bv2)
        bv = em.dma("G", sbc, wscB[:, 3:9], bass.AP(wscD, 3, [[1, 1], [0, 128], [1, 6]]))
        bv2 = em.dma("G", sbc, wscB[:, 0:3], bass.AP(wscD, 0, [[1, 1], [0, 128], [1, 3]]))
        PREP["wscB_ready"] = bv2
        thr_ready = bv2
        yield
        em.wait("V", sT, sT.n)
        wbufs = [wqb, wqb2]
        wqb_rd = [0, 0]
        pend2 = [None]

        def flush2():
            t_, tw_, dst_, dcols_, cpv_, wb_, tg_ = pend2[0]
            em.wait("T", sV, cpv_)
            nblk = tw_ // 128
            tvl = None
            for b0 in range(0, nblk, 8):
                nb = min(8, nblk - b0)
                ss = (b0 // 8) % 2
                tv = None
                for j in range(nb):
                    tv = em.op("T", sT, lambda e, j=j, b0=b0, wb=wb_: e.transpose(
                        psB[:, j * 128:(j + 1) * 128], wb[:, (b0 + j) * 128:(b0 + j + 1) * 128], idb[:]))
                tvl = tv
                em.wait("V", sT, tv)
                em.wait("V", pst[ss], stage_rd[ss])
                scv = em.op("V", sV, lambda e, nb=nb, ss=ss: e.tensor_copy(
                    wts[ss][:, 0:nb * 128], psB[:, 0:nb * 128]))
                em.wait("T", sV, scv)
                em.wait("Y", sV, scv)
                dst_ap = bass.AP(dst_, (b0 * 128) * dcols_ + t_ * 128,
                                 [[dcols_, 128], [128 * dcols_, nb], [1, 128]])
                src_ap = bass.AP(wts[ss], 0, [[1024, 128], [128, nb], [1, 128]])
                stage_rd[ss] = em.dma("Y", pst[ss], dst_ap, src_ap)
            wqb_rd[tg_ % 2] = tvl
            pend2[0] = None
        tiles2 = [(W, t, tw, dst, dcols, wi) for wi, (W, n_t, tw, dst, dcols) in enumerate(WSPEC)
                  for t in range(n_t)]
        lv2 = {}

        def issue2(k):
            Wk, tk, twk, _, _, _ = tiles2[k]
            sk = k % 2
            em.wait("Y", sV, slot_rd[sk])
            lv2[k] = em.dma("Y", gld[sk], wtile[sk][:, 0:twk], Wk.ap()[tk * 128:(tk + 1) * 128, :])
        issue2(0)
        for k, (W_, t_, tw_, dst_, dcols_, wi_) in enumerate(tiles2):
            if k + 1 < len(tiles2):
                issue2(k + 1)
            s = k % 2
            em.wait("V", gld[s], lv2[k])
            if k == 0:
                em.wait("V", SEMS["sbc"], thr_ready)
            em.wait("V", sV, sV.n)
            em.op("V", sV, lambda e, s=s, w=tw_, wi=wi_: e.tensor_scalar(
                wq8f[:, 0:w], wtile[s][:, 0:w], wscB[:, 4 + 2 * wi:5 + 2 * wi], None, AL.is_le))
            em.wait("V", sV, sV.n)
            em.wait("V", sT, wqb_rd[k % 2])
            cpv = em.op("V", sV, lambda e, s=s, w=tw_, wi=wi_, wb=wbufs[k % 2]: e.scalar_tensor_tensor(
                wb[:, 0:w], wtile[s][:, 0:w], wscB[:, 3 + 2 * wi:4 + 2 * wi],
                wq8f[:, 0:w], AL.is_ge, AL.subtract))
            slot_rd[s] = cpv
            if pend2[0] is not None:
                flush2()
            pend2[0] = (t_, tw_, dst_, dcols_, cpv, wbufs[k % 2], k)
            yield
        flush2()
        PREP["done"] = (SEMS["pwst0"].n, SEMS["pwst1"].n)

    _pgen = [None]

    def pump_prep(n):
        if "prep" not in phases:
            return
        if _pgen[0] is None:
            _pgen[0] = prep_gen()
        for _ in range(n):
            try:
                next(_pgen[0])
            except StopIteration:
                break

    sx = [sem("sx0"), sem("sx1")]
    sxp = [sem("sxp0"), sem("sxp1")]
    for b in range(B_PER_CORE):
        xrow = b * L
        if "p1" in phases:
            swq2 = [sem("p1w0"), sem("p1w1")]
            sqk = [sem("p1q0"), sem("p1q1")]
            if idv is None:
                gid = sem("gid")
                epsv = em.op("V", sV, lambda e: e.memset(wscB[:, 9:10], EPS))
                em.wait("S", sV, epsv)
                em.wait("V", sV, epsv)
                idv = em.dma("G", gid, idb[:], identC.ap())
                onv = em.dma("G", gid, onb[:], onesC.ap())
                onv = em.dma("G", gid, selB[:], selC.ap())
                em.wait("T", gid, onv)
            em.wait("G", sV, sV.n)
            em.wait("G", sT, sT.n)
            em.wait("T", sV, sV.n)
            em.wait("T", sS, sS.n)
            em.wait("S", sV, sV.n)
            if "outst" in SEMS:
                em.wait("S", SEMS["outst"], SEMS["outst"].n)
            wqstg = [wq_f32a, wq_f32b]
            wq_cast = [sV.n, sV.n]
            wv = 0
            for ctw in range(CT):
                sw_ = ctw % 2
                em.wait("Y", sV, wq_cast[sw_])
                lvw = em.dma("Y", swq2[sw_], wqstg[sw_][:],
                             Wqkv.ap()[ctw * 128:(ctw + 1) * 128, :])
                em.wait("V", swq2[sw_], lvw)
                wq_cast[sw_] = em.op("V", sV, lambda e, ctw=ctw, sw_=sw_: e.tensor_copy(
                    WqB[:, ctw * 3072:(ctw + 1) * 3072], wqstg[sw_][:]))
            wv = max(wq_cast)
            x1T_ready = [0] * LT
            xb_rd = [sS.n, sS.n]
            qkv_ev = [0, 0]
            qsb_rd = [0, 0]
            qkv_store_tile = []
            wq_started = False
            for lt in range(LT):
                s = lt % 2
                em.wait("S", sS, xb_rd[s])
                lv = em.dma("S", sx[s], xb[s][:], xin.ap()[xrow + lt * 128: xrow + (lt + 1) * 128, :])
                em.wait("V", sx[s], lv)
                cw = ln_consts([xb[s][:, 0:512], xb[s][:, 512:1024]])
                em.wait("S", sV, cw)
                av = em.op("S", sS, lambda e, s=s: e.activation(
                    xqsb[:], xb[s][:], AF.Identity, bias=nmb[:], scale=rstd[:]))
                xb_rd[s] = av
                em.wait("T", sS, av)
                tv = None
                for ct in range(CT):
                    tv = em.op("T", sT, lambda e, ct=ct: e.transpose(
                        psB[:, ct * 128:(ct + 1) * 128], xqsb[:, ct * 128:(ct + 1) * 128], idb[:]))
                em.wait("V", sT, tv)
                cpv = em.op("V", sV, lambda e, lt=lt: e.tensor_copy(
                    bass.AP(featA, lt * 128, [[16384, 128], [2048, 8], [1, 128]]),
                    bass.AP(psB, 0, [[1024, 128], [128, 8], [1, 128]])))
                x1T_ready[lt] = cpv
                em.wait("T", sV, cpv)
                if not wq_started:
                    em.wait("T", sV, wv)
                    wq_started = True
                for hf in range(2):
                    em.wait("T", sV, qkv_ev[hf])
                    mv = None
                    for ct in range(CT):
                        fap = bass.AP(featA, ct * 2048 + lt * 128, [[16384, 128], [1, 128]])
                        for nc3 in range(3):
                            nc6 = hf * 3 + nc3
                            mv = em.op("T", sT, lambda e, a=fap, ct=ct, nc6=nc6, bk=hf * 3 + nc3: e.matmul(
                                bank(bk), a, WqB[:, ct * 3072 + nc6 * 512: ct * 3072 + nc6 * 512 + 512],
                                start=(ct == 0), stop=(ct == CT - 1)))
                    em.wait("V", sT, mv)
                    em.wait("V", sqk[hf], qsb_rd[hf])
                    ev = em.op("V", sV, lambda e, hf=hf: e.tensor_copy(
                        qsbS[hf][:], psF[:, hf * 1536: hf * 1536 + 1536]))
                    qkv_ev[hf] = ev
                    em.wait("Y", sV, ev)
                    dstap = bass.AP(qkvD, (xrow + lt * 128) * 48 * 64 + hf * 1536,
                                    [[3072, 128], [1, 1536]])
                    qsb_rd[hf] = em.dma("Y", sqk[hf], dstap, qsbS[hf][:])
                qkv_store_tile.append((SEMS["p1q0"].n, SEMS["p1q1"].n))
            p1q = (SEMS["p1q0"].n, SEMS["p1q1"].n)
        if "p2" in phases:
            sqh = [sem("p2q0"), sem("p2q1")]
            skh = [sem("p2k0"), sem("p2k1")]
            svh = [sem("p2v0"), sem("p2v1")]
            szc = sem("szc")
            zdma_last = szc.n
            em.wait("G", sT, sT.n)
            em.wait("V", sT, sT.n)
            em.wait("T", sV, sV.n)
            em.wait("T", sS, sS.n)
            for i in range(2):
                em.op("V", sV, lambda e, i=i: e.memset(
                    bass.AP(vS[i], 64, [[1040, 128], [65, 16], [1, 1]]), 1.0))
            vones = sV.n
            em.wait("G", sV, vones)
            em.wait("Y", sV, vones)
            em.wait("S", sV, vones)
            qk_rd = [0, 0]
            v_rd = [0, 0]
            dd_rd = [0, 0]
            az_ev = [0, 0]
            expv = [0, 0]
            pend_head = None
            av_done = 0
            zld = sem("zld")
            base = b * L * 48

            def head_ready(h):
                need = min(LT, (33 + h) * 2048 // (48 * 128) + 1)
                return qkv_store_tile[need - 1]
            for h in range(16):
                hs = h % 2
                rdy = head_ready(h)
                def apt(seg, dstbuf):
                    return (dstbuf[:], bass.AP(qkvD, (base + seg * L) * 64, [[128, 1024], [1, 128]]))
                em.wait("Y", SEMS["p1q0"], rdy[0])
                em.wait("Y", SEMS["p1q1"], rdy[1])
                em.wait("Y", sV, qk_rd[hs])
                o_, i_ = apt(h, qkS[hs])
                qv = em.dma_t("Y", sqh[hs], o_, i_)
                em.wait("Y", SEMS["p1q0"], rdy[0])
                em.wait("Y", SEMS["p1q1"], rdy[1])
                em.wait("Y", sV, qk_rd[hs])
                o_, i_ = apt(16 + h, qkS[2 + hs])
                kv = em.dma_t("Y", skh[hs], o_, i_)
                em.wait("G", SEMS["p1q0"], rdy[0])
                em.wait("G", SEMS["p1q1"], rdy[1])
                em.wait("G", sT, v_rd[hs])
                vv = em.dma("G", svh[hs],
                            bass.AP(vS[hs], 0, [[16 * 65, 128], [65, 16], [1, 64]]),
                            bass.AP(qkvD, (base + (32 + h) * L) * 64,
                                    [[64, 128], [128 * 64, 16], [1, 64]]))

                em.wait("V", sqh[hs], qv)
                em.wait("V", sT, dd_rd[hs])
                for hh in range(2):
                    em.op("V", sV, lambda e, hs=hs, hh=hh: e.tensor_copy(
                        bass.AP(qT1[hs], hh * 64 * L, [[L, 64], [2, 1024]]),
                        bass.AP(qkS[hs], 0, [[1024, 64], [1, 1024]])))
                    qtv = em.op("V", sV, lambda e, hs=hs, hh=hh: e.tensor_copy(
                        bass.AP(qT1[hs], hh * 64 * L + 1, [[L, 64], [2, 1024]]),
                        bass.AP(qkS[hs], 64 * 1024, [[1024, 64], [1, 1024]])))
                em.wait("V", skh[hs], kv)
                for hh in range(2):
                    em.op("V", sV, lambda e, hs=hs, hh=hh: e.tensor_copy(
                        bass.AP(kT1[hs], hh * 64 * L, [[L, 64], [2, 1024]]),
                        bass.AP(qkS[2 + hs], 0, [[1024, 64], [1, 1024]])))
                    ktv = em.op("V", sV, lambda e, hs=hs, hh=hh: e.tensor_copy(
                        bass.AP(kT1[hs], hh * 64 * L + 1, [[L, 64], [2, 1024]]),
                        bass.AP(qkS[2 + hs], 64 * 1024, [[1024, 64], [1, 1024]])))
                em.wait("T", sV, ktv)
                qk_rd[hs] = ktv
                if pend_head is not None:
                    ph, pazs = pend_head
                    em.wait("V", sT, av_done)
                    az_ev[1] = em.op("V", sV, lambda e, azs=pazs: e.tensor_copy(
                        azs[:, 3 * 512:4 * 512], psF[0:65, 5 * 512:6 * 512]))
                    pp_ = ph // 2
                    rlo_ = (ph % 2) * 64
                    em.wait("V", sV, sV.n)
                    em.wait("V", szc, zdma_last)
                    em.op("V", sV, lambda e, p=pp_, rlo=rlo_, azs=pazs: e.tensor_copy(
                        featB[rlo:rlo + 64, p * 2048:(p + 1) * 2048], azs[0:64, :]))
                    em.wait("G", sV, sV.n)
                    zdma_last = em.dma("G", szc, zrowD.ap()[ph:ph + 1, :], pazs[64:65, :])
                    pend_head = None
                em.wait("T", svh[hs], vv)
                if b == 0 and h == 0:
                    em.wait("T", sV, vones)
                azs = azst[h % 2]
                for s4 in range(4):
                    ab = 4 + (s4 % 2)
                    em.wait("T", sV, az_ev[s4 % 2])
                    for ktp in range(9):
                        pp = ktp % 2
                        if ktp < 8:
                            em.wait("T", sS, expv[pp])
                            for half in range(2):
                                kt = 2 * ktp + half
                                rr = slice(half * 64, half * 64 + 64)
                                smv = em.op("T", sT, lambda e, hs=hs, kt=kt, rr=rr, s4=s4, bk=2 * pp + half: e.matmul(
                                    bank(bk), kT1[hs][rr, kt * 128:(kt + 1) * 128],
                                    qT1[hs][rr, s4 * 512:(s4 + 1) * 512],
                                    start=True, stop=True, skip_group_check=True))
                            em.wait("S", sT, smv)
                            expv[pp] = em.op("S", sS, lambda e, pp=pp: e.activation(
                                phS[pp][:], psF[:, 2 * pp * 512: (2 * pp + 2) * 512], AF.Exp, scale=0.125))
                        if ktp >= 1:
                            qp = (ktp - 1) % 2
                            em.wait("T", sS, expv[qp])
                            for half in range(2):
                                kt = 2 * (ktp - 1) + half
                                em.op("T", sT, lambda e, hs=hs, kt=kt, ab=ab, qp=qp, half=half: e.matmul(
                                    psF[0:65, ab * 512:(ab + 1) * 512], vS[hs][:, kt * 65: kt * 65 + 65],
                                    phS[qp][:, half * 512:(half + 1) * 512],
                                    start=(kt == 0), stop=(kt == 15)))
                    v_rd[hs] = sT.n
                    dd_rd[hs] = sT.n
                    if s4 < 3:
                        em.wait("V", sT, sT.n)
                        col = s4 * 512
                        az_ev[s4 % 2] = em.op("V", sV, lambda e, ab=ab, col=col, azs=azs: e.tensor_copy(
                            azs[:, col:col + 512], psF[0:65, ab * 512:(ab + 1) * 512]))
                    else:
                        av_done = sT.n
                        pend_head = (h, azs)
                    if b == 0:
                        pump_prep(3)
            if b == 0:
                pump_prep(1000)
            if pend_head is not None:
                ph, pazs = pend_head
                em.wait("V", sT, av_done)
                em.op("V", sV, lambda e, azs=pazs: e.tensor_copy(
                    azs[:, 3 * 512:4 * 512], psF[0:65, 5 * 512:6 * 512]))
                em.wait("V", sV, sV.n)
                em.wait("V", szc, zdma_last)
                em.op("V", sV, lambda e, p=ph // 2, rlo=(ph % 2) * 64, azs=pazs: e.tensor_copy(
                    featB[rlo:rlo + 64, p * 2048:(p + 1) * 2048], azs[0:64, :]))
                em.wait("G", sV, sV.n)
                zdma_last = em.dma("G", szc, zrowD.ap()[ph:ph + 1, :], pazs[64:65, :])
                pend_head = None
            em.wait("G", szc, zdma_last)
            zlv = em.dma("G", zld, zala[:], bass.AP(zrowD, 0, [[256, 128], [1, 256]]))
            em.wait("V", zld, zlv)
            em.op("V", sV, lambda e: e.reciprocal(zif2[:], zala[:]))
            em.wait("V", sV, sV.n)
            zcv = em.op("V", sV, lambda e: e.tensor_copy(zi2[:], zif2[:]))
            em.wait("G", sV, zcv)
            zsv = em.dma("G", zld, bass.AP(zrowD, 0, [[256, 128], [1, 256]]), zi2[:])
            em.wait("G", zld, zsv)
            zlv2 = em.dma("G", zld, zinv[:], zrowD.ap())
            em.wait("V", zld, zlv2)
            em.wait("T", zld, zlv2)
            zb_rd = [0, 0]
            zidx = 0
            for p in range(8):
                for c in range(4):
                    bk = 4 + (zidx % 2)
                    em.wait("T", sV, zb_rd[zidx % 2])
                    m2 = em.op("T", sT, lambda e, p=p, c=c, bk=bk: e.matmul(
                        bank(bk), selB[0:16, p * 128:(p + 1) * 128],
                        zinv[0:16, c * 512:(c + 1) * 512], start=True, stop=True))
                    em.wait("V", sT, m2)
                    zb_rd[zidx % 2] = em.op("V", sV, lambda e, p=p, c=c, bk=bk: e.tensor_tensor(
                        featB[:, p * 2048 + c * 512: p * 2048 + (c + 1) * 512],
                        featB[:, p * 2048 + c * 512: p * 2048 + (c + 1) * 512],
                        bank(bk), AL.mult))
                    zidx += 1
            featB_done = sV.n
            if azD is not None:
                em.wait("Y", sV, featB_done)
                dv_ = em.dma("Y", sem("dbgaz"), azD.ap()[b * 128:(b + 1) * 128, :], featB[:])
                em.wait("V", SEMS["dbgaz"], dv_)
        if "p3" in phases:
            swo = sem("p3wo")
            em.wait("G", sV, sV.n)
            em.wait("G", sT, sT.n)
            if "prep" in phases:
                em.wait("G", SEMS["pwst0"], SEMS["pwst0"].n)
                em.wait("G", SEMS["pwst1"], SEMS["pwst1"].n)
            wov = em.dma("G", swo, bass.AP(WoB, 0, [[8192, 128], [1024, 8], [1, 1024]]),
                         bass.AP(Wo, 0, [[D, 128], [128 * D, 8], [1, 1024]]))
            em.wait("T", swo, wov)
            if "p4" in phases and "prep" in phases:
                swg = sem("p4wg")
                em.wait("G", SEMS["pwst0"], SEMS["pwst0"].n)
                em.wait("G", SEMS["pwst1"], SEMS["pwst1"].n)
                em.wait("G", sV, sV.n)
                em.wait("G", sT, sT.n)
                for nm in ("p2q0", "p2q1", "p2k0", "p2k1", "p2v0", "p2v1"):
                    if nm in SEMS:
                        em.wait("G", SEMS[nm], SEMS[nm].n)
                em.dma("G", swg, bass.AP(wgT, 0, [[32768, 128], [4096, 8], [1, INNER]]),
                       bass.AP(wgTD, 0, [[INNER, 128], [128 * INNER, 8], [1, INNER]]))
                wg_pref = em.dma("G", swg, bass.AP(wuT8, 0, [[32768, 128], [4096, 8], [1, INNER]]),
                                 bass.AP(wuTD, 0, [[INNER, 128], [128 * INNER, 8], [1, INNER]]))
            if "p2" in phases:
                em.wait("T", sV, featB_done)
            em.wait("T", sS, sS.n)
            if "outst" in SEMS:
                em.wait("V", SEMS["outst"], SEMS["outst"].n)
            prev_yv = [0, 0]
            xb_rd3 = [sV.n, sV.n]
            xp_wr = [SEMS["sxp0"].n, SEMS["sxp1"].n]
            x2_rd = 0
            t0_fr3 = 0
            pend_xq = None

            def flush_xq(p):
                if p is None:
                    return
                qq, qvv = p
                em.wait("T", sV, qvv)
                tv = None
                for ct in range(CT):
                    tv = em.op("T", sT, lambda e, ct=ct: e.transpose(
                        psB[:, ct * 128:(ct + 1) * 128], xqsb[:, ct * 128:(ct + 1) * 128], idb[:]))
                em.wait("V", sT, tv)
                cpv = em.op("V", sV, lambda e, qq=qq: e.tensor_copy(
                    bass.AP(featA, qq * 128, [[16384, 128], [2048, 8], [1, 128]]),
                    bass.AP(psB, 0, [[1024, 128], [128, 8], [1, 128]])))
                em.wait("T", sV, cpv)
            for qi in range(LT):
                sg = sigma(qi)
                s = qi % 2
                pb = s * 2
                em.wait("S", sV, xb_rd3[s])
                lvx = em.dma("S", sx[s], xb[s][:], xin.ap()[xrow + sg * 128: xrow + (sg + 1) * 128, :])
                em.wait("T", sS, prev_yv[s])
                mv = None
                for ft in range(8):
                    fap = featB[:, ft * 2048 + qi * 128: ft * 2048 + qi * 128 + 128]
                    em.op("T", sT, lambda e, a=fap, ft=ft, pb=pb: e.matmul(
                        bank(pb), a, WoB[:, ft * 1024: ft * 1024 + 512], start=(ft == 0), stop=(ft == 7)))
                    mv = em.op("T", sT, lambda e, a=fap, ft=ft, pb=pb: e.matmul(
                        bank(pb + 1), a, WoB[:, ft * 1024 + 512: ft * 1024 + 1024],
                        start=(ft == 0), stop=(ft == 7)))
                flush_xq(pend_xq)
                pend_xq = None
                em.wait("V", sT, mv)
                cw = ln_consts([bank(pb), bank(pb + 1)])
                em.wait("S", sV, cw)
                yv = em.op("S", sS, lambda e, pb=pb: e.activation(
                    ybuf[:], psF[:, pb * 512: pb * 512 + 1024], AF.Identity, bias=nmb[:], scale=rstd[:]))
                prev_yv[s] = yv
                em.wait("V", sS, yv)
                em.wait("V", sx[s], lvx)
                em.wait("V", sxp[s], xp_wr[s])
                xpv = em.op("V", sV, lambda e, s=s: e.tensor_add(xpbuf[s][:], xb[s][:], ybuf[:]))
                xb_rd3[s] = xpv
                em.wait("Y", sV, xpv)
                xp_wr[s] = em.dma("Y", sxp[s], xpD.ap()[xrow + sg * 128: xrow + (sg + 1) * 128, :],
                                  xpbuf[s][:])
                cw = ln_consts([xpbuf[s][:, 0:512], xpbuf[s][:, 512:1024]])
                em.wait("S", sV, cw)
                em.wait("S", sV, x2_rd)
                x2v = em.op("S", sS, lambda e, s=s: e.activation(
                    x2buf[:], xpbuf[s][:], AF.Identity, bias=nmb[:], scale=rstd[:]))
                em.wait("V", sS, x2v)
                em.op("V", sV, lambda e: e.tensor_reduce(rmx[:], x2buf[:], AX.X, AL.max, apply_absolute_value=True))
                em.wait("V", sV, sV.n)
                em.op("V", sV, lambda e: e.tensor_scalar(rmx[:], rmx[:], 1e-5, None, AL.max))
                em.wait("V", sV, sV.n)
                em.op("V", sV, lambda e: e.reciprocal(rmr[:], rmx[:]))
                em.wait("V", sV, sV.n)
                if "prep" in phases:
                    if qi == 0:
                        em.wait("V", SEMS["sbc"], PREP["wscB_ready"])
                    em.op("V", sV, lambda e, qi=qi: e.scalar_tensor_tensor(
                        sgA[:, qi:qi + 1], rmx[:], 1.0 / 127.0, wscB[:, 0:1], AL.mult, AL.mult))
                uq = em.op("V", sV, lambda e: e.tensor_scalar(xqsb[:], x2buf[:], rmr[:], 127.0, AL.mult, AL.mult))
                x2_rd = uq
                em.wait("V", sV, uq)
                c1 = em.op("V", sV, lambda e: e.tensor_copy(i8buf[:], xqsb[:]))
                em.wait("V", sV, c1)
                em.wait("V", sV, t0_fr3)
                c2 = em.op("V", sV, lambda e: e.tensor_copy(t0f[:], i8buf[:]))
                em.wait("V", sV, c2)
                em.op("V", sV, lambda e: e.tensor_sub(xqsb[:], xqsb[:], t0f[:]))
                em.wait("V", sV, sV.n)
                em.op("V", sV, lambda e: e.scalar_tensor_tensor(t0f[:], xqsb[:], 0.5, t0f[:], AL.is_ge, AL.add))
                em.wait("V", sV, sV.n)
                em.op("V", sV, lambda e: e.tensor_scalar(xqsb[:], xqsb[:], -0.5, None, AL.is_le))
                em.wait("V", sV, sV.n)
                qv = em.op("V", sV, lambda e: e.tensor_sub(xqsb[:], t0f[:], xqsb[:]))
                t0_fr3 = qv
                pend_xq = (qi, qv)
                if xqD is not None:
                    em.wait("Y", sV, qv)
                    dqv_ = em.dma("Y", sem("dbgxq"), xqD.ap()[xrow + sg * 128: xrow + (sg + 1) * 128, :], xqsb[:])
                    em.wait("V", SEMS["dbgxq"], dqv_)
            flush_xq(pend_xq)
            pend_xq = None
            xqT_done = sV.n
            p3_xp = (SEMS["sxp0"].n, SEMS["sxp1"].n)

        if "p4" in phases:
            swu = sem("p4wu")
            swd = sem("p5wd")
            sout = sem("outst")
            sxg = [sem("p5x0"), sem("p5x1")]
            em.wait("G", sT, sT.n)
            em.wait("G", sV, sV.n)
            em.wait("G", SEMS["sxp0"], SEMS["sxp0"].n)
            em.wait("G", SEMS["sxp1"], SEMS["sxp1"].n)
            em.wait("S", SEMS["sxp0"], SEMS["sxp0"].n)
            em.wait("S", SEMS["sxp1"], SEMS["sxp1"].n)
            dv = em.dma("G", swd, bass.AP(wdT, 0, [[32768, 128], [1024, 32], [1, D]]),
                        bass.AP(wdTD, 0, [[D, 128], [128 * D, 32], [1, D]]))
            em.wait("T", SEMS["p4wg"], wg_pref)
            em.wait("T", swd, dv)
            if "prep" in phases:
                em.wait("T", SEMS["pwst0"], PREP["done"][0])
                em.wait("T", SEMS["pwst1"], PREP["done"][1])
            em.wait("T", sV, xqT_done)
            em.wait("T", sS, sS.n)
            pair_s = [0, 0, 0]
            pair_v = [0, 0, 0]
            gb_rd = [0, 0]
            pend = None

            def fused_tail(p):
                bb_unused = None
                qq, quarters, sgq, dpair, s5 = p
                cpvs = [0, 0]
                for oh in range(4):
                    pb_ = (psB, psB2)[oh % 2]
                    em.wait("T", sV, quarters[oh])
                    em.wait("T", sS, cpvs[oh % 2])
                    tv = None
                    for j in range(8):
                        blk = oh * 8 + j
                        tv = em.op("T", sT, lambda e, blk=blk, j=j, pb_=pb_: e.transpose(
                            pb_[:, j * 128:(j + 1) * 128],
                            hqsb[:, blk * 128:(blk + 1) * 128], idb[:]))
                    em.wait("S", sT, tv)
                    cpvs[oh % 2] = em.op("S", sS, lambda e, oh=oh, pb_=pb_: e.activation(
                        hqTst[:, oh * 1024:(oh + 1) * 1024], pb_[:], AF.Identity))
                em.wait("T", sS, cpvs[0])
                em.wait("T", sS, cpvs[1])
                em.wait("T", sS, pair_s[dpair])
                em.wait("T", sV, pair_v[dpair])
                pb = 2 * dpair
                mv = None
                for it in range(IT):
                    em.op("T", sT, lambda e, it=it, pb=pb: e.matmul(
                        bank(pb), hqTst[:, it * 128:(it + 1) * 128],
                        wdT[:, it * 1024: it * 1024 + 512], start=(it == 0), stop=(it == IT - 1)))
                    mv = em.op("T", sT, lambda e, it=it, pb=pb: e.matmul(
                        bank(pb + 1), hqTst[:, it * 128:(it + 1) * 128],
                        wdT[:, it * 1024 + 512: it * 1024 + 1024], start=(it == 0), stop=(it == IT - 1)))
                em.wait("V", sT, mv)
                em.wait("S", sV, xld_rd[s5])
                em.wait("S", SEMS["outst"], SEMS["outst"].n)
                xv = em.dma("S", sxg[s5], xfb[s5][:], xpD.ap()[xrow + sgq * 128: xrow + (sgq + 1) * 128, :])
                em.op("V", sV, lambda e, pb=pb: e.bn_stats(bnb[:, 4:10], bank(pb)))
                em.op("V", sV, lambda e, pb=pb: e.bn_stats(bnb[:, 10:16], bank(pb + 1)))
                em.wait("V", sV, sV.n)
                em.op("V", sV, lambda e: e.bn_aggr(st2[:], bnb[:, 4:16]))
                em.wait("V", sV, sV.n)
                em.op("V", sV, lambda e, qq=qq: e.tensor_tensor(veps[:], sdA[:, qq:qq + 1], sdA[:, qq:qq + 1], AL.mult))
                vw = sV.n
                em.wait("S", sV, vw)
                sw = em.op("S", sS, lambda e: e.activation(sq[:], st2[:, 1:2], AF.Sqrt, bias=wscB[:, 9:10], scale=veps[:]))
                em.wait("V", sS, sw)
                em.op("V", sV, lambda e: e.reciprocal(rstd[:], sq[:]))
                em.wait("V", sV, sV.n)
                em.op("V", sV, lambda e, qq=qq: e.tensor_tensor(rmr[:], sdA[:, qq:qq + 1], rstd[:], AL.mult))
                em.wait("V", sV, sV.n)
                em.op("V", sV, lambda e: e.scalar_tensor_tensor(nmb[:], st2[:, 0:1], -1.0, rmr[:], AL.mult, AL.mult))
                cw = sV.n
                em.wait("S", sV, cw)
                em.wait("S", sV, dn_rd[0])
                yv = em.op("S", sS, lambda e, pb=pb: e.activation(
                    dnT[:], psF[:, pb * 512: pb * 512 + 1024], AF.Identity, bias=nmb[:], scale=rmr[:]))
                pair_s[dpair] = yv
                em.wait("V", sS, yv)
                em.wait("V", sxg[s5], SEMS[f"p5x{s5}"].n)
                ov = em.op("V", sV, lambda e, s5=s5: e.tensor_add(xfb[s5][:], xfb[s5][:], dnT[:]))
                dn_rd[0] = ov
                xld_rd[s5] = ov
                pair_v[dpair] = ov
                em.wait("Y", sV, ov)
                em.dma("Y", sout, out.ap()[xrow + sgq * 128: xrow + (sgq + 1) * 128, :], xfb[s5][:])

            dn_rd = [0]
            xld_rd = [sV.n, sV.n]
            for qi in range(LT):
                s = qi % 3
                dprev = (s + 2) % 3
                ogp = [(dprev, s), ((s + 1) % 3, dprev), (s, (s + 1) % 3), ((s + 1) % 3, s)]
                for og in range(4):
                    if og == 2 and pend is not None:
                        fused_tail(pend)
                        pend = None
                    gp, up = ogp[og]
                    mv = None
                    if og == 3:
                        em.wait("T", sS, pair_s[up])
                        em.wait("T", sV, pair_v[up])
                        for it in range(8):
                            fap = bass.AP(featA, it * 2048 + qi * 128, [[16384, 128], [1, 128]])
                            for oc in range(2):
                                col = og * 1024 + oc * 512
                                em.op("T", sT, lambda e, a=fap, it=it, oc=oc, col=col, up=up: e.matmul(
                                    bank(2 * up + oc), a, wuT8[:, it * 4096 + col: it * 4096 + col + 512],
                                    start=(it == 0), stop=(it == 7)))
                        em.wait("T", sS, pair_s[gp])
                        em.wait("T", sV, pair_v[gp])
                        for it in range(8):
                            fap = bass.AP(featA, it * 2048 + qi * 128, [[16384, 128], [1, 128]])
                            for oc in range(2):
                                col = og * 1024 + oc * 512
                                mv = em.op("T", sT, lambda e, a=fap, it=it, oc=oc, col=col, gp=gp: e.matmul(
                                    bank(2 * gp + oc), a, wgT[:, it * 4096 + col: it * 4096 + col + 512],
                                    start=(it == 0), stop=(it == 7)))
                    else:
                        for it in range(8):
                            fap = bass.AP(featA, it * 2048 + qi * 128, [[16384, 128], [1, 128]])
                            for oc in range(2):
                                col = og * 1024 + oc * 512
                                if it == 0 and oc == 0:
                                    em.wait("T", sS, max(pair_s[gp], pair_s[up]))
                                    em.wait("T", sV, max(pair_v[gp], pair_v[up]))
                                em.op("T", sT, lambda e, a=fap, it=it, oc=oc, col=col, gp=gp: e.matmul(
                                    bank(2 * gp + oc), a, wgT[:, it * 4096 + col: it * 4096 + col + 512],
                                    start=(it == 0), stop=(it == 7)))
                                mv = em.op("T", sT, lambda e, a=fap, it=it, oc=oc, col=col, up=up: e.matmul(
                                    bank(2 * up + oc), a, wuT8[:, it * 4096 + col: it * 4096 + col + 512],
                                    start=(it == 0), stop=(it == 7)))
                    em.wait("S", sT, mv)
                    for oc in range(2):
                        col = og * 1024 + oc * 512
                        em.wait("S", sV, gb_rd[oc])
                        gev = em.op("S", sS, lambda e, oc=oc, qi=qi, gp=gp: e.activation(
                            gbuf[:, oc * 512:(oc + 1) * 512], bank(2 * gp + oc), AF.Silu, scale=sgA[:, qi:qi + 1]))
                        pair_s[gp] = gev
                        em.wait("V", sS, gev)
                        hv = em.op("V", sV, lambda e, oc=oc, qi=qi, col=col, up=up: e.scalar_tensor_tensor(
                            hbuf[:, col:col + 512], bank(2 * up + oc), sgA[:, qi:qi + 1],
                            gbuf[:, oc * 512:(oc + 1) * 512], AL.mult, AL.mult))
                        pair_v[up] = hv
                        gb_rd[oc] = hv
                    em.wait("V", sV, sV.n)
                    em.op("V", sV, lambda e, og=og: e.tensor_reduce(
                        bnb[:, og:og + 1], hbuf[:, og * 1024:(og + 1) * 1024], AX.X, AL.max,
                        apply_absolute_value=True))
                em.wait("V", sV, sV.n)
                em.op("V", sV, lambda e: e.tensor_reduce(rmx[:], bnb[:, 0:4], AX.X, AL.max))
                em.wait("V", sV, sV.n)
                em.op("V", sV, lambda e: e.tensor_scalar(rmx[:], rmx[:], 1e-5, None, AL.max))
                em.wait("V", sV, sV.n)
                em.op("V", sV, lambda e: e.reciprocal(rmr[:], rmx[:]))
                em.wait("V", sV, sV.n)
                em.op("V", sV, lambda e, qi=qi: e.scalar_tensor_tensor(
                    sdA[:, qi:qi + 1], rmx[:], 1.0 / 127.0, wscB[:, 2:3], AL.mult, AL.mult))
                em.wait("V", sV, sV.n)
                r127 = em.op("V", sV, lambda e: e.tensor_scalar(
                    bnb[:, 16:17], rmr[:], 127.0, None, AL.mult))
                em.wait("S", sV, r127)
                t0_free = 0
                hq_quarters = []
                for oh in range(4):
                    cs = slice(oh * 1024, (oh + 1) * 1024)
                    uqv = em.op("S", sS, lambda e, cs=cs: e.activation(
                        hqsb[:, cs], hbuf[:, cs], AF.Identity, scale=bnb[:, 16:17]))
                    em.wait("S", sS, uqv)
                    c1 = em.op("S", sS, lambda e, cs=cs: e.activation(i8h[:, cs], hqsb[:, cs], AF.Identity))
                    em.wait("S", sS, c1)
                    em.wait("S", sV, t0_free)
                    c2 = em.op("S", sS, lambda e, cs=cs: e.activation(t0f[:], i8h[:, cs], AF.Identity))
                    em.wait("V", sS, c2)
                    em.op("V", sV, lambda e, cs=cs: e.tensor_sub(hqsb[:, cs], hqsb[:, cs], t0f[:]))
                    em.wait("V", sV, sV.n)
                    em.op("V", sV, lambda e, cs=cs: e.scalar_tensor_tensor(t0f[:], hqsb[:, cs], 0.5, t0f[:], AL.is_ge, AL.add))
                    em.wait("V", sV, sV.n)
                    em.op("V", sV, lambda e, cs=cs: e.tensor_scalar(hqsb[:, cs], hqsb[:, cs], -0.5, None, AL.is_le))
                    em.wait("V", sV, sV.n)
                    hqv = em.op("V", sV, lambda e, cs=cs: e.tensor_sub(hqsb[:, cs], t0f[:], hqsb[:, cs]))
                    t0_free = hqv
                    hq_quarters.append(hqv)
                if hqD is not None:
                    em.wait("Y", sV, sV.n)
                    dhq = em.dma("Y", sem("dbghq"), hqD.ap()[xrow + sigma(qi) * 128: xrow + (sigma(qi) + 1) * 128, :], hqsb[:])
                    em.wait("V", SEMS["dbghq"], dhq)
                pend = (qi, hq_quarters, sigma(qi), qi % 3, qi % 2)
                if qi == LT - 1:
                    fused_tail(pend)
                    pend = None

    if "p4" in phases:
        em.wait("Y", SEMS["outst"], SEMS["outst"].n)

    with nc.Block() as block:
        @block.tensor
        def _(e):
            for f in em.q["T"]:
                f(e)

        @block.vector
        def _(e):
            for f in em.q["V"]:
                f(e)

        @block.scalar
        def _(e):
            for f in em.q["S"]:
                f(e)

        @block.gpsimd
        def _(e):
            for f in em.q["G"]:
                f(e)

        @block.sync
        def _(e):
            for f in em.q["Y"]:
                f(e)
    stack.close()
    return nc


from concourse.bass_utils import run_bass_kernel_spmd
import ml_dtypes


def _sel_const():
    s = np.zeros((16, 1024), dtype=np.float32)
    for p in range(8):
        for m in range(128):
            s[2 * p + (m >= 64), p * 128 + m] = 1.0
    return s.astype(ml_dtypes.bfloat16)


def kernel(L0_attn_xin, L0_attn_W_qkv, L0_attn_W_o, L0_ffn_w_gate, L0_ffn_w_up, L0_ffn_w_down):
    x = np.ascontiguousarray(np.asarray(L0_attn_xin, dtype=np.float32))
    consts = {
        "wqkv": np.ascontiguousarray(np.asarray(L0_attn_W_qkv, dtype=np.float32)),
        "wo": np.ascontiguousarray(np.asarray(L0_attn_W_o, dtype=np.float32)),
        "wg": np.ascontiguousarray(np.asarray(L0_ffn_w_gate, dtype=np.float32)),
        "wu": np.ascontiguousarray(np.asarray(L0_ffn_w_up, dtype=np.float32)),
        "wd": np.ascontiguousarray(np.asarray(L0_ffn_w_down, dtype=np.float32)),
        "identc": np.eye(128, dtype=ml_dtypes.bfloat16),
        "onesc": np.ones((128, 128), dtype=ml_dtypes.bfloat16),
        "selc": _sel_const(),
    }
    nc = bass.Bass("TRN2", target_bir_lowering=False)
    build(nc)
    in_maps = []
    for c in range(8):
        im = dict(consts)
        im["xin"] = np.ascontiguousarray(x[2 * c: 2 * c + 2].reshape(2 * L, D))
        in_maps.append(im)
    res = run_bass_kernel_spmd(nc, in_maps, core_ids=list(range(8)))
    out = np.empty((16, L, D), dtype=np.float32)
    for c in range(8):
        out[2 * c: 2 * c + 2] = res.results[c]["out"].reshape(2, L, D)
    return out


# version 32
# speedup vs baseline: 1.0973x; 1.0331x over previous
import contextlib
import numpy as np
import concourse.bass as bass
import concourse.mybir as mybir

F32 = mybir.dt.float32
FP8 = mybir.dt.float8e4
BF16 = mybir.dt.bfloat16
I8 = mybir.dt.int8
AL = mybir.AluOpType
AF = mybir.ActivationFunctionType
AX = mybir.AxisListType

B_PER_CORE = 2
L = 2048
D = 1024
N3 = 3072
INNER = 4096
LT, CT, IT = 16, 8, 32
EPS = 1e-5


def sigma(qi):
    return qi


class Sem:
    __slots__ = ("h", "name", "n")

    def __init__(self, h, name):
        self.h, self.name, self.n = h, name, 0


class Em:
    def __init__(self):
        self.q = {k: [] for k in ("T", "V", "S", "G", "Y")}

    def add(self, eng, fn):
        self.q[eng].append(fn)

    def op(self, eng, sem, fn):
        sem.n += 1
        v = sem.n
        self.q[eng].append(lambda e, f=fn, s=sem: f(e).then_inc(s.h, 1))
        return v

    def dma(self, eng, sem, out_ap, in_ap):
        sem.n += 16
        v = sem.n
        self.q[eng].append(lambda e, o=out_ap, i=in_ap, s=sem: e.dma_start(o, i).then_inc(s.h, 16))
        return v

    def dma_t(self, eng, sem, out_ap, in_ap):
        sem.n += 16
        v = sem.n
        self.q[eng].append(lambda e, o=out_ap, i=in_ap, s=sem: e.dma_start_transpose(o, i).then_inc(s.h, 16))
        return v

    def wait(self, eng, sem, val):
        if val:
            self.q[eng].append(lambda e, s=sem, v=val: e.wait_ge(s.h, v))


def build(nc, dbg=(), phases=("prep", "p1", "p2", "p3", "p4", "p5")):
    dbg = set(dbg)
    xin = nc.dram_tensor("xin", [B_PER_CORE * L, D], F32, kind="ExternalInput")
    Wqkv = nc.dram_tensor("wqkv", [D, N3], F32, kind="ExternalInput")
    Wo = nc.dram_tensor("wo", [D, D], F32, kind="ExternalInput")
    wg = nc.dram_tensor("wg", [INNER, D], F32, kind="ExternalInput")
    wu = nc.dram_tensor("wu", [INNER, D], F32, kind="ExternalInput")
    wd = nc.dram_tensor("wd", [D, INNER], F32, kind="ExternalInput")
    identC = nc.dram_tensor("identc", [128, 128], BF16, kind="ExternalInput")
    onesC = nc.dram_tensor("onesc", [128, 128], BF16, kind="ExternalInput")
    selC = nc.dram_tensor("selc", [16, 1024], BF16, kind="ExternalInput")
    out = nc.dram_tensor("out", [B_PER_CORE * L, D], F32, kind="ExternalOutput")

    def scr(name, shape, dt):
        return nc.dram_tensor(name, shape, dt,
                              kind="ExternalOutput" if name in dbg else "Internal")

    qkvD = scr("qkvd", [B_PER_CORE * L * 48, 64], BF16)
    wgTD = scr("wgtd", [D, INNER], FP8)
    wuTD = scr("wutd", [D, INNER], FP8)
    wdTD = scr("wdtd", [INNER, D], FP8)
    xpD = scr("xpd", [B_PER_CORE * L, D], F32)
    zrowD = scr("zrowd", [16, L], BF16)
    hqTD = scr("hqtd", [B_PER_CORE * LT * 128, INNER], BF16)
    wscD = scr("wscd", [1, 12], F32)
    azD = scr("azd", [B_PER_CORE * 128, 16384], BF16) if "azd" in dbg else None
    hqD = scr("hqd", [B_PER_CORE * L, INNER], BF16) if "hqd" in dbg else None
    xqD = scr("xqd", [B_PER_CORE * L, D], BF16) if "xqd" in dbg else None

    A = lambda n, s, d, o: nc.alloc_sbuf_tensor_at(n, s, d, offset=o)
    MB = 18432
    W1, W2, W3, W4 = MB, MB + 65536, MB + 131072, MB + 163840
    wtile = [A(f"wtile{i}", [128, 4096], F32, W2 + 32768 + i * 16384) for i in range(2)]
    wq8f = A("wq8f", [128, 4096], F32, W4 + 11264)
    wqb = A("wqb", [128, 4096], BF16, W4 + 27648)
    wqb2 = A("wqb2", [128, 4096], BF16, W3)
    wts = [A(f"wts{i}", [128, 1024], FP8, W4 + 35840 + i * 1024) for i in range(2)]
    pbn = A("pbn", [128, 32], F32, W4 + 37888)
    thr2 = A("thr2", [1, 6], F32, W4 + 44096)
    WqB = A("wqb_full", [128, 8 * 3072], BF16, W1)
    qsbS = [A(f"qsb{i}", [128, 1536], BF16, W1 + 49152 + i * 3072) for i in range(2)]
    qkS = [A(f"qks{i}", [128, 16 * 64], BF16, W1 + i * 2048) for i in range(4)]
    vS = [A(f"vs{i}", [128, 16 * 65], BF16, W1 + 8192 + i * 2112) for i in range(2)]
    qT1 = [A(f"qt1_{i}", [128, L], BF16, W1 + 12544 + i * 4096) for i in range(2)]
    kT1 = [A(f"kt1_{i}", [128, L], BF16, W1 + 20736 + i * 4096) for i in range(2)]
    phS = [A(f"phs{i}", [128, 1024], BF16, W1 + 28928 + i * 2048) for i in range(3)]
    azst = [A(f"azst{i}", [65, L], BF16, W1 + 35072 + i * 4096) for i in range(2)]
    zall = A("zall", [16, L], BF16, W1 + 43264)
    zala = A("zala", [128, 256], BF16, W1 + 43264)
    zif2 = A("zif2", [128, 256], F32, W1 + 47360)
    zi2 = A("zi2", [128, 256], BF16, W1 + 55552)
    zinvF = A("zinvf", [16, L], F32, W1 + 47360)
    zinv = A("zinv", [16, L], BF16, W1 + 55552)
    wgT = A("wgt", [128, 8 * 4096], FP8, W1)
    wuT8 = A("wut8", [128, 8 * 4096], FP8, W1 + 32768)
    wdT = A("wdt", [128, 32 * 1024], FP8, W2)
    featB = A("featb", [128, 8 * 2048], BF16, W2)
    x2buf = A("x2buf", [128, 1024], F32, W2 + 32768)
    ybuf = A("ybuf", [128, 1024], F32, W2 + 36864)
    xpbuf = [A(f"xpbuf{i}", [128, 1024], F32, W2 + 40960 + i * 4096) for i in range(2)]
    WoB = A("wob", [128, 8 * 1024], BF16, W2 + 49152)
    i8buf = A("i8buf", [128, 1024], I8, W4 + 13312)
    xqsb = A("xqsb", [128, 1024], BF16, W4 + 11264)
    featA = A("feata", [128, 8 * 2048], BF16, W3)
    xb = [A(f"xb{i}", [128, 1024], F32, W4 + i * 4096) for i in range(2)]
    gbuf = A("gbuf", [128, 1024], F32, W4)
    bnb = A("bnb", [128, 48], F32, W4 + 8192)
    st2 = A("st2", [128, 2], F32, W4 + 8256)
    veps = A("veps", [128, 1], F32, W4 + 8288)
    sq = A("sq", [128, 1], F32, W4 + 8320)
    rstd = A("rstd", [128, 1], F32, W4 + 8352)
    nmb = A("nmb", [128, 1], F32, W4 + 8384)
    rmx = A("rmx", [128, 1], F32, W4 + 8416)
    rmr = A("rmr", [128, 1], F32, W4 + 8448)
    accw = A("accw", [128, 4], F32, W4 + 8480)
    sgA = A("sga", [128, 16], F32, W4 + 8512)
    sdA = A("sda", [128, 16], F32, W4 + 8576)
    wsc = A("wsc", [1, 4], F32, W4 + 8640)
    red1 = A("red1", [1, 1], F32, W4 + 8672)
    wscB = A("wscb", [128, 12], F32, W4 + 44032)
    idb = A("idb", [128, 128], BF16, W4 + 8704)
    onb = A("onb", [128, 128], BF16, W4 + 8960)
    selB = A("selb", [16, 1024], BF16, W4 + 9216)
    i8h = A("i8h", [128, 4096], I8, W4 + 4096)
    hbuf = A("hbuf", [128, 4096], F32, W4 + 11264)
    hqsb = A("hqsb", [128, 4096], BF16, W4 + 27648)
    hqTst = A("hqtst", [128, 4096], BF16, W4 + 35840)
    t0f = A("t0f", [128, 1024], BF16, W4 + 44128)
    wq_f32a = A("wqf32a", [128, 3072], F32, W4 + 14336)
    wq_f32b = A("wqf32b", [128, 3072], F32, W4 + 27648)
    hq2 = [A(f"hq2_{i}", [128, 4096], BF16, W4 + 11264 + i * 8192) for i in range(2)]
    dnT = A("dnt", [128, 1024], F32, W2 + 32768)
    xfb = [A(f"xfb{i}", [128, 1024], F32, W2 + 36864 + i * 4096) for i in range(2)]

    em = Em()
    stack = contextlib.ExitStack()
    SEMS = {}

    def sem(name):
        if name not in SEMS:
            SEMS[name] = Sem(stack.enter_context(nc.semaphore(name)), name)
        return SEMS[name]

    sT, sV, sS, sG = sem("c_t"), sem("c_v"), sem("c_s"), sem("c_g")
    psF = stack.enter_context(nc.psum_tensor("psf", [128, 3072], F32))
    psB = stack.enter_context(nc.psum_tensor("psb", [128, 1024], BF16))
    psB2 = stack.enter_context(nc.psum_tensor("psb2", [128, 1024], BF16))
    bank = lambda i: psF[:, i * 512:(i + 1) * 512]

    def ln_consts(src_aps):
        em.wait("V", sV, sV.n)
        for c, ap in enumerate(src_aps):
            em.op("V", sV, lambda e, c=c, a=ap: e.bn_stats(bnb[:, c * 6:(c + 1) * 6], a))
        em.wait("V", sV, sV.n)
        em.op("V", sV, lambda e, n=len(src_aps): e.bn_aggr(st2[:], bnb[:, 0:6 * n]))
        vw = sV.n
        em.wait("S", sV, vw)
        sw = em.op("S", sS, lambda e: e.activation(sq[:], st2[:, 1:2], AF.Sqrt, bias=wscB[:, 9:10]))
        em.wait("V", sS, sw)
        em.op("V", sV, lambda e: e.reciprocal(rstd[:], sq[:]))
        em.wait("V", sV, sV.n)
        em.op("V", sV, lambda e: e.scalar_tensor_tensor(nmb[:], st2[:, 0:1], -1.0, rstd[:], AL.mult, AL.mult))
        cw = sV.n
        return cw

    idv = onv = None
    prep_done_y = 0

    PREP = {"wscB_ready": 0, "done": (0, 0)}
    def prep_gen():
        gld = [sem("pw0"), sem("pw1")]
        pst = [sem("pwst0"), sem("pwst1")]
        sbc = sem("sbc")
        WSPEC = ((wg, 32, 1024, wgTD, INNER), (wu, 32, 1024, wuTD, INNER),
                 (wd, 8, 4096, wdTD, D))
        slot_rd = [0, 0]
        stage_rd = [0, 0]
        tiles1 = [(W, t, tw, n_t, wi) for wi, (W, n_t, tw, dst, dcols) in enumerate(WSPEC)
                  for t in range(n_t)]
        lv1 = {}

        def issue1(k):
            Wk, tk, twk, _, _ = tiles1[k]
            sk = k % 2
            em.wait("Y", sV, slot_rd[sk])
            lv1[k] = em.dma("Y", gld[sk], wtile[sk][:, 0:twk], Wk.ap()[tk * 128:(tk + 1) * 128, :])
        issue1(0)
        for k, (W_, t_, tw_, nt_, wi_) in enumerate(tiles1):
            if k + 1 < len(tiles1):
                issue1(k + 1)
            s = k % 2
            em.wait("V", gld[s], lv1[k])
            if t_ == 0:
                em.wait("V", sV, sV.n)
            em.op("V", sV, lambda e, s=s, w=tw_, t=t_: e.tensor_reduce(
                pbn[:, t % 32: t % 32 + 1], wtile[s][:, 0:w], AX.X, AL.add,
                apply_absolute_value=True))
            slot_rd[s] = sV.n
            if t_ == nt_ - 1:
                em.wait("V", sV, sV.n)
                em.op("V", sV, lambda e, wi=wi_, n_t=nt_: e.tensor_reduce(
                    accw[:, wi:wi + 1], pbn[:, 0:n_t] if n_t <= 32 else pbn[:, 0:32],
                    AX.X, AL.add))
            yield
        em.wait("G", sV, sV.n)
        gv = em.op("G", sG, lambda e: e.tensor_reduce(red1[0:1, 0:1], accw[:, 0:1], AX.C, AL.add))
        gv = em.op("G", sG, lambda e: e.tensor_reduce(wsc[0:1, 3:4], accw[:, 1:2], AX.C, AL.add))
        gv = em.op("G", sG, lambda e: e.tensor_reduce(thr2[0:1, 0:1], accw[:, 2:3], AX.C, AL.add))
        em.wait("V", sG, gv)
        for wi, (W, n_t, tw, dst, dcols) in enumerate(WSPEC):
            NN = float(n_t * 128 * tw)
            srcap = (red1[0:1, 0:1], wsc[0:1, 3:4], thr2[0:1, 0:1])[wi]
            em.op("V", sV, lambda e, wi=wi, NN=NN, a=srcap: e.tensor_scalar(
                wsc[:, wi:wi + 1], a, 1.0 / NN, 1e-5, AL.mult, AL.max))
        em.wait("V", sV, sV.n)
        for wi in range(3):
            em.op("V", sV, lambda e, wi=wi: e.tensor_scalar(thr2[:, 2 * wi:2 * wi + 1], wsc[:, wi:wi + 1], 0.5, None, AL.mult))
            em.op("V", sV, lambda e, wi=wi: e.tensor_scalar(thr2[:, 2 * wi + 1:2 * wi + 2], wsc[:, wi:wi + 1], -0.5, None, AL.mult))
        tv_ = sV.n
        em.wait("G", sV, tv_)
        bv = em.dma("G", sbc, wscD.ap()[0:1, 3:9], thr2[:])
        bv2 = em.dma("G", sbc, wscD.ap()[0:1, 0:3], wsc[:, 0:3])
        em.wait("G", sbc, bv2)
        bv = em.dma("G", sbc, wscB[:, 3:9], bass.AP(wscD, 3, [[1, 1], [0, 128], [1, 6]]))
        bv2 = em.dma("G", sbc, wscB[:, 0:3], bass.AP(wscD, 0, [[1, 1], [0, 128], [1, 3]]))
        PREP["wscB_ready"] = bv2
        thr_ready = bv2
        yield
        em.wait("V", sT, sT.n)
        wbufs = [wqb, wqb2]
        wqb_rd = [0, 0]
        pend2 = [None]

        def flush2():
            t_, tw_, dst_, dcols_, cpv_, wb_, tg_ = pend2[0]
            em.wait("T", sV, cpv_)
            nblk = tw_ // 128
            tvl = None
            for b0 in range(0, nblk, 8):
                nb = min(8, nblk - b0)
                ss = (b0 // 8) % 2
                tv = None
                for j in range(nb):
                    tv = em.op("T", sT, lambda e, j=j, b0=b0, wb=wb_: e.transpose(
                        psB[:, j * 128:(j + 1) * 128], wb[:, (b0 + j) * 128:(b0 + j + 1) * 128], idb[:]))
                tvl = tv
                em.wait("V", sT, tv)
                em.wait("V", pst[ss], stage_rd[ss])
                scv = em.op("V", sV, lambda e, nb=nb, ss=ss: e.tensor_copy(
                    wts[ss][:, 0:nb * 128], psB[:, 0:nb * 128]))
                em.wait("T", sV, scv)
                em.wait("Y", sV, scv)
                dst_ap = bass.AP(dst_, (b0 * 128) * dcols_ + t_ * 128,
                                 [[dcols_, 128], [128 * dcols_, nb], [1, 128]])
                src_ap = bass.AP(wts[ss], 0, [[1024, 128], [128, nb], [1, 128]])
                stage_rd[ss] = em.dma("Y", pst[ss], dst_ap, src_ap)
            wqb_rd[tg_ % 2] = tvl
            pend2[0] = None
        tiles2 = [(W, t, tw, dst, dcols, wi) for wi, (W, n_t, tw, dst, dcols) in enumerate(WSPEC)
                  for t in range(n_t)]
        lv2 = {}

        def issue2(k):
            Wk, tk, twk, _, _, _ = tiles2[k]
            sk = k % 2
            em.wait("Y", sV, slot_rd[sk])
            lv2[k] = em.dma("Y", gld[sk], wtile[sk][:, 0:twk], Wk.ap()[tk * 128:(tk + 1) * 128, :])
        issue2(0)
        for k, (W_, t_, tw_, dst_, dcols_, wi_) in enumerate(tiles2):
            if k + 1 < len(tiles2):
                issue2(k + 1)
            s = k % 2
            em.wait("V", gld[s], lv2[k])
            if k == 0:
                em.wait("V", SEMS["sbc"], thr_ready)
            em.wait("V", sV, sV.n)
            em.op("V", sV, lambda e, s=s, w=tw_, wi=wi_: e.tensor_scalar(
                wq8f[:, 0:w], wtile[s][:, 0:w], wscB[:, 4 + 2 * wi:5 + 2 * wi], None, AL.is_le))
            em.wait("V", sV, sV.n)
            em.wait("V", sT, wqb_rd[k % 2])
            cpv = em.op("V", sV, lambda e, s=s, w=tw_, wi=wi_, wb=wbufs[k % 2]: e.scalar_tensor_tensor(
                wb[:, 0:w], wtile[s][:, 0:w], wscB[:, 3 + 2 * wi:4 + 2 * wi],
                wq8f[:, 0:w], AL.is_ge, AL.subtract))
            slot_rd[s] = cpv
            if pend2[0] is not None:
                flush2()
            pend2[0] = (t_, tw_, dst_, dcols_, cpv, wbufs[k % 2], k)
            yield
        flush2()
        PREP["done"] = (SEMS["pwst0"].n, SEMS["pwst1"].n)

    _pgen = [None]

    def pump_prep(n):
        if "prep" not in phases:
            return
        if _pgen[0] is None:
            _pgen[0] = prep_gen()
        for _ in range(n):
            try:
                next(_pgen[0])
            except StopIteration:
                break

    sx = [sem("sx0"), sem("sx1")]
    sxp = [sem("sxp0"), sem("sxp1")]
    for b in range(B_PER_CORE):
        xrow = b * L
        if "p1" in phases:
            swq2 = [sem("p1w0"), sem("p1w1")]
            sqk = [sem("p1q0"), sem("p1q1")]
            if idv is None:
                gid = sem("gid")
                epsv = em.op("V", sV, lambda e: e.memset(wscB[:, 9:10], EPS))
                em.wait("S", sV, epsv)
                em.wait("V", sV, epsv)
                idv = em.dma("G", gid, idb[:], identC.ap())
                onv = em.dma("G", gid, onb[:], onesC.ap())
                onv = em.dma("G", gid, selB[:], selC.ap())
                em.wait("T", gid, onv)
            em.wait("G", sV, sV.n)
            em.wait("G", sT, sT.n)
            em.wait("T", sV, sV.n)
            em.wait("T", sS, sS.n)
            em.wait("S", sV, sV.n)
            if "outst" in SEMS:
                em.wait("S", SEMS["outst"], SEMS["outst"].n)
            wqstg = [wq_f32a, wq_f32b]
            wq_cast = [sV.n, sV.n]
            wv = 0
            for ctw in range(CT):
                sw_ = ctw % 2
                em.wait("Y", sV, wq_cast[sw_])
                lvw = em.dma("Y", swq2[sw_], wqstg[sw_][:],
                             Wqkv.ap()[ctw * 128:(ctw + 1) * 128, :])
                em.wait("V", swq2[sw_], lvw)
                wq_cast[sw_] = em.op("V", sV, lambda e, ctw=ctw, sw_=sw_: e.tensor_copy(
                    WqB[:, ctw * 3072:(ctw + 1) * 3072], wqstg[sw_][:]))
            wv = max(wq_cast)
            x1T_ready = [0] * LT
            xb_rd = [sS.n, sS.n]
            qkv_ev = [0, 0]
            qsb_rd = [0, 0]
            qkv_store_tile = []
            wq_started = False
            for lt in range(LT):
                s = lt % 2
                em.wait("S", sS, xb_rd[s])
                lv = em.dma("S", sx[s], xb[s][:], xin.ap()[xrow + lt * 128: xrow + (lt + 1) * 128, :])
                em.wait("V", sx[s], lv)
                cw = ln_consts([xb[s][:, 0:512], xb[s][:, 512:1024]])
                em.wait("S", sV, cw)
                av = em.op("S", sS, lambda e, s=s: e.activation(
                    xqsb[:], xb[s][:], AF.Identity, bias=nmb[:], scale=rstd[:]))
                xb_rd[s] = av
                em.wait("T", sS, av)
                tv = None
                for ct in range(CT):
                    tv = em.op("T", sT, lambda e, ct=ct: e.transpose(
                        psB[:, ct * 128:(ct + 1) * 128], xqsb[:, ct * 128:(ct + 1) * 128], idb[:]))
                em.wait("V", sT, tv)
                cpv = em.op("V", sV, lambda e, lt=lt: e.tensor_copy(
                    bass.AP(featA, lt * 128, [[16384, 128], [2048, 8], [1, 128]]),
                    bass.AP(psB, 0, [[1024, 128], [128, 8], [1, 128]])))
                x1T_ready[lt] = cpv
                em.wait("T", sV, cpv)
                if not wq_started:
                    em.wait("T", sV, wv)
                    wq_started = True
                for hf in range(2):
                    em.wait("T", sV, qkv_ev[hf])
                    mv = None
                    for ct in range(CT):
                        fap = bass.AP(featA, ct * 2048 + lt * 128, [[16384, 128], [1, 128]])
                        for nc3 in range(3):
                            nc6 = hf * 3 + nc3
                            mv = em.op("T", sT, lambda e, a=fap, ct=ct, nc6=nc6, bk=hf * 3 + nc3: e.matmul(
                                bank(bk), a, WqB[:, ct * 3072 + nc6 * 512: ct * 3072 + nc6 * 512 + 512],
                                start=(ct == 0), stop=(ct == CT - 1)))
                    em.wait("V", sT, mv)
                    em.wait("V", sqk[hf], qsb_rd[hf])
                    ev = em.op("V", sV, lambda e, hf=hf: e.tensor_copy(
                        qsbS[hf][:], psF[:, hf * 1536: hf * 1536 + 1536]))
                    qkv_ev[hf] = ev
                    em.wait("Y", sV, ev)
                    dstap = bass.AP(qkvD, (xrow + lt * 128) * 48 * 64 + hf * 1536,
                                    [[3072, 128], [1, 1536]])
                    qsb_rd[hf] = em.dma("Y", sqk[hf], dstap, qsbS[hf][:])
                qkv_store_tile.append((SEMS["p1q0"].n, SEMS["p1q1"].n))
            p1q = (SEMS["p1q0"].n, SEMS["p1q1"].n)
        if "p2" in phases:
            sqh = [sem("p2q0"), sem("p2q1")]
            skh = [sem("p2k0"), sem("p2k1")]
            svh = [sem("p2v0"), sem("p2v1")]
            szc = sem("szc")
            zdma_last = szc.n
            em.wait("G", sT, sT.n)
            em.wait("V", sT, sT.n)
            em.wait("T", sV, sV.n)
            em.wait("T", sS, sS.n)
            for i in range(2):
                em.op("V", sV, lambda e, i=i: e.memset(
                    bass.AP(vS[i], 64, [[1040, 128], [65, 16], [1, 1]]), 1.0))
            vones = sV.n
            em.wait("G", sV, vones)
            em.wait("Y", sV, vones)
            em.wait("S", sV, vones)
            qk_rd = [0, 0]
            v_rd = [0, 0]
            dd_rd = [0, 0]
            az_ev = [0, 0]
            expv = [0, 0]
            pend_head = None
            av_done = 0
            zld = sem("zld")
            base = b * L * 48

            def head_ready(h):
                need = min(LT, (33 + h) * 2048 // (48 * 128) + 1)
                return qkv_store_tile[need - 1]
            for h in range(16):
                hs = h % 2
                rdy = head_ready(h)
                def apt(seg, dstbuf):
                    return (dstbuf[:], bass.AP(qkvD, (base + seg * L) * 64, [[128, 1024], [1, 128]]))
                em.wait("Y", SEMS["p1q0"], rdy[0])
                em.wait("Y", SEMS["p1q1"], rdy[1])
                em.wait("Y", sV, qk_rd[hs])
                o_, i_ = apt(h, qkS[hs])
                qv = em.dma_t("Y", sqh[hs], o_, i_)
                em.wait("Y", SEMS["p1q0"], rdy[0])
                em.wait("Y", SEMS["p1q1"], rdy[1])
                em.wait("Y", sV, qk_rd[hs])
                o_, i_ = apt(16 + h, qkS[2 + hs])
                kv = em.dma_t("Y", skh[hs], o_, i_)
                em.wait("G", SEMS["p1q0"], rdy[0])
                em.wait("G", SEMS["p1q1"], rdy[1])
                em.wait("G", sT, v_rd[hs])
                vv = em.dma("G", svh[hs],
                            bass.AP(vS[hs], 0, [[16 * 65, 128], [65, 16], [1, 64]]),
                            bass.AP(qkvD, (base + (32 + h) * L) * 64,
                                    [[64, 128], [128 * 64, 16], [1, 64]]))

                em.wait("V", sqh[hs], qv)
                em.wait("V", sT, dd_rd[hs])
                for hh in range(2):
                    em.op("V", sV, lambda e, hs=hs, hh=hh: e.tensor_copy(
                        bass.AP(qT1[hs], hh * 64 * L, [[L, 64], [2, 1024]]),
                        bass.AP(qkS[hs], 0, [[1024, 64], [1, 1024]])))
                    qtv = em.op("V", sV, lambda e, hs=hs, hh=hh: e.tensor_copy(
                        bass.AP(qT1[hs], hh * 64 * L + 1, [[L, 64], [2, 1024]]),
                        bass.AP(qkS[hs], 64 * 1024, [[1024, 64], [1, 1024]])))
                em.wait("V", skh[hs], kv)
                for hh in range(2):
                    em.op("V", sV, lambda e, hs=hs, hh=hh: e.tensor_copy(
                        bass.AP(kT1[hs], hh * 64 * L, [[L, 64], [2, 1024]]),
                        bass.AP(qkS[2 + hs], 0, [[1024, 64], [1, 1024]])))
                    ktv = em.op("V", sV, lambda e, hs=hs, hh=hh: e.tensor_copy(
                        bass.AP(kT1[hs], hh * 64 * L + 1, [[L, 64], [2, 1024]]),
                        bass.AP(qkS[2 + hs], 64 * 1024, [[1024, 64], [1, 1024]])))
                em.wait("T", sV, ktv)
                qk_rd[hs] = ktv
                if pend_head is not None:
                    ph, pazs = pend_head
                    em.wait("V", sT, av_done)
                    az_ev[1] = em.op("V", sV, lambda e, azs=pazs: e.tensor_copy(
                        azs[:, 3 * 512:4 * 512], psF[0:65, 5 * 512:6 * 512]))
                    pp_ = ph // 2
                    rlo_ = (ph % 2) * 64
                    em.wait("V", sV, sV.n)
                    em.wait("V", szc, zdma_last)
                    em.op("V", sV, lambda e, p=pp_, rlo=rlo_, azs=pazs: e.tensor_copy(
                        featB[rlo:rlo + 64, p * 2048:(p + 1) * 2048], azs[0:64, :]))
                    em.wait("G", sV, sV.n)
                    zdma_last = em.dma("G", szc, zrowD.ap()[ph:ph + 1, :], pazs[64:65, :])
                    pend_head = None
                em.wait("T", svh[hs], vv)
                if b == 0 and h == 0:
                    em.wait("T", sV, vones)
                azs = azst[h % 2]
                for s4 in range(4):
                    ab = 4 + (s4 % 2)
                    em.wait("T", sV, az_ev[s4 % 2])
                    for ktp in range(9):
                        pp = ktp % 2
                        if ktp < 8:
                            em.wait("T", sS, expv[pp])
                            for half in range(2):
                                kt = 2 * ktp + half
                                rr = slice(half * 64, half * 64 + 64)
                                smv = em.op("T", sT, lambda e, hs=hs, kt=kt, rr=rr, s4=s4, bk=2 * pp + half: e.matmul(
                                    bank(bk), kT1[hs][rr, kt * 128:(kt + 1) * 128],
                                    qT1[hs][rr, s4 * 512:(s4 + 1) * 512],
                                    start=True, stop=True, skip_group_check=True))
                            em.wait("S", sT, smv)
                            expv[pp] = em.op("S", sS, lambda e, pp=pp: e.activation(
                                phS[pp][:], psF[:, 2 * pp * 512: (2 * pp + 2) * 512], AF.Exp, scale=0.125))
                        if ktp >= 1:
                            qp = (ktp - 1) % 2
                            em.wait("T", sS, expv[qp])
                            for half in range(2):
                                kt = 2 * (ktp - 1) + half
                                em.op("T", sT, lambda e, hs=hs, kt=kt, ab=ab, qp=qp, half=half: e.matmul(
                                    psF[0:65, ab * 512:(ab + 1) * 512], vS[hs][:, kt * 65: kt * 65 + 65],
                                    phS[qp][:, half * 512:(half + 1) * 512],
                                    start=(kt == 0), stop=(kt == 15)))
                    v_rd[hs] = sT.n
                    dd_rd[hs] = sT.n
                    if s4 < 3:
                        em.wait("V", sT, sT.n)
                        col = s4 * 512
                        az_ev[s4 % 2] = em.op("V", sV, lambda e, ab=ab, col=col, azs=azs: e.tensor_copy(
                            azs[:, col:col + 512], psF[0:65, ab * 512:(ab + 1) * 512]))
                    else:
                        av_done = sT.n
                        pend_head = (h, azs)
                    if b == 0:
                        pump_prep(3)
            if b == 0:
                pump_prep(1000)
            if pend_head is not None:
                ph, pazs = pend_head
                em.wait("V", sT, av_done)
                em.op("V", sV, lambda e, azs=pazs: e.tensor_copy(
                    azs[:, 3 * 512:4 * 512], psF[0:65, 5 * 512:6 * 512]))
                em.wait("V", sV, sV.n)
                em.wait("V", szc, zdma_last)
                em.op("V", sV, lambda e, p=ph // 2, rlo=(ph % 2) * 64, azs=pazs: e.tensor_copy(
                    featB[rlo:rlo + 64, p * 2048:(p + 1) * 2048], azs[0:64, :]))
                em.wait("G", sV, sV.n)
                zdma_last = em.dma("G", szc, zrowD.ap()[ph:ph + 1, :], pazs[64:65, :])
                pend_head = None
            em.wait("G", szc, zdma_last)
            zlv = em.dma("G", zld, zala[:], bass.AP(zrowD, 0, [[256, 128], [1, 256]]))
            em.wait("V", zld, zlv)
            em.op("V", sV, lambda e: e.reciprocal(zif2[:], zala[:]))
            em.wait("V", sV, sV.n)
            zcv = em.op("V", sV, lambda e: e.tensor_copy(zi2[:], zif2[:]))
            em.wait("G", sV, zcv)
            zsv = em.dma("G", zld, bass.AP(zrowD, 0, [[256, 128], [1, 256]]), zi2[:])
            em.wait("G", zld, zsv)
            zlv2 = em.dma("G", zld, zinv[:], zrowD.ap())
            em.wait("V", zld, zlv2)
            em.wait("T", zld, zlv2)
            zb_rd = [0, 0]
            zidx = 0
            for p in range(8):
                for c in range(4):
                    bk = 4 + (zidx % 2)
                    em.wait("T", sV, zb_rd[zidx % 2])
                    m2 = em.op("T", sT, lambda e, p=p, c=c, bk=bk: e.matmul(
                        bank(bk), selB[0:16, p * 128:(p + 1) * 128],
                        zinv[0:16, c * 512:(c + 1) * 512], start=True, stop=True))
                    em.wait("V", sT, m2)
                    zb_rd[zidx % 2] = em.op("V", sV, lambda e, p=p, c=c, bk=bk: e.tensor_tensor(
                        featB[:, p * 2048 + c * 512: p * 2048 + (c + 1) * 512],
                        featB[:, p * 2048 + c * 512: p * 2048 + (c + 1) * 512],
                        bank(bk), AL.mult))
                    zidx += 1
            featB_done = sV.n
            if azD is not None:
                em.wait("Y", sV, featB_done)
                dv_ = em.dma("Y", sem("dbgaz"), azD.ap()[b * 128:(b + 1) * 128, :], featB[:])
                em.wait("V", SEMS["dbgaz"], dv_)
        if "p3" in phases:
            swo = sem("p3wo")
            em.wait("G", sV, sV.n)
            em.wait("G", sT, sT.n)
            if "prep" in phases:
                em.wait("G", SEMS["pwst0"], SEMS["pwst0"].n)
                em.wait("G", SEMS["pwst1"], SEMS["pwst1"].n)
            wov = em.dma("G", swo, bass.AP(WoB, 0, [[8192, 128], [1024, 8], [1, 1024]]),
                         bass.AP(Wo, 0, [[D, 128], [128 * D, 8], [1, 1024]]))
            em.wait("T", swo, wov)
            if "p4" in phases and "prep" in phases:
                swg = sem("p4wg")
                em.wait("G", SEMS["pwst0"], SEMS["pwst0"].n)
                em.wait("G", SEMS["pwst1"], SEMS["pwst1"].n)
                em.wait("G", sV, sV.n)
                em.wait("G", sT, sT.n)
                for nm in ("p2q0", "p2q1", "p2k0", "p2k1", "p2v0", "p2v1"):
                    if nm in SEMS:
                        em.wait("G", SEMS[nm], SEMS[nm].n)
                em.dma("G", swg, bass.AP(wgT, 0, [[32768, 128], [4096, 8], [1, INNER]]),
                       bass.AP(wgTD, 0, [[INNER, 128], [128 * INNER, 8], [1, INNER]]))
                wg_pref = em.dma("G", swg, bass.AP(wuT8, 0, [[32768, 128], [4096, 8], [1, INNER]]),
                                 bass.AP(wuTD, 0, [[INNER, 128], [128 * INNER, 8], [1, INNER]]))
            if "p2" in phases:
                em.wait("T", sV, featB_done)
            em.wait("T", sS, sS.n)
            if "outst" in SEMS:
                em.wait("V", SEMS["outst"], SEMS["outst"].n)
            prev_yv = [0, 0]
            xb_rd3 = [sV.n, sV.n]
            xp_wr = [SEMS["sxp0"].n, SEMS["sxp1"].n]
            x2_rd = 0
            t0_fr3 = 0
            pend_xq = None

            def flush_xq(p):
                if p is None:
                    return
                qq, qvv = p
                em.wait("T", sV, qvv)
                tv = None
                for ct in range(CT):
                    tv = em.op("T", sT, lambda e, ct=ct: e.transpose(
                        psB[:, ct * 128:(ct + 1) * 128], xqsb[:, ct * 128:(ct + 1) * 128], idb[:]))
                em.wait("V", sT, tv)
                cpv = em.op("V", sV, lambda e, qq=qq: e.tensor_copy(
                    bass.AP(featA, qq * 128, [[16384, 128], [2048, 8], [1, 128]]),
                    bass.AP(psB, 0, [[1024, 128], [128, 8], [1, 128]])))
                em.wait("T", sV, cpv)
            for qi in range(LT):
                sg = sigma(qi)
                s = qi % 2
                pb = s * 2
                em.wait("S", sV, xb_rd3[s])
                lvx = em.dma("S", sx[s], xb[s][:], xin.ap()[xrow + sg * 128: xrow + (sg + 1) * 128, :])
                em.wait("T", sS, prev_yv[s])
                mv = None
                for ft in range(8):
                    fap = featB[:, ft * 2048 + qi * 128: ft * 2048 + qi * 128 + 128]
                    em.op("T", sT, lambda e, a=fap, ft=ft, pb=pb: e.matmul(
                        bank(pb), a, WoB[:, ft * 1024: ft * 1024 + 512], start=(ft == 0), stop=(ft == 7)))
                    mv = em.op("T", sT, lambda e, a=fap, ft=ft, pb=pb: e.matmul(
                        bank(pb + 1), a, WoB[:, ft * 1024 + 512: ft * 1024 + 1024],
                        start=(ft == 0), stop=(ft == 7)))
                flush_xq(pend_xq)
                pend_xq = None
                em.wait("V", sT, mv)
                cw = ln_consts([bank(pb), bank(pb + 1)])
                em.wait("S", sV, cw)
                yv = em.op("S", sS, lambda e, pb=pb: e.activation(
                    ybuf[:], psF[:, pb * 512: pb * 512 + 1024], AF.Identity, bias=nmb[:], scale=rstd[:]))
                prev_yv[s] = yv
                em.wait("V", sS, yv)
                em.wait("V", sx[s], lvx)
                em.wait("V", sxp[s], xp_wr[s])
                xpv = em.op("V", sV, lambda e, s=s: e.tensor_add(xpbuf[s][:], xb[s][:], ybuf[:]))
                xb_rd3[s] = xpv
                em.wait("Y", sV, xpv)
                xp_wr[s] = em.dma("Y", sxp[s], xpD.ap()[xrow + sg * 128: xrow + (sg + 1) * 128, :],
                                  xpbuf[s][:])
                cw = ln_consts([xpbuf[s][:, 0:512], xpbuf[s][:, 512:1024]])
                em.wait("S", sV, cw)
                em.wait("S", sV, x2_rd)
                x2v = em.op("S", sS, lambda e, s=s: e.activation(
                    x2buf[:], xpbuf[s][:], AF.Identity, bias=nmb[:], scale=rstd[:]))
                em.wait("V", sS, x2v)
                em.op("V", sV, lambda e: e.tensor_reduce(rmx[:], x2buf[:], AX.X, AL.max, apply_absolute_value=True))
                em.wait("V", sV, sV.n)
                em.op("V", sV, lambda e: e.tensor_scalar(rmx[:], rmx[:], 1e-5, None, AL.max))
                em.wait("V", sV, sV.n)
                em.op("V", sV, lambda e: e.reciprocal(rmr[:], rmx[:]))
                em.wait("V", sV, sV.n)
                if "prep" in phases:
                    if qi == 0:
                        em.wait("V", SEMS["sbc"], PREP["wscB_ready"])
                    em.op("V", sV, lambda e, qi=qi: e.scalar_tensor_tensor(
                        sgA[:, qi:qi + 1], rmx[:], 1.0 / 127.0, wscB[:, 0:1], AL.mult, AL.mult))
                uq = em.op("V", sV, lambda e: e.tensor_scalar(xqsb[:], x2buf[:], rmr[:], 127.0, AL.mult, AL.mult))
                x2_rd = uq
                em.wait("V", sV, uq)
                c1 = em.op("V", sV, lambda e: e.tensor_copy(i8buf[:], xqsb[:]))
                em.wait("V", sV, c1)
                em.wait("V", sV, t0_fr3)
                c2 = em.op("V", sV, lambda e: e.tensor_copy(t0f[:], i8buf[:]))
                em.wait("V", sV, c2)
                em.op("V", sV, lambda e: e.tensor_sub(xqsb[:], xqsb[:], t0f[:]))
                em.wait("V", sV, sV.n)
                em.op("V", sV, lambda e: e.scalar_tensor_tensor(t0f[:], xqsb[:], 0.5, t0f[:], AL.is_ge, AL.add))
                em.wait("V", sV, sV.n)
                em.op("V", sV, lambda e: e.tensor_scalar(xqsb[:], xqsb[:], -0.5, None, AL.is_le))
                em.wait("V", sV, sV.n)
                qv = em.op("V", sV, lambda e: e.tensor_sub(xqsb[:], t0f[:], xqsb[:]))
                t0_fr3 = qv
                pend_xq = (qi, qv)
                if xqD is not None:
                    em.wait("Y", sV, qv)
                    dqv_ = em.dma("Y", sem("dbgxq"), xqD.ap()[xrow + sg * 128: xrow + (sg + 1) * 128, :], xqsb[:])
                    em.wait("V", SEMS["dbgxq"], dqv_)
            flush_xq(pend_xq)
            pend_xq = None
            xqT_done = sV.n
            p3_xp = (SEMS["sxp0"].n, SEMS["sxp1"].n)

        if "p4" in phases:
            swu = sem("p4wu")
            swd = sem("p5wd")
            sout = sem("outst")
            sxg = [sem("p5x0"), sem("p5x1")]
            em.wait("G", sT, sT.n)
            em.wait("G", sV, sV.n)
            em.wait("G", SEMS["sxp0"], SEMS["sxp0"].n)
            em.wait("G", SEMS["sxp1"], SEMS["sxp1"].n)
            em.wait("S", SEMS["sxp0"], SEMS["sxp0"].n)
            em.wait("S", SEMS["sxp1"], SEMS["sxp1"].n)
            dv = em.dma("G", swd, bass.AP(wdT, 0, [[32768, 128], [1024, 32], [1, D]]),
                        bass.AP(wdTD, 0, [[D, 128], [128 * D, 32], [1, D]]))
            em.wait("T", SEMS["p4wg"], wg_pref)
            em.wait("T", swd, dv)
            if "prep" in phases:
                em.wait("T", SEMS["pwst0"], PREP["done"][0])
                em.wait("T", SEMS["pwst1"], PREP["done"][1])
            em.wait("T", sV, xqT_done)
            em.wait("T", sS, sS.n)
            pair_s = [0, 0, 0]
            pair_v = [0, 0, 0]
            gb_rd = [0, 0]
            pend = None

            def fused_tail(p):
                bb_unused = None
                qq, quarters, sgq, dpair, s5 = p
                cpvs = [0, 0]
                for oh in range(4):
                    pb_ = (psB, psB2)[oh % 2]
                    em.wait("T", sV, quarters[oh])
                    em.wait("T", sS, cpvs[oh % 2])
                    tv = None
                    for j in range(8):
                        blk = oh * 8 + j
                        tv = em.op("T", sT, lambda e, blk=blk, j=j, pb_=pb_: e.transpose(
                            pb_[:, j * 128:(j + 1) * 128],
                            hqsb[:, blk * 128:(blk + 1) * 128], idb[:]))
                    em.wait("S", sT, tv)
                    cpvs[oh % 2] = em.op("S", sS, lambda e, oh=oh, pb_=pb_: e.activation(
                        hqTst[:, oh * 1024:(oh + 1) * 1024], pb_[:], AF.Identity))
                em.wait("T", sS, cpvs[0])
                em.wait("T", sS, cpvs[1])
                em.wait("T", sS, pair_s[dpair])
                em.wait("T", sV, pair_v[dpair])
                pb = 2 * dpair
                mv = None
                for it in range(IT):
                    em.op("T", sT, lambda e, it=it, pb=pb: e.matmul(
                        bank(pb), hqTst[:, it * 128:(it + 1) * 128],
                        wdT[:, it * 1024: it * 1024 + 512], start=(it == 0), stop=(it == IT - 1)))
                    mv = em.op("T", sT, lambda e, it=it, pb=pb: e.matmul(
                        bank(pb + 1), hqTst[:, it * 128:(it + 1) * 128],
                        wdT[:, it * 1024 + 512: it * 1024 + 1024], start=(it == 0), stop=(it == IT - 1)))
                em.wait("V", sT, mv)
                em.wait("S", sV, xld_rd[s5])
                em.wait("S", SEMS["outst"], SEMS["outst"].n)
                xv = em.dma("S", sxg[s5], xfb[s5][:], xpD.ap()[xrow + sgq * 128: xrow + (sgq + 1) * 128, :])
                em.op("V", sV, lambda e, pb=pb: e.bn_stats(bnb[:, 4:10], bank(pb)))
                em.op("V", sV, lambda e, pb=pb: e.bn_stats(bnb[:, 10:16], bank(pb + 1)))
                em.wait("V", sV, sV.n)
                em.op("V", sV, lambda e: e.bn_aggr(st2[:], bnb[:, 4:16]))
                em.wait("V", sV, sV.n)
                em.op("V", sV, lambda e, qq=qq: e.tensor_tensor(veps[:], sdA[:, qq:qq + 1], sdA[:, qq:qq + 1], AL.mult))
                vw = sV.n
                em.wait("S", sV, vw)
                sw = em.op("S", sS, lambda e: e.activation(sq[:], st2[:, 1:2], AF.Sqrt, bias=wscB[:, 9:10], scale=veps[:]))
                em.wait("V", sS, sw)
                em.op("V", sV, lambda e: e.reciprocal(rstd[:], sq[:]))
                em.wait("V", sV, sV.n)
                em.op("V", sV, lambda e, qq=qq: e.tensor_tensor(rmr[:], sdA[:, qq:qq + 1], rstd[:], AL.mult))
                em.wait("V", sV, sV.n)
                em.op("V", sV, lambda e: e.scalar_tensor_tensor(nmb[:], st2[:, 0:1], -1.0, rmr[:], AL.mult, AL.mult))
                cw = sV.n
                em.wait("S", sV, cw)
                em.wait("S", sV, dn_rd[0])
                yv = em.op("S", sS, lambda e, pb=pb: e.activation(
                    dnT[:], psF[:, pb * 512: pb * 512 + 1024], AF.Identity, bias=nmb[:], scale=rmr[:]))
                pair_s[dpair] = yv
                em.wait("V", sS, yv)
                em.wait("V", sxg[s5], SEMS[f"p5x{s5}"].n)
                ov = em.op("V", sV, lambda e, s5=s5: e.tensor_add(xfb[s5][:], xfb[s5][:], dnT[:]))
                dn_rd[0] = ov
                xld_rd[s5] = ov
                pair_v[dpair] = ov
                em.wait("Y", sV, ov)
                em.dma("Y", sout, out.ap()[xrow + sgq * 128: xrow + (sgq + 1) * 128, :], xfb[s5][:])

            dn_rd = [0]
            xld_rd = [sV.n, sV.n]
            for qi in range(LT):
                s = qi % 3
                dprev = (s + 2) % 3
                ogp = [(dprev, s), ((s + 1) % 3, dprev), (s, (s + 1) % 3), ((s + 1) % 3, s)]
                for og in range(4):
                    if og == 2 and pend is not None:
                        fused_tail(pend)
                        pend = None
                    gp, up = ogp[og]
                    mv = None
                    if og == 3:
                        em.wait("T", sS, pair_s[up])
                        em.wait("T", sV, pair_v[up])
                        for it in range(8):
                            fap = bass.AP(featA, it * 2048 + qi * 128, [[16384, 128], [1, 128]])
                            for oc in range(2):
                                col = og * 1024 + oc * 512
                                em.op("T", sT, lambda e, a=fap, it=it, oc=oc, col=col, up=up: e.matmul(
                                    bank(2 * up + oc), a, wuT8[:, it * 4096 + col: it * 4096 + col + 512],
                                    start=(it == 0), stop=(it == 7)))
                        em.wait("T", sS, pair_s[gp])
                        em.wait("T", sV, pair_v[gp])
                        for it in range(8):
                            fap = bass.AP(featA, it * 2048 + qi * 128, [[16384, 128], [1, 128]])
                            for oc in range(2):
                                col = og * 1024 + oc * 512
                                mv = em.op("T", sT, lambda e, a=fap, it=it, oc=oc, col=col, gp=gp: e.matmul(
                                    bank(2 * gp + oc), a, wgT[:, it * 4096 + col: it * 4096 + col + 512],
                                    start=(it == 0), stop=(it == 7)))
                    else:
                        for it in range(8):
                            fap = bass.AP(featA, it * 2048 + qi * 128, [[16384, 128], [1, 128]])
                            for oc in range(2):
                                col = og * 1024 + oc * 512
                                if it == 0 and oc == 0:
                                    em.wait("T", sS, max(pair_s[gp], pair_s[up]))
                                    em.wait("T", sV, max(pair_v[gp], pair_v[up]))
                                em.op("T", sT, lambda e, a=fap, it=it, oc=oc, col=col, gp=gp: e.matmul(
                                    bank(2 * gp + oc), a, wgT[:, it * 4096 + col: it * 4096 + col + 512],
                                    start=(it == 0), stop=(it == 7)))
                                mv = em.op("T", sT, lambda e, a=fap, it=it, oc=oc, col=col, up=up: e.matmul(
                                    bank(2 * up + oc), a, wuT8[:, it * 4096 + col: it * 4096 + col + 512],
                                    start=(it == 0), stop=(it == 7)))
                    em.wait("S", sT, mv)
                    for oc in range(2):
                        col = og * 1024 + oc * 512
                        em.wait("S", sV, gb_rd[oc])
                        gev = em.op("S", sS, lambda e, oc=oc, qi=qi, gp=gp: e.activation(
                            gbuf[:, oc * 512:(oc + 1) * 512], bank(2 * gp + oc), AF.Silu, scale=sgA[:, qi:qi + 1]))
                        pair_s[gp] = gev
                        em.wait("V", sS, gev)
                        hv = em.op("V", sV, lambda e, oc=oc, qi=qi, col=col, up=up: e.scalar_tensor_tensor(
                            hbuf[:, col:col + 512], bank(2 * up + oc), sgA[:, qi:qi + 1],
                            gbuf[:, oc * 512:(oc + 1) * 512], AL.mult, AL.mult))
                        pair_v[up] = hv
                        gb_rd[oc] = hv
                    em.wait("V", sV, sV.n)
                    em.op("V", sV, lambda e, og=og: e.tensor_reduce(
                        bnb[:, og:og + 1], hbuf[:, og * 1024:(og + 1) * 1024], AX.X, AL.max,
                        apply_absolute_value=True))
                em.wait("V", sV, sV.n)
                em.op("V", sV, lambda e: e.tensor_reduce(rmx[:], bnb[:, 0:4], AX.X, AL.max))
                em.wait("V", sV, sV.n)
                em.op("V", sV, lambda e: e.tensor_scalar(rmx[:], rmx[:], 1e-5, None, AL.max))
                em.wait("V", sV, sV.n)
                em.op("V", sV, lambda e: e.reciprocal(rmr[:], rmx[:]))
                em.wait("V", sV, sV.n)
                em.op("V", sV, lambda e, qi=qi: e.scalar_tensor_tensor(
                    sdA[:, qi:qi + 1], rmx[:], 1.0 / 127.0, wscB[:, 2:3], AL.mult, AL.mult))
                em.wait("V", sV, sV.n)
                r127 = em.op("V", sV, lambda e: e.tensor_scalar(
                    bnb[:, 16:17], rmr[:], 127.0, None, AL.mult))
                em.wait("S", sV, r127)
                t0_free = 0
                hq_quarters = []
                for oh in range(4):
                    cs = slice(oh * 1024, (oh + 1) * 1024)
                    uqv = em.op("S", sS, lambda e, cs=cs: e.activation(
                        hqsb[:, cs], hbuf[:, cs], AF.Identity, scale=bnb[:, 16:17]))
                    em.wait("V", sS, uqv)
                    c1 = em.op("V", sV, lambda e, cs=cs: e.tensor_copy(i8h[:, cs], hqsb[:, cs]))
                    em.wait("V", sV, c1)
                    em.wait("V", sV, t0_free)
                    c2 = em.op("V", sV, lambda e, cs=cs: e.tensor_copy(t0f[:], i8h[:, cs]))
                    em.wait("V", sV, c2)
                    em.op("V", sV, lambda e, cs=cs: e.tensor_sub(hqsb[:, cs], hqsb[:, cs], t0f[:]))
                    em.wait("V", sV, sV.n)
                    em.op("V", sV, lambda e, cs=cs: e.scalar_tensor_tensor(t0f[:], hqsb[:, cs], 0.5, t0f[:], AL.is_ge, AL.add))
                    em.wait("V", sV, sV.n)
                    em.op("V", sV, lambda e, cs=cs: e.tensor_scalar(hqsb[:, cs], hqsb[:, cs], -0.5, None, AL.is_le))
                    em.wait("V", sV, sV.n)
                    hqv = em.op("V", sV, lambda e, cs=cs: e.tensor_sub(hqsb[:, cs], t0f[:], hqsb[:, cs]))
                    t0_free = hqv
                    hq_quarters.append(hqv)
                if hqD is not None:
                    em.wait("Y", sV, sV.n)
                    dhq = em.dma("Y", sem("dbghq"), hqD.ap()[xrow + sigma(qi) * 128: xrow + (sigma(qi) + 1) * 128, :], hqsb[:])
                    em.wait("V", SEMS["dbghq"], dhq)
                pend = (qi, hq_quarters, sigma(qi), qi % 3, qi % 2)
                if qi == LT - 1:
                    fused_tail(pend)
                    pend = None

    if "p4" in phases:
        em.wait("Y", SEMS["outst"], SEMS["outst"].n)

    with nc.Block() as block:
        @block.tensor
        def _(e):
            for f in em.q["T"]:
                f(e)

        @block.vector
        def _(e):
            for f in em.q["V"]:
                f(e)

        @block.scalar
        def _(e):
            for f in em.q["S"]:
                f(e)

        @block.gpsimd
        def _(e):
            for f in em.q["G"]:
                f(e)

        @block.sync
        def _(e):
            for f in em.q["Y"]:
                f(e)
    stack.close()
    return nc


from concourse.bass_utils import run_bass_kernel_spmd
import ml_dtypes


def _sel_const():
    s = np.zeros((16, 1024), dtype=np.float32)
    for p in range(8):
        for m in range(128):
            s[2 * p + (m >= 64), p * 128 + m] = 1.0
    return s.astype(ml_dtypes.bfloat16)


def kernel(L0_attn_xin, L0_attn_W_qkv, L0_attn_W_o, L0_ffn_w_gate, L0_ffn_w_up, L0_ffn_w_down):
    x = np.ascontiguousarray(np.asarray(L0_attn_xin, dtype=np.float32))
    consts = {
        "wqkv": np.ascontiguousarray(np.asarray(L0_attn_W_qkv, dtype=np.float32)),
        "wo": np.ascontiguousarray(np.asarray(L0_attn_W_o, dtype=np.float32)),
        "wg": np.ascontiguousarray(np.asarray(L0_ffn_w_gate, dtype=np.float32)),
        "wu": np.ascontiguousarray(np.asarray(L0_ffn_w_up, dtype=np.float32)),
        "wd": np.ascontiguousarray(np.asarray(L0_ffn_w_down, dtype=np.float32)),
        "identc": np.eye(128, dtype=ml_dtypes.bfloat16),
        "onesc": np.ones((128, 128), dtype=ml_dtypes.bfloat16),
        "selc": _sel_const(),
    }
    nc = bass.Bass("TRN2", target_bir_lowering=False)
    build(nc)
    in_maps = []
    for c in range(8):
        im = dict(consts)
        im["xin"] = np.ascontiguousarray(x[2 * c: 2 * c + 2].reshape(2 * L, D))
        in_maps.append(im)
    res = run_bass_kernel_spmd(nc, in_maps, core_ids=list(range(8)))
    out = np.empty((16, L, D), dtype=np.float32)
    for c in range(8):
        out[2 * c: 2 * c + 2] = res.results[c]["out"].reshape(2, L, D)
    return out
